# version 19
# baseline (speedup 1.0000x reference)
import numpy as np
import concourse.bass as bass
import concourse.mybir as mybir
from concourse.bass_utils import run_bass_kernel_spmd

F32 = mybir.dt.float32
BF16 = mybir.dt.bfloat16
AF = mybir.ActivationFunctionType
ALU = mybir.AluOpType
AX = mybir.AxisListType

D = 1024
DC = 8
NCORES = 8
T = 16384

ENGINES = ("pe", "act", "dve", "pool", "sp")
SEMS = ("pe", "act", "dve", "pool", "cc",
        "dx", "dq", "dqh", "dcb", "dout", "dcv", "dcq", "dpf",
        "dcq0", "dcq1", "dcq2", "dcq3", "dcq4", "dcq5", "dcq6", "dcq7",
        "dw0", "dw1",
        "ds0", "ds1", "ds2", "ds3", "ds4", "ds5",
        "di0", "di1", "di2", "di3", "dcr0", "dcr1",
        "dcp0", "dcp1")


class Buf:
    __slots__ = ("w", "r", "name")

    def __init__(self, name=""):
        self.w = {}
        self.r = {}
        self.name = name


class Em:
    def __init__(self):
        self.items = {e: [] for e in ENGINES}
        self.cnt = {s: 0 for s in SEMS}
        self.waited = {e: {} for e in ENGINES}

    def _wait(self, eng, sem, val):
        if val <= 0:
            return
        assert val <= self.cnt[sem], (eng, sem, val, self.cnt[sem])
        if self.waited[eng].get(sem, 0) >= val:
            return
        self.waited[eng][sem] = val
        self.items[eng].append(("w", sem, val))

    TOTAL_SEMS = ("dx", "dq", "dqh", "dcb", "dout", "dcv", "dcq", "dpf")

    def _wait_edge(self, eng, sem, val):
        if sem in self.TOTAL_SEMS:
            val = self.cnt[sem]
        self._wait(eng, sem, val)

    def op(self, eng, fn, reads=(), writes=(), dma=False, inc=True, sem=None,
           incval=None):
        assert sem is not None or not dma, "dma ops must name a channel sem"
        if sem is None:
            sem = eng
        if incval is None:
            incval = 16 if dma else 1
        for b in reads:
            for s, c in b.w.items():
                self._wait_edge(eng, s, c)
        for b in writes:
            for s, c in b.w.items():
                self._wait_edge(eng, s, c)
            for s, c in b.r.items():
                self._wait_edge(eng, s, c)
        if inc:
            self.cnt[sem] += incval
            self.items[eng].append(("c", fn, sem, incval))
            cur = (sem, self.cnt[sem])
            for b in writes:
                if b.name.startswith("w_img"):
                    b.w[sem] = self.cnt[sem]
                else:
                    b.w = {sem: self.cnt[sem]}
                b.r = {}
            for b in reads:
                b.r[sem] = cur[1]
            return cur
        self.items[eng].append(("c", fn, None, 0))
        return None

    def fence(self):
        for eng in ENGINES:
            for sem in SEMS:
                self._wait(eng, sem, self.cnt[sem])

    def raw_wait(self, eng, sem, val):
        self._wait(eng, sem, val)


def pack_lhsT(w):
    E, K = w.shape
    w4 = w.reshape(E // 128, 128, K // 128, 128).transpose(0, 3, 2, 1)
    return np.ascontiguousarray(w4.reshape(E, K))


VR_BDW, VR_BPW, VR_FB2, VR_NB2, VR_FB1, VR_NB1 = 0, 1, 2, 3, 4, 5
VR_T0, VR_T1, VR_T2 = 6, 7, 8
VR_GAMMA, VR_CONVG, VR_GRUG, VR_FFG = 9, 10, 11, 12


SUB3 = 9


def build_nc(tcore=2048, tt=512, upto=4, debug=0):
    nt = tcore // tt
    nc = bass.Bass("TRN2", target_bir_lowering=False, num_devices=8)

    x_in = nc.dram_tensor("x_in", [D, tcore], F32, kind="ExternalInput").ap()
    out_o = nc.dram_tensor("out_o", [D, tcore], F32, kind="ExternalOutput").ap()
    wq_p = nc.dram_tensor("wq_p", [3 * D, D], F32, kind="ExternalInput").ap()
    pw_p = nc.dram_tensor("pw_p", [D, D], F32, kind="ExternalInput").ap()
    gru_p = nc.dram_tensor("gru_p", [2 * D, D], F32, kind="ExternalInput").ap()
    ff1_p = nc.dram_tensor("ff1_p", [4 * D, D], F32, kind="ExternalInput").ap()
    ff2_p = nc.dram_tensor("ff2_p", [D, 4 * D], F32, kind="ExternalInput").ap()
    n1_p = nc.dram_tensor("n1_p", [4 * D, D], F32, kind="ExternalInput").ap()
    n2_p = nc.dram_tensor("n2_p", [D, 4 * D], F32, kind="ExternalInput").ap()
    n1_nat = nc.dram_tensor("n1_nat", [4 * D, D], F32, kind="ExternalInput").ap()
    n2_nat = nc.dram_tensor("n2_nat", [D, 4 * D], F32, kind="ExternalInput").ap()
    vrows_in = nc.dram_tensor("vrows_in", [16, 4096], F32, kind="ExternalInput").ap()
    ident_in = nc.dram_tensor("ident_in", [128, 128], F32, kind="ExternalInput").ap()
    mprev_in = nc.dram_tensor("mprev_in", [128, 8], F32, kind="ExternalInput").ap()
    smask_in = nc.dram_tensor("smask_in", [16, 1], F32, kind="ExternalInput").ap()
    smr_in = nc.dram_tensor("smr_in", [1, 32], F32, kind="ExternalInput").ap()

    wimg_q = nc.dram_tensor("wimg_q", [3 * D, D], BF16).ap()
    wimg_pw = nc.dram_tensor("wimg_pw", [D, D], BF16).ap()
    wimg_gru = nc.dram_tensor("wimg_gru", [2 * D, D], BF16).ap()
    wimg_ff1 = nc.dram_tensor("wimg_ff1", [4 * D, D], BF16).ap()
    wimg_ff2 = nc.dram_tensor("wimg_ff2", [D, 4 * D], BF16).ap()
    wimg_s1 = nc.dram_tensor("wimg_s1", [4 * D, D], BF16).ap()
    wimg_s2 = nc.dram_tensor("wimg_s2", [D, 4 * D], BF16).ap()
    qp_scr = nc.dram_tensor("qp_scr", [nt, DC, 128, tt], F32).ap()
    pg_scr = nc.dram_tensor("pg_scr", [nt, DC, 128, tt], F32).ap()
    xold_scr = nc.dram_tensor("xold_scr", [nt, DC, 128, tt], F32).ap()
    cc1 = nc.dram_tensor("cc1", [16, D], F32).ap()
    cc2 = nc.dram_tensor("cc2", [16, D], F32).ap()
    cc3 = nc.dram_tensor("cc3", [16, D], F32).ap()
    scr2a = nc.dram_tensor("scr2a", [1, 2 * D], F32).ap()
    scr2b = nc.dram_tensor("scr2b", [1, 2 * D], F32).ap()
    scr2c = nc.dram_tensor("scr2c", [1, 2 * D], F32).ap()

    from contextlib import ExitStack
    es = ExitStack()
    with es:
        def sb(name, shape, dtype=F32):
            return es.enter_context(nc.sbuf_tensor(name, shape, dtype))

        x_sb = sb("x_sb", [128, DC * tcore])
        U = sb("U", [128, 6688])
        stg = sb("stg", [128, 2 * 1024])
        cv16 = sb("cv16", [128, 4 * 1024], BF16)
        wstr = sb("wstr", [128, 2 * 4096], BF16)
        xnb = sb("xnb", [128, DC * tt], BF16)
        hb16 = sb("hb16", [128, 16384], BF16)
        sqb = sb("sqb", [128, 2 * tt])
        tmp5 = sb("tmp5", [128, 4 * tt])
        corr = sb("corr", [128, 2 * tt])
        vcols = sb("vcols", [128, 512])
        Rg = sb("Rg", [128, 2176])
        Rs = sb("Rs", [128, 3072])
        ident = sb("ident", [128, 128])
        ones_c = sb("ones_c", [128, 1])
        ones_h = sb("ones_h", [128, 1], BF16)
        hcar = sb("hcar", [128, 8])
        pcar = sb("pcar", [128, 8])
        hsel = sb("hsel", [128, 8])
        mprev = sb("mprev", [128, 8])
        smask = sb("smask", [16, 1])
        smr = sb("smr", [1, 32])
        G = sb("G", [128, 128])
        ccst = sb("ccst", [16, D])
        alpha1 = sb("alpha1", [128, 32])
        alpha2 = sb("alpha2", [128, 8])
        uh2 = sb("uh2", [128, 16])
        epsc = sb("epsc", [128, 4])

        ps = [es.enter_context(nc.psum_tensor(f"ps{i}", [128, 512], F32))
              for i in range(8)]
        sems = {s: es.enter_context(nc.semaphore(s)) for s in SEMS}

        em = Em()

        bank = [Buf(f"bank{i}") for i in range(8)]
        b_x = [Buf(f"x{j}") for j in range(nt)]
        b_tA = [Buf(f"tA{i}") for i in range(4)]
        b_tV = [Buf(f"tV{i}") for i in range(4)]
        b_tQ = [Buf(f"tQ{i}") for i in range(4)]
        b_tH = [Buf(f"tH{i}") for i in range(4)]
        b_tq = [Buf(f"tq{i}") for i in range(4)]
        b_fs = [Buf(f"fs{i}") for i in range(6)]
        b_cv16 = [Buf(f"cv16{i}") for i in range(4)]
        b_wstr = [Buf("wstr0"), Buf("wstr1")]
        b_xnb = [Buf(f"xnb{c}") for c in range(DC)]
        b_xnbh = [Buf(f"xnbh{c}") for c in range(DC)]
        b_hbf = [Buf(f"hbf{c}") for c in range(32)]
        b_sqb = [Buf("sqb0"), Buf("sqb1")]
        b_tmp = [Buf(f"tmp{i}") for i in range(4)]
        b_corr = [Buf("corr0"), Buf("corr1")]
        b_uext = [Buf(f"uext{c}") for c in range(DC)]
        b_ycv = Buf("ycv")
        b_ybf = [Buf(f"ybf{c}") for c in range(DC)]
        b_vrows = Buf("vrows")
        b_vcols = Buf("vcols")
        b_Rg = Buf("Rg")
        b_ident = Buf("ident")
        b_ones = Buf("ones")
        b_zeros = Buf("zeros")
        b_hcar = [Buf(f"hcar{c}") for c in range(DC)]
        b_pcar = [Buf(f"pcar{c}") for c in range(DC)]
        b_hsel = Buf("hsel")
        b_mprev = Buf("mprev")
        b_smask = Buf("smask")
        b_smr = Buf("smr")
        b_G = Buf("G")
        b_ccst = Buf("ccst")
        b_rows2 = Buf("rows2")
        b_alpha1 = Buf("alpha1")
        b_alpha2 = Buf("alpha2")
        b_uh2 = Buf("uh2")
        b_epsc = Buf("epsc")
        b_wimg = {k: Buf("w_img_" + k) for k in
                  ("q", "pw", "gru", "ff1", "ff2", "s1", "s2")}
        b_wq_ch = [Buf(f"wq_ch{c}") for c in range(DC)]
        b_qp = [Buf(f"qp{j}") for j in range(nt)]
        b_pg = [Buf(f"pg{j}") for j in range(nt)]
        b_xold = [Buf(f"xold{j}") for j in range(nt)]
        b_cc = {"cc1": Buf("cc1"), "cc2": Buf("cc2"), "cc3": Buf("cc3")}
        b_scr2 = {"cc1": Buf("s2a"), "cc2": Buf("s2b"), "cc3": Buf("s2c")}
        b_out = Buf("out")
        RSB = {}

        xv = x_sb.ap()

        def x_ch(ch, j, n=None):
            n = tt if n is None else n
            return xv[:, ch * tcore + j * tt: ch * tcore + j * tt + n]

        def x_cols(ch, c0, n):
            return xv[:, ch * tcore + c0: ch * tcore + c0 + n]

        Uv = U.ap()
        vrowsv = Uv[0:16, 0:4096]
        UE = DC * (tt + 4)
        R2OFF = max(8 * tt, UE + tt)

        def tsl(base, hoff, ch, n=tt):
            s = ch % 4
            if s < 2:
                return Uv[:, base + s * tt: base + s * tt + n]
            return hbfc[:, hoff + (s - 2) * tt: hoff + (s - 2) * tt + n]

        def tA_v(ch, n=tt):
            return tsl(0, 4 * tt, ch, n)

        def tV_v(ch, n=tt):
            return tsl(2 * tt, 6 * tt, ch, n)

        def tQ_v(ch, n=tt):
            return tsl(4 * tt, 8 * tt, ch, n)

        def tH_v(ch, n=tt):
            return tsl(6 * tt, 10 * tt, ch, n)

        def tq_v(ch, n=tt):
            return hbfc[:, 12 * tt + (ch % 4) * tt: 12 * tt + (ch % 4) * tt + n]

        def u_ch(ch, c0, n):
            return Uv[:, ch * (tt + 4) + c0: ch * (tt + 4) + c0 + n]

        ycvv = Uv[:, UE:UE + tt]
        rows2v = Uv[0:1, R2OFF:R2OFF + 2 * D]
        zerosv = Uv[:, R2OFF:R2OFF + tt]
        stg2v = Uv[:, R2OFF + 2 * D - 1024: R2OFF + 2 * D]

        stgv = stg.ap()
        cv16v = cv16.ap()
        stgb = stg.ap().bitcast(BF16)
        Ub = U.ap().bitcast(BF16)
        FS_OFF = [0, 1024, 2048, 3072, 2 * 5152, 2 * 5152 + 1024]

        def fs_v(slot, n):
            base = stgb if slot < 4 else Ub
            return base[:, FS_OFF[slot]: FS_OFF[slot] + n]
        hbfc = hb16.ap().bitcast(F32)
        wstrv = wstr.ap()
        xnbv = xnb.ap()
        hbfv = hb16.ap()

        def hbf_v(ec):
            return hbfv[:, ec * tt: (ec + 1) * tt]

        def ybf_v(ch):
            return hbfv[:, ch * tt: (ch + 1) * tt]

        sqv = sqb.ap()
        tmpv = tmp5.ap()
        corrv = corr.ap()
        vcolsv = vcols.ap()
        Rgv = Rg.ap()
        Rsv = Rs.ap()
        identv = ident.ap()
        onesv = ones_c.ap()
        oneshv = ones_h.ap()
        sqhv = sqb.ap().bitcast(BF16)
        hcarv, pcarv, hselv = hcar.ap(), pcar.ap(), hsel.ap()
        mprevv, smaskv = mprev.ap(), smask.ap()
        smrv = smr.ap()
        Gv, ccstv = G.ap(), ccst.ap()
        a1v, a2v = alpha1.ap(), alpha2.ap()
        uh2v = uh2.ap()
        epscv = epsc.ap()
        EPS_COL = {1e-30: 0, 1e-5: 1}

        ROW_GAMMA = (0, 0)
        ROW_CONVG = (0, 1024)
        ROW_GRU32 = (0, 0)
        ROW_FF32 = (0, 1024)

        def gain_slice(row, ch):
            base, col = row
            return Rgv[base:base + 1, col + ch * 128: col + (ch + 1) * 128]

        def ones_row():
            return Rgv[0:1, 2048:2048 + 128]

        RS = {"m": (0, 0), "vtmp": (0, 512),
              "invE": (0, 1024), "minvE": (0, 1536), "inv0": (0, 2048),
              "minv0": (0, 2560), "norm": (0, 512), "r": (0, 0)}
        for k in RS:
            RSB[k] = Buf("rs_" + k)
        RSB["norm"] = RSB["vtmp"]
        RSB["r"] = RSB["m"]

        def rs(name, n=512):
            base, col = RS[name]
            return Rsv[base:base + 1, col: col + n]

        def E(eng, fn, reads=(), writes=(), dma=False, inc=True, sem=None,
              incval=None):
            return em.op(eng, fn, reads=reads, writes=writes, dma=dma, inc=inc,
                         sem=sem, incval=incval)

        def matmul_group2(items, bank_buf):
            nit = len(items)
            allreads = []
            for it in items:
                for b in it[3]:
                    if b not in allreads:
                        allreads.append(b)
            for i, (o, l, r, rd) in enumerate(items):
                st, sp = (i == 0), (i == nit - 1)
                fn = (lambda o=o, l=l, r=r, st=st, sp=sp:
                      lambda h: h.matmul(o, l, r, start=st, stop=sp))()
                if i == nit - 1:
                    E("pe", fn, reads=allreads, writes=[bank_buf], inc=True)
                else:
                    for b in rd:
                        for s_, c_ in b.w.items():
                            em._wait_edge("pe", s_, c_)
                    if i == 0:
                        for s_, c_ in bank_buf.w.items():
                            em._wait_edge("pe", s_, c_)
                        for s_, c_ in bank_buf.r.items():
                            em._wait_edge("pe", s_, c_)
                    em.items["pe"].append(("c", fn, None, 0))

        E("sp", lambda h: h.dma_start(out=ident[:], in_=ident_in[:, :]),
          writes=[b_ident], dma=True, sem="dx")
        E("sp", lambda h: h.dma_start(out=vrowsv, in_=vrows_in[:, :]),
          writes=[b_vrows], dma=True, sem="dx")
        E("sp", lambda h: h.dma_start(out=mprev[:], in_=mprev_in[:, :]),
          writes=[b_mprev], dma=True, sem="dx")
        E("sp", lambda h: h.dma_start(out=smask[:], in_=smask_in[:, :]),
          writes=[b_smask], dma=True, sem="dx")
        E("sp", lambda h: h.dma_start(out=smr[:], in_=smr_in[:, :]),
          writes=[b_smr], dma=True, sem="dx")
        for ch in range(DC):
            E("sp", (lambda ch=ch: lambda h: h.dma_start(
                out=xv[:, ch * tcore:(ch + 1) * tcore],
                in_=x_in[ch * 128:(ch + 1) * 128, :]))(),
              writes=[b_x[j] for j in range(nt)], dma=True, sem="dx")
        E("dve", lambda h: h.memset(ones_c[:], 1.0), writes=[b_ones])
        E("dve", lambda h: h.memset(ones_h[:], 1.0), writes=[b_ones])
        E("dve", lambda h: h.memset(epscv[:, 0:1], 1e-30), writes=[b_epsc])
        E("dve", lambda h: h.memset(epscv[:, 1:2], 1e-5), writes=[b_epsc])
        E("dve", lambda h: h.memset(epscv[:, 2:3], -0.5), writes=[b_epsc])
        E("dve", lambda h: h.memset(Rg[:], 0.0), writes=[b_Rg])

        for s in range(32):
            E("pe", (lambda s=s: lambda h: h.transpose(
                ps[6][:, 0:16], vrowsv[0:16, s * 128:(s + 1) * 128],
                identv[0:16, 0:16]))(),
              reads=[b_vrows, b_ident], writes=[bank[6]])
            E("dve", (lambda s=s: lambda h: h.tensor_copy(
                out=vcolsv[:, s * 16:(s + 1) * 16], in_=ps[6][:, 0:16]))(),
              reads=[bank[6]], writes=[b_vcols])

        def vcol(r, ec):
            return vcolsv[:, ec * 16 + r: ec * 16 + r + 1]

        def build_gain(vr, row, addv, mulv):
            for ch in range(DC):
                E("pe", (lambda ch=ch, vr=vr: lambda h: h.transpose(
                    ps[7][0:1, 0:128], vcol(vr, ch), identv[:, :]))(),
                  reads=[b_vcols, b_ident], writes=[bank[7]])
                base, col = row
                dst = Rgv[base:base + 1, col + ch * 128: col + (ch + 1) * 128]
                if addv == 0.0 and mulv == 1.0:
                    E("dve", (lambda dst=dst: lambda h: h.tensor_copy(
                        out=dst, in_=ps[7][0:1, 0:128]))(),
                      reads=[bank[7]], writes=[b_Rg])
                else:
                    E("dve", (lambda dst=dst, addv=addv, mulv=mulv:
                              lambda h: h.tensor_scalar(
                                  out=dst, in0=ps[7][0:1, 0:128], scalar1=addv,
                                  scalar2=mulv, op0=ALU.add, op1=ALU.mult))(),
                      reads=[bank[7]], writes=[b_Rg])

        build_gain(VR_GAMMA, ROW_GAMMA, 0.0, 1.0)
        build_gain(VR_CONVG, ROW_CONVG, 1.0, 1.0)
        _ones_ap = Rgv[0:1, 2048:2048 + 128]
        E("dve", (lambda ap=_ones_ap: lambda h: h.memset(ap, 1.0))(),
          writes=[b_Rg])

        fslot = [0]
        cvi_slot = [0]

        def sign_unit(src_ap, dst_ap, rows0, ncols, img_buf, col0=0):
            st = {}

            def load():
                slot = fslot[0]
                fslot[0] = (fslot[0] + 1) % 6
                st["slot"] = slot
                st["sgv"] = fs_v(slot, ncols)
                E("pool", (lambda sgv=st["sgv"]: lambda h: h.dma_start(
                    out=sgv, in_=src_ap[rows0:rows0 + 128,
                                        col0:col0 + ncols]))(),
                  writes=[b_fs[slot]], dma=True, sem=f"ds{slot}")

            def compute():
                slot, sgv = st["slot"], st["sgv"]
                cslot = cvi_slot[0]
                cvi_slot[0] = (cvi_slot[0] + 1) % 4
                cg = cv16v[:, cslot * 1024: cslot * 1024 + ncols]
                E("act", (lambda cg=cg, sgv=sgv: lambda h: h.activation(
                    out=cg, in_=sgv, func=AF.Sign))(),
                  reads=[b_fs[slot]], writes=[b_cv16[cslot]])

                def write():
                    E("sp", (lambda cg=cg: lambda h: h.dma_start(
                        out=dst_ap[rows0:rows0 + 128, col0:col0 + ncols],
                        in_=cg))(),
                      reads=[b_cv16[cslot]], writes=[img_buf], dma=True,
                      sem=f"di{cslot}")
                return write
            return (load, compute)

        def alpha_unit(nat_ap, rows0, col0, ncols, dst_col, acc, bufa):
            st = {}

            def load():
                slot = fslot[0]
                fslot[0] = (fslot[0] + 1) % 6
                st["slot"] = slot
                st["sgv"] = fs_v(slot, ncols)
                E("pool", (lambda sgv=st["sgv"]: lambda h: h.dma_start(
                    out=sgv, in_=nat_ap[rows0:rows0 + 128,
                                        col0:col0 + ncols]))(),
                  writes=[b_fs[slot]], dma=True, sem=f"ds{slot}")

            def compute():
                slot, sgv = st["slot"], st["sgv"]
                if not acc:
                    E("act", (lambda sgv=sgv: lambda h: h.activation(
                        out=sgv, in_=sgv, func=AF.Abs, accum_out=dst_col))(),
                      reads=[b_fs[slot]], writes=[b_fs[slot], bufa])
                else:
                    t = tmpv[:, 3 * tt: 3 * tt + 1]
                    E("act", (lambda sgv=sgv, t=t: lambda h: h.activation(
                        out=sgv, in_=sgv, func=AF.Abs, accum_out=t))(),
                      reads=[b_fs[slot]], writes=[b_fs[slot], b_tmp[3]])
                    E("dve", (lambda t=t: lambda h: h.tensor_add(
                        out=dst_col, in0=dst_col, in1=t))(),
                      reads=[b_tmp[3], bufa], writes=[bufa])
                return None
            return (load, compute)

        def cast_img(src_ap, dst_ap, img_buf, sem="dcv", rows=None):
            if rows is None:
                sl_ = (slice(None), slice(None))
            else:
                sl_ = (slice(rows[0], rows[1]), slice(None))
            E("pool", (lambda sl_=sl_: lambda h: h.dma_start(
                out=dst_ap[sl_], in_=src_ap[sl_]))(),
              writes=[img_buf], dma=True, sem=sem)

        def cast_pieces(src_ap, dst_ap, img_buf, piece_rows):
            R = src_ap.shape[0]
            out = []
            for r0 in range(0, R, piece_rows):
                r1 = min(r0 + piece_rows, R)
                out.append((lambda r0=r0, r1=r1: cast_img(
                    src_ap, dst_ap, img_buf, rows=(r0, r1))))
            return out

        class Feeder:

            def __init__(self, units):
                self.units = list(units)
                self.li = 0
                self.ci = 0
                self.pending = []

            def take(self, k):
                if self.ci >= len(self.units):
                    while self.pending:
                        self.pending.pop(0)()
                else:
                    while len(self.pending) > 1:
                        self.pending.pop(0)()
                lim = self.li
                self.li = min(self.li + k, len(self.units))
                for u in self.units[lim:self.li]:
                    u[0]()
                for u in self.units[self.ci:lim]:
                    w = u[1]()
                    if w is not None:
                        self.pending.append(w)
                self.ci = lim

            def drain(self):
                while self.ci < len(self.units):
                    self.take(3)
                for w in self.pending:
                    w()
                self.pending = []

        for _c in (0, 1):
            cast_img(wq_p, wimg_q, b_wq_ch[_c], sem=f"dcq{_c}",
                     rows=(_c * 384, (_c + 1) * 384))
        _signs = ([sign_unit(n1_p, wimg_s1, ec * 128, D, b_wimg["s1"])
                   for ec in range(32)]
                  + [sign_unit(n2_p, wimg_s2, ec * 128, 1024, b_wimg["s2"],
                               col0=hh * 1024)
                     for ec in range(8) for hh in range(4)])
        _alphas = ([alpha_unit(n1_nat, ec * 128, 0, D, a1v[:, ec:ec + 1],
                               False, b_alpha1) for ec in range(32)]
                   + [alpha_unit(n2_nat, ec * 128, p * 1024, 1024,
                                 a2v[:, ec:ec + 1], p > 0, b_alpha2)
                      for ec in range(8) for p in range(4)])
        _s1signs = _signs[:32]
        _s2signs = _signs[32:]
        units_s = []
        for i in range(32):
            units_s.append(_s1signs[i])
            units_s.append(_alphas[2 * i])
            units_s.append(_alphas[2 * i + 1])
        units_b = _s2signs

        BK_S1, BK_S2 = 4, 5
        wl_slot = [0]

        def wload(img_ap, img_buf, rows0, k, col0=0, nrow=128):
            slot = wl_slot[0]
            wl_slot[0] ^= 1
            dst = wstrv[:, slot * 4096: slot * 4096 + k]
            if nrow != 128:
                nc_ = nrow // 128
                ima = img_ap[rows0:rows0 + nrow, col0:col0 + k // nc_]
                ima = ima.rearrange("(c p) k -> p c k", p=128)
                dst3 = dst.rearrange("p (c k) -> p c k", c=nc_)
                E("sp", (lambda dst3=dst3, ima=ima: lambda h: h.dma_start(
                    out=dst3, in_=ima))(),
                  reads=[img_buf], writes=[b_wstr[slot]], dma=True,
                  sem=f"dw{slot}")
                return dst, b_wstr[slot]
            E("sp", (lambda dst=dst: lambda h: h.dma_start(
                out=dst, in_=img_ap[rows0:rows0 + 128, col0:col0 + k]))(),
              reads=[img_buf], writes=[b_wstr[slot]], dma=True,
              sem=f"dw{slot}")
            return dst, b_wstr[slot]

        def site_stats(n, src_fn, src_bufs, want_ln, eps_list):
            if want_ln:
                items = [(ps[BK_S1][0:1, 0:n], onesv[:, :], src_fn(ch),
                          [b_ones] + src_bufs) for ch in range(DC)]
                matmul_group2(items, bank[BK_S1])
            for ch in range(DC):
                slot = ch % 2
                sq = sqhv[:, slot * tt: slot * tt + n]
                E("act", (lambda sq=sq, ch=ch: lambda h: h.activation(
                    out=sq, in_=src_fn(ch), func=AF.Square))(),
                  reads=src_bufs, writes=[b_sqb[slot]])
                fn = (lambda sq=sq, ch=ch, n=n: lambda h: h.matmul(
                    ps[BK_S2][0:1, 0:n], oneshv[:, :], sq,
                    start=(ch == 0), stop=(ch == DC - 1)))()
                E("pe", fn, reads=[b_ones, b_sqb[slot]],
                  writes=[bank[BK_S2]], inc=True)
            if want_ln:
                E("dve", (lambda n=n: lambda h: h.tensor_scalar_mul(
                    out=rs("m", n), in0=ps[BK_S1][0:1, 0:n], scalar1=1.0 / D))(),
                  reads=[bank[BK_S1]], writes=[RSB["m"]])
                E("pool", (lambda n=n: lambda h: h.tensor_mul(
                    out=rs("vtmp", n), in0=rs("m", n), in1=rs("m", n)))(),
                  reads=[RSB["m"]], writes=[RSB["vtmp"]])
                E("dve", (lambda n=n: lambda h: h.scalar_tensor_tensor(
                    out=rs("vtmp", n), in0=ps[BK_S2][0:1, 0:n],
                    scalar=1.0 / D, in1=rs("vtmp", n),
                    op0=ALU.mult, op1=ALU.subtract))(),
                  reads=[bank[BK_S2], RSB["vtmp"]], writes=[RSB["vtmp"]])
                for (inv_name, minv_name, eps) in eps_list:
                    ec_ = EPS_COL[eps]
                    E("act", (lambda n=n, ec_=ec_, minv_name=minv_name:
                              lambda h: h.activation(
                                  out=rs(minv_name, n), in_=rs("vtmp", n),
                                  func=AF.Ln, bias=epscv[0:1, ec_:ec_ + 1]))(),
                      reads=[RSB["vtmp"], b_epsc], writes=[RSB[minv_name]])
                    E("act", (lambda n=n, inv_name=inv_name,
                              minv_name=minv_name: lambda h: h.activation(
                                  out=rs(inv_name, n), in_=rs(minv_name, n),
                                  func=AF.Exp, scale=epscv[0:1, 2:3]))(),
                      reads=[RSB[minv_name], b_epsc], writes=[RSB[inv_name]])
                    E("pool", (lambda n=n, inv_name=inv_name,
                               minv_name=minv_name: lambda h: h.tensor_mul(
                                   out=rs(minv_name, n), in0=rs("m", n),
                                   in1=rs(inv_name, n)))(),
                      reads=[RSB["m"], RSB[inv_name]], writes=[RSB[minv_name]])
            else:
                E("dve", (lambda n=n: lambda h: h.tensor_scalar_max(
                    out=rs("norm", n), in0=ps[BK_S2][0:1, 0:n],
                    scalar1=1e-24))(),
                  reads=[bank[BK_S2]], writes=[RSB["norm"]])
                E("act", (lambda n=n: lambda h: h.activation(
                    out=rs("minvE", n), in_=rs("norm", n), func=AF.Ln))(),
                  reads=[RSB["norm"]], writes=[RSB["minvE"]])
                E("act", (lambda n=n: lambda h: h.activation(
                    out=rs("r", n), in_=rs("minvE", n), func=AF.Exp,
                    scale=epscv[0:1, 2:3]))(),
                  reads=[RSB["minvE"], b_epsc], writes=[RSB["r"]])

        def apply_ln(n, src_fn, src_bufs, gain_row, inv_name, minv_name,
                     out_fn, out_bufs, bb=6, bc=7, tmp_i=0):
            for ch in range(DC):
                p2 = 2 * (ch % 2)
                bbx, bcx = 4 + p2, 5 + p2
                g = gain_slice(gain_row, ch) if gain_row is not None else ones_row()
                E("pe", (lambda g=g, n=n, inv_name=inv_name, bbx=bbx:
                         lambda h: h.matmul(ps[bbx][:, 0:n], g, rs(inv_name, n),
                                            start=True, stop=True))(),
                  reads=[b_Rg, RSB[inv_name]], writes=[bank[bbx]])
                E("pe", (lambda g=g, n=n, minv_name=minv_name, bcx=bcx:
                         lambda h: h.matmul(ps[bcx][:, 0:n], g, rs(minv_name, n),
                                            start=True, stop=True))(),
                  reads=[b_Rg, RSB[minv_name]], writes=[bank[bcx]])
                t = tmpv[:, (tmp_i + ch % 2) * tt: (tmp_i + ch % 2) * tt + n]
                E("dve", (lambda t=t, ch=ch, n=n, bbx=bbx: lambda h: h.tensor_mul(
                    out=t, in0=src_fn(ch), in1=ps[bbx][:, 0:n]))(),
                  reads=src_bufs + [bank[bbx]], writes=[b_tmp[tmp_i + ch % 2]])
                E("dve", (lambda t=t, ch=ch, n=n, bcx=bcx: lambda h: h.tensor_sub(
                    out=out_fn(ch), in0=t, in1=ps[bcx][:, 0:n]))(),
                  reads=[b_tmp[tmp_i + ch % 2], bank[bcx]], writes=[out_bufs[ch]])

        def apply_rms(n, src_fn, src_bufs, gain_row, out_fn, out_bufs, bb=6):
            for ch in range(DC):
                bbx = 4 + (ch % 4)
                g = gain_slice(gain_row, ch)
                E("pe", (lambda g=g, n=n, bbx=bbx: lambda h: h.matmul(
                    ps[bbx][:, 0:n], g, rs("r", n), start=True, stop=True))(),
                  reads=[b_Rg, RSB["r"]], writes=[bank[bbx]])
                E("dve", (lambda ch=ch, n=n, bbx=bbx: lambda h: h.tensor_mul(
                    out=out_fn(ch), in0=src_fn(ch), in1=ps[bbx][:, 0:n]))(),
                  reads=src_bufs + [bank[bbx]], writes=[out_bufs[ch]])

        def exchange_prep(colA_fn, colA_bufs, colB_fn, colB_bufs):
            for ch in range(DC):
                E("pe", (lambda ch=ch: lambda h: h.transpose(
                    ps[7][0:1, 0:128], colA_fn(ch), identv[:, :]))(),
                  reads=colA_bufs + [b_ident], writes=[bank[7]])
                E("dve", (lambda ch=ch: lambda h: h.tensor_copy(
                    out=rows2v[0:1, ch * 128:(ch + 1) * 128],
                    in_=ps[7][0:1, 0:128]))(),
                  reads=[bank[7]], writes=[b_rows2])
                E("pe", (lambda ch=ch: lambda h: h.transpose(
                    ps[7][0:1, 0:128], colB_fn(ch), identv[:, :]))(),
                  reads=colB_bufs + [b_ident], writes=[bank[7]])
                E("dve", (lambda ch=ch: lambda h: h.tensor_copy(
                    out=rows2v[0:1, D + ch * 128: D + (ch + 1) * 128],
                    in_=ps[7][0:1, 0:128]))(),
                  reads=[bank[7]], writes=[b_rows2])
            for ch in range(DC):
                bk_ = 6 if ch < 4 else 5
                co = (ch % 4) * 128
                E("pe", (lambda ch=ch, bk_=bk_, co=co: lambda h: h.matmul(
                    ps[bk_][0:16, co:co + 128], smrv[0:1, 0:16],
                    rows2v[0:1, ch * 128:(ch + 1) * 128],
                    start=True, stop=False))(),
                  reads=[b_smr, b_rows2], writes=[bank[bk_]])
                E("pe", (lambda ch=ch, bk_=bk_, co=co: lambda h: h.matmul(
                    ps[bk_][0:16, co:co + 128], smrv[0:1, 16:32],
                    rows2v[0:1, D + ch * 128: D + (ch + 1) * 128],
                    start=False, stop=True))(),
                  reads=[b_smr, b_rows2], writes=[bank[bk_]])

        def exchange_send(tag, cc_ap):
            E("dve", lambda h: h.tensor_copy(
                out=ccstv[0:16, 0:512], in_=ps[6][0:16, 0:512]),
              reads=[bank[6]], writes=[b_ccst])
            E("dve", lambda h: h.tensor_copy(
                out=ccstv[0:16, 512:1024], in_=ps[5][0:16, 0:512]),
              reads=[bank[5]], writes=[b_ccst])
            E("sp", (lambda cc_ap=cc_ap: lambda h: h.dma_start(
                out=cc_ap[:, :], in_=ccstv))(),
              reads=[b_ccst], writes=[b_cc[tag]], dma=True, sem="dcb")
            flat = cc_ap.rearrange("a b -> (a b)").rearrange("(o f) -> o f",
                                                             o=1)
            E("pool", (lambda flat=flat: lambda h: h.collective_compute(
                "AllReduce", ALU.add, replica_groups=[list(range(8))],
                ins=[flat], outs=[flat]))(),
              reads=[b_cc[tag]], writes=[b_cc[tag]], sem="cc", incval=1)

        def exchange_recv(tag, cc_ap):
            E("sp", (lambda cc_ap=cc_ap: lambda h: h.dma_start(
                out=ccst[:], in_=cc_ap[:, :]))(),
              reads=[b_cc[tag]], writes=[b_ccst], dma=True, sem="dcb")
            for ch in range(DC):
                E("pe", (lambda ch=ch: lambda h: h.transpose(
                    ps[7][:, 0:16], ccstv[0:16, ch * 128:(ch + 1) * 128],
                    identv[0:16, 0:16]))(),
                  reads=[b_ccst, b_ident], writes=[bank[7]])
                E("dve", (lambda ch=ch: lambda h: h.tensor_copy(
                    out=Gv[:, ch * 16:(ch + 1) * 16], in_=ps[7][:, 0:16]))(),
                  reads=[bank[7]], writes=[b_G])

        qpf_bufs = [b_tA[0], b_tA[1], b_tV[0], b_tV[1],
                    b_tQ[0], b_tQ[1], b_tH[0], b_tH[1]]

        def qpf_v(ch):
            return Uv[:, ch * tt: (ch + 1) * tt]

        def prefetch_corr(scr, scr_buf):
            for ch in range(DC):
                E("pool", (lambda ch=ch, scr=scr: lambda h: h.dma_start(
                    out=qpf_v(ch), in_=scr[0, ch, :, :]))(),
                  reads=[scr_buf], writes=[qpf_bufs[ch], b_uext[ch]],
                  dma=True, sem="dpf")

        def combine_carries():
            Hc = tmpv[:, 0:8]
            E("dve", lambda h: h.memset(hsel[:], 0.0), writes=[b_hsel])
            Gr = Gv.rearrange("p (c s) -> p c s", s=16)
            for k in range(7):
                Ak = Gr[:, :, 2 * k:2 * k + 1]
                bk_ = Gr[:, :, 2 * k + 1:2 * k + 2]
                if k == 0:
                    E("dve", (lambda bk_=bk_: lambda h: h.tensor_copy(
                        out=Hc, in_=bk_))(),
                      reads=[b_G], writes=[b_tmp[0]])
                else:
                    E("dve", (lambda Ak=Ak: lambda h: h.tensor_mul(
                        out=Hc, in0=Hc, in1=Ak))(),
                      reads=[b_G, b_tmp[0]], writes=[b_tmp[0]])
                    E("dve", (lambda bk_=bk_: lambda h: h.tensor_add(
                        out=Hc, in0=Hc, in1=bk_))(),
                      reads=[b_G, b_tmp[0]], writes=[b_tmp[0]])
                E("dve", (lambda k=k: lambda h: h.scalar_tensor_tensor(
                    out=hsel[:], in0=Hc, scalar=mprevv[:, k:k + 1],
                    in1=hsel[:], op0=ALU.mult, op1=ALU.add))(),
                  reads=[b_tmp[0], b_mprev, b_hsel], writes=[b_hsel])

        em.fence()

        E("dve", lambda h: h.memset(hcar[:], 0.0), writes=b_hcar)
        E("dve", lambda h: h.memset(pcar[:], 1.0), writes=b_pcar)
        E("dve", lambda h: h.memset(zerosv, 0.0), writes=[b_zeros])
        sfeed = Feeder(units_s)

        cpieces_a = (cast_pieces(pw_p, wimg_pw, b_wimg["pw"], 128)
                     + cast_pieces(gru_p, wimg_gru, b_wimg["gru"], 128))
        cpieces_b = (cast_pieces(ff1_p, wimg_ff1, b_wimg["ff1"], 128)
                     + cast_pieces(ff2_p, wimg_ff2, b_wimg["ff2"], 32))
        cpa_i = [0]
        cpb_i = [0]

        def feed_casts_a(k):
            for fn in cpieces_a[cpa_i[0]:cpa_i[0] + k]:
                fn()
            cpa_i[0] = min(cpa_i[0] + k, len(cpieces_a))

        def feed_casts_b(k):
            for fn in cpieces_b[cpb_i[0]:cpb_i[0] + k]:
                fn()
            cpb_i[0] = min(cpb_i[0] + k, len(cpieces_b))

        def xnb_v(j, ch):
            base = xnbv if j % 2 == 0 else hbfv
            return base[:, ch * tt:(ch + 1) * tt]

        def xnb_b(j):
            return b_xnb if j % 2 == 0 else b_xnbh

        def a_stats(j):
            site_stats(tt, lambda ch, j=j: x_ch(ch, j), [b_x[j]], True,
                       [("invE", "minvE", 1e-30)])

        def a_apply(j):
            apply_ln(tt, lambda ch, j=j: x_ch(ch, j), [b_x[j]], ROW_GAMMA,
                     "invE", "minvE", lambda ch, j=j: xnb_v(j, ch), xnb_b(j))

        a_stats(0)
        a_apply(0)
        for j in range(nt):
            qk_rot = [0]
            for ch in range(DC):
                sl = ch % 4
                wv3, wb3 = wload(wimg_q, b_wq_ch[ch], ch * 384, 3 * D,
                                 nrow=384)
                trip = [(0, "kv"), (1, "a"), (2, "q")]
                for (ti, kind) in trip:
                    bi = qk_rot[0] % 4
                    qk_rot[0] += 1
                    wv = wv3[:, ti * D: (ti + 1) * D]
                    items = [(ps[bi][:, 0:tt], wv[:, kc * 128:(kc + 1) * 128],
                              xnb_v(j, kc),
                              [wb3, xnb_b(j)[kc]]) for kc in range(DC)]
                    matmul_group2(items, bank[bi])
                    if kind == "kv":
                        E("act", (lambda ch=ch, bi=bi: lambda h: h.activation(
                            out=tV_v(ch), in_=ps[bi][:, 0:tt],
                            func=AF.Copy))(),
                          reads=[bank[bi]], writes=[b_tV[sl]])
                    elif kind == "a":
                        E("act", (lambda ch=ch, bi=bi: lambda h: h.activation(
                            out=tA_v(ch), in_=ps[bi][:, 0:tt],
                            func=AF.Sigmoid))(),
                          reads=[bank[bi]], writes=[b_tA[sl]])
                    else:
                        E("act", (lambda ch=ch, bi=bi: lambda h: h.activation(
                            out=tQ_v(ch), in_=ps[bi][:, 0:tt], func=AF.Copy))(),
                          reads=[bank[bi]], writes=[b_tQ[sl]])
                E("dve", (lambda ch=ch: lambda h: h.tensor_tensor_scan(
                    out=tH_v(ch), data0=tA_v(ch), data1=tV_v(ch),
                    initial=hcarv[:, ch:ch + 1], op0=ALU.mult, op1=ALU.add))(),
                  reads=[b_tA[sl], b_tV[sl], b_hcar[ch]], writes=[b_tH[sl]])
                E("dve", (lambda ch=ch: lambda h: h.tensor_copy(
                    out=hcarv[:, ch:ch + 1], in_=tH_v(ch)[:, tt - 1:tt]))(),
                  reads=[b_tH[sl]], writes=[b_hcar[ch]])
                E("dve", (lambda ch=ch: lambda h: h.tensor_tensor_scan(
                    out=tV_v(ch), data0=tA_v(ch), data1=zerosv[:, :],
                    initial=pcarv[:, ch:ch + 1], op0=ALU.mult, op1=ALU.add))(),
                  reads=[b_tA[sl], b_zeros, b_pcar[ch], b_tV[sl]],
                  writes=[b_tV[sl]])
                E("dve", (lambda ch=ch: lambda h: h.tensor_copy(
                    out=pcarv[:, ch:ch + 1], in_=tV_v(ch)[:, tt - 1:tt]))(),
                  reads=[b_tV[sl]], writes=[b_pcar[ch]])
                tq = tq_v(ch)
                E("pool", (lambda ch=ch, tq=tq: lambda h: h.tensor_mul(
                    out=tq, in0=tQ_v(ch), in1=tH_v(ch)))(),
                  reads=[b_tQ[sl], b_tH[sl]], writes=[b_tq[sl]])
                E("dve", (lambda ch=ch, j=j, tq=tq: lambda h: h.tensor_add(
                    out=x_ch(ch, j), in0=x_ch(ch, j), in1=tq))(),
                  reads=[b_tq[sl], b_x[j]], writes=[b_x[j]])
                E("pool", (lambda ch=ch: lambda h: h.tensor_mul(
                    out=tH_v(ch), in0=tQ_v(ch), in1=tV_v(ch)))(),
                  reads=[b_tQ[sl], b_tV[sl]], writes=[b_tH[sl]])
                E("pool", (lambda j=j, ch=ch: lambda h: h.dma_start(
                    out=qp_scr[j, ch, :, :], in_=tH_v(ch)))(),
                  reads=[b_tH[sl]], writes=[b_qp[j]], dma=True, sem="dq")
                if j == 0 and ch < 6:
                    cast_img(wq_p, wimg_q, b_wq_ch[ch + 2], sem=f"dcq{ch + 2}",
                             rows=((ch + 2) * 384, (ch + 3) * 384))
                if ch == 3 and j + 1 < nt:
                    a_stats(j + 1)
                if ch == 6 and j + 1 < nt:
                    a_apply(j + 1)
                sfeed.take(3)
                if j > 0 or nt == 1:
                    feed_casts_a(1)

        sfeed.drain()
        feed_casts_a(len(cpieces_a))
        E("dve", lambda h: h.tensor_scalar_mul(
            out=a1v[:, :], in0=a1v[:, :], scalar1=1.0 / D),
          reads=[b_alpha1], writes=[b_alpha1])
        E("dve", lambda h: h.tensor_scalar_mul(
            out=a2v[:, :], in0=a2v[:, :], scalar1=1.0 / (4 * D)),
          reads=[b_alpha2], writes=[b_alpha2])

        em.fence()
        exchange_prep(lambda ch: pcarv[:, ch:ch + 1], list(b_pcar),
                      lambda ch: hcarv[:, ch:ch + 1], list(b_hcar))
        exchange_send("cc1", cc1)
        prefetch_corr(qp_scr, b_qp[0])
        exchange_recv("cc1", cc1)
        combine_carries()

        def corr_tile(scr, j, pref=False):
            for ch in range(DC):
                sl = ch % 2
                if pref and j == 0:
                    E("dve", (lambda j=j, ch=ch: lambda h:
                              h.scalar_tensor_tensor(
                                  out=x_ch(ch, j), in0=qpf_v(ch),
                                  scalar=hselv[:, ch:ch + 1], in1=x_ch(ch, j),
                                  op0=ALU.mult, op1=ALU.add))(),
                      reads=[qpf_bufs[ch], b_uext[ch], b_hsel, b_x[j]],
                      writes=[b_x[j]])
                    continue
                cr = corrv[:, sl * tt: (sl + 1) * tt]
                E("pool", (lambda j=j, ch=ch, cr=cr, scr=scr:
                           lambda h: h.dma_start(
                               out=cr, in_=scr[j, ch, :, :]))(),
                  reads=[b_qp[j] if scr is qp_scr else b_pg[j]],
                  writes=[b_corr[sl]], dma=True, sem=f"dcp{sl}")
                E("dve", (lambda j=j, ch=ch, cr=cr:
                          lambda h: h.scalar_tensor_tensor(
                              out=x_ch(ch, j), in0=cr,
                              scalar=hselv[:, ch:ch + 1], in1=x_ch(ch, j),
                              op0=ALU.mult, op1=ALU.add))(),
                  reads=[b_corr[sl], b_hsel, b_x[j]], writes=[b_x[j]])

        def emit_mlp(kind, w1_img, w1_buf, w2_img, w2_buf, b1_col, epilogue,
                     alpha, bunits=None, gain_row=None, pre_tile=None,
                     cast_fn=None):
            bfeed = Feeder(bunits) if bunits else None

            def head_stats(j):
                if kind == "ln":
                    site_stats(tt, lambda ch, j=j: x_ch(ch, j), [b_x[j]], True,
                               [("invE", "minvE", 1e-5)])
                else:
                    site_stats(tt, lambda ch, j=j: x_ch(ch, j), [b_x[j]],
                               False, [])

            def head_apply(j):
                if kind == "ln":
                    apply_ln(tt, lambda ch, j=j: x_ch(ch, j), [b_x[j]], None,
                             "invE", "minvE",
                             lambda ch: xnbv[:, ch * tt:(ch + 1) * tt], b_xnb)
                else:
                    apply_rms(tt, lambda ch, j=j: x_ch(ch, j), [b_x[j]],
                              gain_row,
                              lambda ch: xnbv[:, ch * tt:(ch + 1) * tt], b_xnb)

            if pre_tile is not None:
                pre_tile(0)
            head_stats(0)
            head_apply(0)
            for j in range(nt):
                for ec in range(32):
                    bi = ec % 2
                    wv, wb = wload(w1_img, w1_buf, ec * 128, D)
                    items = [(ps[bi][:, 0:tt], wv[:, kc * 128:(kc + 1) * 128],
                              xnbv[:, kc * tt:(kc + 1) * tt],
                              [wb, b_xnb[kc]]) for kc in range(DC)]
                    matmul_group2(items, bank[bi])
                    bi_c = b1_col(ec)
                    rd = [bank[bi], b_vcols] + ([b_alpha1] if alpha else [])
                    if alpha:
                        sc = a1v[:, ec:ec + 1]
                        E("act", (lambda ec=ec, bi=bi, sc=sc, bi_c=bi_c:
                                  lambda h: h.activation(
                                      out=hbf_v(ec), in_=ps[bi][:, 0:tt],
                                      func=AF.Gelu, scale=sc, bias=bi_c))(),
                          reads=rd, writes=[b_hbf[ec]])
                    else:
                        E("act", (lambda ec=ec, bi=bi, bi_c=bi_c:
                                  lambda h: h.activation(
                                      out=hbf_v(ec), in_=ps[bi][:, 0:tt],
                                      func=AF.Gelu, bias=bi_c))(),
                          reads=rd, writes=[b_hbf[ec]])
                    if bfeed:
                        bfeed.take(1)
                    if cast_fn:
                        cast_fn()
                    if ec == 14 and j + 1 < nt and pre_tile is not None:
                        pre_tile(j + 1)
                    if ec == 16 and j + 1 < nt:
                        head_stats(j + 1)
                for dc in range(DC):
                    bi = 2 + dc % 2
                    wv1, wb1 = wload(w2_img, w2_buf, dc * 128, 4096)
                    items = [(ps[bi][:, 0:tt], wv1[:, kc * 128:(kc + 1) * 128],
                              hbf_v(kc), [wb1, b_hbf[kc]]) for kc in range(32)]
                    matmul_group2(items, bank[bi])
                    epilogue(j, dc, ps[bi][:, 0:tt], bank[bi])
                    if bfeed:
                        bfeed.take(1)
                    if cast_fn:
                        cast_fn()
                    if dc == 2 and j + 1 < nt:
                        head_apply(j + 1)
            if bfeed:
                bfeed.drain()

        def ffn_epilogue(store):
            def ep(j, dc, psum_ap, bk):
                sl = dc % 2
                t = tmpv[:, sl * tt: (sl + 1) * tt]
                E("dve", (lambda psum_ap=psum_ap, dc=dc, t=t:
                          lambda h: h.tensor_scalar(
                              out=t, in0=psum_ap, scalar1=a2v[:, dc:dc + 1],
                              scalar2=vcol(VR_NB2, dc), op0=ALU.mult,
                              op1=ALU.add))(),
                  reads=[bk, b_alpha2, b_vcols], writes=[b_tmp[sl]])
                E("dve", (lambda j=j, dc=dc, t=t: lambda h: h.tensor_add(
                    out=x_ch(dc, j), in0=x_ch(dc, j), in1=t))(),
                  reads=[b_tmp[sl], b_x[j]], writes=[b_x[j]])
                if store:
                    E("pool", (lambda j=j, dc=dc: lambda h: h.dma_start(
                        out=out_o.rearrange("(c p) t -> p c t", p=128)
                        [:, dc, j * tt:(j + 1) * tt],
                        in_=x_ch(dc, j)))(),
                      reads=[b_x[j]], writes=[b_out], dma=True, sem="dout")
            return ep

        def _bcasts(j):
            corr_tile(qp_scr, j, pref=(j == 0))

        em.fence()
        if upto >= 2:
            emit_mlp("ln", wimg_s1, b_wimg["s1"], wimg_s2, b_wimg["s2"],
                     lambda ec: vcol(VR_NB1, ec), ffn_epilogue(False), True,
                     bunits=units_b, pre_tile=_bcasts,
                     cast_fn=lambda: feed_casts_b(1))
        elif upto == 1:
            pass
        else:
            for j in range(nt):
                corr_tile(qp_scr, j)
        feed_casts_b(len(cpieces_b))

        class _EndPhase3(Exception):
            pass
        try:
         if upto >= 3:
            build_gain(VR_GRUG, ROW_GRU32, 1.0, 32.0)
            site_stats(2, lambda ch: x_cols(ch, tcore - 2, 2), [b_x[nt - 1]],
                       True, [("inv0", "minv0", 1e-30)])
            for ch in range(DC):
                g = gain_slice(ROW_CONVG, ch)
                E("pe", (lambda g=g: lambda h: h.matmul(
                    ps[6][:, 0:2], g, rs("inv0", 2), start=True, stop=True))(),
                  reads=[b_Rg, RSB["inv0"]], writes=[bank[6]])
                E("pe", (lambda g=g: lambda h: h.matmul(
                    ps[7][:, 0:2], g, rs("minv0", 2), start=True, stop=True))(),
                  reads=[b_Rg, RSB["minv0"]], writes=[bank[7]])
                t = tmpv[:, 0:2]
                E("dve", (lambda ch=ch, t=t: lambda h: h.tensor_mul(
                    out=t, in0=x_cols(ch, tcore - 2, 2), in1=ps[6][:, 0:2]))(),
                  reads=[b_x[nt - 1], bank[6]], writes=[b_tmp[0]])
                E("dve", (lambda ch=ch, t=t: lambda h: h.tensor_sub(
                    out=uh2v[:, ch * 2: ch * 2 + 2], in0=t,
                    in1=ps[7][:, 0:2]))(),
                  reads=[b_tmp[0], bank[7]], writes=[b_uh2])
            em.fence()
            exchange_prep(lambda ch: uh2v[:, ch * 2: ch * 2 + 1], [b_uh2],
                          lambda ch: uh2v[:, ch * 2 + 1: ch * 2 + 2], [b_uh2])
            exchange_send("cc2", cc2)
            _s3 = SUB3

            uer = Uv[:, 0:DC * (tt + 4)].rearrange("p (c t) -> p c t", c=DC)
            Gr = Gv.rearrange("p (c s) -> p c s", s=16)

            def a3_stats(j):
                for ch in range(DC):
                    E("sp", (lambda j=j, ch=ch: lambda h: h.dma_start(
                        out=xold_scr[j, ch, :, :], in_=x_ch(ch, j)))(),
                      reads=[b_x[j]], writes=[b_xold[j]], dma=True, sem="dqh")
                site_stats(tt, lambda ch, j=j: x_ch(ch, j), [b_x[j]], True,
                           [("invE", "minvE", 1e-5), ("inv0", "minv0", 1e-30)])

            def a3_uln(j):
                for ch in range(DC):
                    g2 = gain_slice(ROW_CONVG, ch)
                    E("pe", (lambda g2=g2: lambda h: h.matmul(
                        ps[6][:, 0:tt], g2, rs("inv0", tt),
                        start=True, stop=True))(),
                      reads=[b_Rg, RSB["inv0"]], writes=[bank[6]])
                    E("pe", (lambda g2=g2: lambda h: h.matmul(
                        ps[7][:, 0:tt], g2, rs("minv0", tt),
                        start=True, stop=True))(),
                      reads=[b_Rg, RSB["minv0"]], writes=[bank[7]])
                    t0 = tmpv[:, 0:tt]
                    E("dve", (lambda ch=ch, j=j, t0=t0: lambda h: h.tensor_mul(
                        out=t0, in0=x_ch(ch, j), in1=ps[6][:, 0:tt]))(),
                      reads=[b_x[j], bank[6]], writes=[b_tmp[0]])
                    E("dve", (lambda ch=ch, t0=t0: lambda h: h.tensor_sub(
                        out=u_ch(ch, 2, tt), in0=t0, in1=ps[7][:, 0:tt]))(),
                      reads=[b_tmp[0], bank[7]], writes=[b_uext[ch]])
                    o = ones_row()
                    E("pe", (lambda o=o: lambda h: h.matmul(
                        ps[4][:, 0:tt], o, rs("invE", tt),
                        start=True, stop=True))(),
                      reads=[b_Rg, RSB["invE"]], writes=[bank[4]])
                    E("pe", (lambda o=o: lambda h: h.matmul(
                        ps[5][:, 0:tt], o, rs("minvE", tt),
                        start=True, stop=True))(),
                      reads=[b_Rg, RSB["minvE"]], writes=[bank[5]])
                    t1 = tmpv[:, tt:2 * tt]
                    E("dve", (lambda ch=ch, j=j, t1=t1: lambda h: h.tensor_mul(
                        out=t1, in0=x_ch(ch, j), in1=ps[4][:, 0:tt]))(),
                      reads=[b_x[j], bank[4], b_xold[j]], writes=[b_tmp[1]])
                    E("dve", (lambda ch=ch, j=j, t1=t1: lambda h: h.tensor_sub(
                        out=x_ch(ch, j), in0=t1, in1=ps[5][:, 0:tt]))(),
                      reads=[b_tmp[1], bank[5]], writes=[b_x[j]])

            def a3_taps_pw(j):
                for ch in range(DC):
                    E("dve", (lambda ch=ch: lambda h: h.tensor_scalar(
                        out=ycvv, in0=u_ch(ch, 2, tt),
                        scalar1=vcol(VR_T2, ch), scalar2=vcol(VR_BDW, ch),
                        op0=ALU.mult, op1=ALU.add))(),
                      reads=[b_uext[ch], b_vcols], writes=[b_ycv])
                    E("dve", (lambda ch=ch: lambda h: h.scalar_tensor_tensor(
                        out=ycvv, in0=u_ch(ch, 1, tt),
                        scalar=vcol(VR_T1, ch), in1=ycvv,
                        op0=ALU.mult, op1=ALU.add))(),
                      reads=[b_uext[ch], b_vcols, b_ycv], writes=[b_ycv])
                    E("dve", (lambda ch=ch: lambda h: h.scalar_tensor_tensor(
                        out=ybf_v(ch), in0=u_ch(ch, 0, tt),
                        scalar=vcol(VR_T0, ch), in1=ycvv,
                        op0=ALU.mult, op1=ALU.add))(),
                      reads=[b_uext[ch], b_vcols, b_ycv], writes=[b_ybf[ch]])
                    E("dve", (lambda ch=ch: lambda h: h.tensor_copy(
                        out=u_ch(ch, 0, 2), in_=u_ch(ch, tt, 2)))(),
                      reads=[b_uext[ch]], writes=[b_uext[ch]])
                for ec in range(DC):
                    bi = 2 + ec % 2
                    wv, wb = wload(wimg_pw, b_wimg["pw"], ec * 128, D)
                    items = [(ps[bi][:, 0:tt], wv[:, kc * 128:(kc + 1) * 128],
                              ybf_v(kc), [wb, b_ybf[kc]]) for kc in range(DC)]
                    matmul_group2(items, bank[bi])
                    sl = ec % 2
                    t = tmpv[:, (2 + sl) * tt: (3 + sl) * tt]
                    E("dve", (lambda bi=bi, ec=ec, t=t:
                              lambda h: h.tensor_scalar_add(
                                  out=t, in0=ps[bi][:, 0:tt],
                                  scalar1=vcol(VR_BPW, ec)))(),
                      reads=[bank[bi], b_vcols], writes=[b_tmp[2 + sl]])
                    E("dve", (lambda j=j, ec=ec, t=t: lambda h: h.tensor_add(
                        out=x_ch(ec, j), in0=x_ch(ec, j), in1=t))(),
                      reads=[b_tmp[2 + sl], b_x[j]], writes=[b_x[j]])
                    if ec == 2 and j + 1 < nt and _s3 >= 3:
                        a3_stats(j + 1)

            if _s3 >= 3:
                a3_stats(0)
                a3_uln(0)
            exchange_recv("cc2", cc2)
            for par in range(2):
                dst = uer[:, :, par:par + 1]
                E("dve", (lambda dst=dst: lambda h: h.memset(dst, 0.0))(),
                  writes=list(b_uext))
            for k in range(7):
                for par in range(2):
                    dst = uer[:, :, par:par + 1]
                    src_ = Gr[:, :, 2 * k + par:2 * k + par + 1]
                    E("dve", (lambda dst=dst, src_=src_, k=k:
                              lambda h: h.scalar_tensor_tensor(
                                  out=dst, in0=src_, scalar=mprevv[:, k:k + 1],
                                  in1=dst, op0=ALU.mult, op1=ALU.add))(),
                      reads=[b_G, b_mprev] + list(b_uext),
                      writes=list(b_uext))
            if _s3 >= 3:
                a3_taps_pw(0)
                for j in range(1, nt):
                    a3_uln(j)
                    a3_taps_pw(j)

            em.fence()

            E("dve", lambda h: h.memset(hcar[:], 0.0), writes=b_hcar)
            E("dve", lambda h: h.memset(pcar[:], 1.0), writes=b_pcar)
            E("dve", lambda h: h.memset(zerosv, 0.0), writes=[b_zeros])
            for j in range(nt if _s3 >= 4 else 0):
                site_stats(tt, lambda ch, j=j: x_ch(ch, j), [b_x[j]],
                           False, [])
                apply_rms(tt, lambda ch, j=j: x_ch(ch, j), [b_x[j]],
                          ROW_GRU32,
                          lambda ch: xnbv[:, ch * tt:(ch + 1) * tt], b_xnb)
                for ch in range(DC):
                    sl = ch % 4
                    sl2 = ch % 2
                    bh = 2 + sl2
                    bg = sl2
                    wvg, wbg = wload(wimg_gru, b_wimg["gru"], ch * 256, 2 * D,
                                     nrow=256)
                    items = [(ps[bh][:, 0:tt], wvg[:, kc * 128:(kc + 1) * 128],
                              xnbv[:, kc * tt:(kc + 1) * tt],
                              [wbg, b_xnb[kc]]) for kc in range(DC)]
                    matmul_group2(items, bank[bh])
                    items = [(ps[bg][:, 0:tt],
                              wvg[:, D + kc * 128: D + (kc + 1) * 128],
                              xnbv[:, kc * tt:(kc + 1) * tt],
                              [wbg, b_xnb[kc]]) for kc in range(DC)]
                    matmul_group2(items, bank[bg])
                    s_t = tmpv[:, sl2 * tt: (sl2 + 1) * tt]
                    sv_t = tmpv[:, (2 + sl2) * tt: (3 + sl2) * tt]
                    E("act", (lambda bh=bh, s_t=s_t: lambda h: h.activation(
                        out=s_t, in_=ps[bh][:, 0:tt], func=AF.Sigmoid))(),
                      reads=[bank[bh]], writes=[b_tmp[sl2]])
                    E("dve", (lambda bh=bh, ch=ch: lambda h: h.tensor_scalar_add(
                        out=tV_v(ch), in0=ps[bh][:, 0:tt], scalar1=0.5))(),
                      reads=[bank[bh], b_tmp[sl2]], writes=[b_tV[sl]])
                    E("act", (lambda bg=bg, ch=ch: lambda h: h.activation(
                        out=tA_v(ch), in_=ps[bg][:, 0:tt], func=AF.Sigmoid,
                        scale=-1.0))(),
                      reads=[bank[bg]], writes=[b_tA[sl]])
                    E("act", (lambda bg=bg, sv_t=sv_t: lambda h: h.activation(
                        out=sv_t, in_=ps[bg][:, 0:tt], func=AF.Sigmoid))(),
                      reads=[bank[bg]], writes=[b_tmp[2 + sl2]])
                    E("dve", (lambda ch=ch, s_t=s_t: lambda h: h.tensor_max(
                        out=tV_v(ch), in0=tV_v(ch), in1=s_t))(),
                      reads=[b_tmp[sl2], b_tV[sl]], writes=[b_tV[sl]])
                    E("pool", (lambda ch=ch, sv_t=sv_t: lambda h: h.tensor_mul(
                        out=tV_v(ch), in0=tV_v(ch), in1=sv_t))(),
                      reads=[b_tmp[2 + sl2], b_tV[sl]], writes=[b_tV[sl]])
                    E("dve", (lambda ch=ch: lambda h: h.tensor_tensor_scan(
                        out=tH_v(ch), data0=tA_v(ch), data1=tV_v(ch),
                        initial=hcarv[:, ch:ch + 1], op0=ALU.mult,
                        op1=ALU.add))(),
                      reads=[b_tA[sl], b_tV[sl], b_hcar[ch]], writes=[b_tH[sl]])
                    E("dve", (lambda ch=ch: lambda h: h.tensor_copy(
                        out=hcarv[:, ch:ch + 1], in_=tH_v(ch)[:, tt - 1:tt]))(),
                      reads=[b_tH[sl]], writes=[b_hcar[ch]])
                    E("dve", (lambda ch=ch: lambda h: h.tensor_tensor_scan(
                        out=tV_v(ch), data0=tA_v(ch), data1=zerosv[:, :],
                        initial=pcarv[:, ch:ch + 1], op0=ALU.mult,
                        op1=ALU.add))(),
                      reads=[b_tA[sl], b_zeros, b_pcar[ch], b_tV[sl]],
                      writes=[b_tV[sl]])
                    E("dve", (lambda ch=ch: lambda h: h.tensor_copy(
                        out=pcarv[:, ch:ch + 1], in_=tV_v(ch)[:, tt - 1:tt]))(),
                      reads=[b_tV[sl]], writes=[b_pcar[ch]])
                    E("dve", (lambda ch=ch, j=j: lambda h: h.tensor_add(
                        out=x_ch(ch, j), in0=x_ch(ch, j), in1=tH_v(ch)))(),
                      reads=[b_tH[sl], b_x[j]], writes=[b_x[j]])
                    E("pool", (lambda j=j, ch=ch: lambda h: h.dma_start(
                        out=pg_scr[j, ch, :, :], in_=tV_v(ch)))(),
                      reads=[b_tV[sl]], writes=[b_pg[j]], dma=True, sem="dq")
            if _s3 < 5:
                raise _EndPhase3()
            em.fence()
            exchange_prep(lambda ch: pcarv[:, ch:ch + 1], list(b_pcar),
                          lambda ch: hcarv[:, ch:ch + 1], list(b_hcar))
            exchange_send("cc3", cc3)
            prefetch_corr(pg_scr, b_pg[0])
            build_gain(VR_FFG, ROW_FF32, 1.0, 32.0)
            exchange_recv("cc3", cc3)
            combine_carries()

            em.fence()

            def ff_epilogue(j, dc, psum_ap, bk):
                sl = dc % 2
                t = tmpv[:, sl * tt: (sl + 1) * tt]
                E("dve", (lambda psum_ap=psum_ap, dc=dc, t=t:
                          lambda h: h.tensor_scalar_add(
                              out=t, in0=psum_ap, scalar1=vcol(VR_FB2, dc)))(),
                  reads=[bk, b_vcols], writes=[b_tmp[sl]])
                E("dve", (lambda j=j, dc=dc, t=t: lambda h: h.tensor_add(
                    out=x_ch(dc, j), in0=x_ch(dc, j), in1=t))(),
                  reads=[b_tmp[sl], b_x[j]], writes=[b_x[j]])
                xo = corrv[:, sl * tt: (sl + 1) * tt]
                E("pool", (lambda j=j, dc=dc, xo=xo: lambda h: h.dma_start(
                    out=xo, in_=xold_scr[j, dc, :, :]))(),
                  reads=[b_xold[j]], writes=[b_corr[sl]], dma=True,
                  sem=f"dcp{sl}")
                E("dve", (lambda j=j, dc=dc, xo=xo: lambda h: h.tensor_add(
                    out=x_ch(dc, j), in0=x_ch(dc, j), in1=xo))(),
                  reads=[b_corr[sl], b_x[j]], writes=[b_x[j]])

            emit_mlp("rms", wimg_ff1, b_wimg["ff1"], wimg_ff2, b_wimg["ff2"],
                     lambda ec: vcol(VR_FB1, ec), ff_epilogue, False,
                     gain_row=ROW_FF32,
                     pre_tile=lambda j: corr_tile(pg_scr, j,
                                                   pref=(j == 0)))

        except _EndPhase3:
            pass
        if upto >= 4:
            emit_mlp("ln", wimg_s1, b_wimg["s1"], wimg_s2, b_wimg["s2"],
                     lambda ec: vcol(VR_NB1, ec), ffn_epilogue(True), True)
        else:
            for j in range(nt):
                for dc in range(DC):
                    E("pool", (lambda j=j, dc=dc: lambda h: h.dma_start(
                        out=out_o.rearrange("(c p) t -> p c t", p=128)
                        [:, dc, j * tt:(j + 1) * tt],
                        in_=x_ch(dc, j)))(),
                      reads=[b_x[j]], writes=[b_out], dma=True, sem="dout")

        for s_ in ("dout", "dq", "dqh", "di0", "di1", "dcb", "dcv"):
            em.raw_wait("sp", s_, em.cnt[s_])

        with nc.Block() as block:
            def mk(eng):
                def run(h):
                    for it in em.items[eng]:
                        if it[0] == "w":
                            h.wait_ge(sems[it[1]], it[2])
                        else:
                            inst = it[1](h)
                            if it[2] is not None:
                                inst.then_inc(sems[it[2]], it[3])
                return run
            block.tensor(mk("pe"))
            block.scalar(mk("act"))
            block.vector(mk("dve"))
            block.gpsimd(mk("pool"))
            block.sync(mk("sp"))

    return nc


_NC_CACHE = {}


def _get_nc(key):
    if key not in _NC_CACHE:
        _NC_CACHE[key] = build_nc(*key)
    return _NC_CACHE[key]


def make_inputs(full_inputs, tcore):
    ii = full_inputs
    Tloc = tcore * NCORES
    x = np.asarray(ii["L0_attn_a_x"], dtype=np.float32).reshape(-1, D)[:Tloc]

    vrows = np.zeros((16, 4096), np.float32)
    vrows[VR_BDW, :D] = np.asarray(ii["L0_attn_b_conv_dw_b"], np.float32)
    vrows[VR_BPW, :D] = np.asarray(ii["L0_attn_b_conv_pw_b"], np.float32)
    vrows[VR_FB2, :D] = np.asarray(ii["L0_attn_b_ff_b2"], np.float32)
    vrows[VR_NB2, :D] = np.asarray(ii["L0_ffn_b2"], np.float32)
    vrows[VR_FB1, :] = np.asarray(ii["L0_attn_b_ff_b1"], np.float32)
    vrows[VR_NB1, :] = np.asarray(ii["L0_ffn_b1"], np.float32)
    cdw = np.asarray(ii["L0_attn_b_conv_dw_w"], np.float32)[:, 0, :]
    vrows[VR_T0, :D] = cdw[:, 0]
    vrows[VR_T1, :D] = cdw[:, 1]
    vrows[VR_T2, :D] = cdw[:, 2]
    vrows[VR_GAMMA, :D] = np.asarray(ii["L0_attn_a_gamma"], np.float32)
    vrows[VR_CONVG, :D] = np.asarray(ii["L0_attn_b_conv_norm_g"], np.float32)
    vrows[VR_GRUG, :D] = np.asarray(ii["L0_attn_b_gru_norm_g"], np.float32)
    vrows[VR_FFG, :D] = np.asarray(ii["L0_attn_b_ff_norm_g"], np.float32)

    wq_packed = pack_lhsT(np.asarray(ii["L0_attn_a_w_qkva"], np.float32))
    wq4 = wq_packed.reshape(24, 128, 1024)
    order = []
    for ch in range(8):
        order += [8 + ch, 16 + ch, ch]
    wq_tripped = np.ascontiguousarray(wq4[order].reshape(24 * 128, 1024))
    gru_packed = pack_lhsT(np.asarray(ii["L0_attn_b_gru_w"], np.float32))
    g4 = gru_packed.reshape(16, 128, 1024)
    gorder = []
    for ch in range(8):
        gorder += [ch, 8 + ch]
    gru_paired = np.ascontiguousarray(g4[gorder].reshape(16 * 128, 1024))
    packs = dict(
        wq_p=wq_tripped,
        pw_p=pack_lhsT(np.asarray(ii["L0_attn_b_conv_pw_w"], np.float32)),
        gru_p=gru_paired,
        ff1_p=pack_lhsT(np.asarray(ii["L0_attn_b_ff_w1"], np.float32)),
        ff2_p=pack_lhsT(np.asarray(ii["L0_attn_b_ff_w2"], np.float32)),
        n1_p=pack_lhsT(np.asarray(ii["L0_ffn_w1"], np.float32)),
        n2_p=pack_lhsT(np.asarray(ii["L0_ffn_w2"], np.float32)),
        n1_nat=np.ascontiguousarray(np.asarray(ii["L0_ffn_w1"], np.float32)),
        n2_nat=np.ascontiguousarray(np.asarray(ii["L0_ffn_w2"], np.float32)),
        vrows_in=vrows,
        ident_in=np.eye(128, dtype=np.float32),
    )

    in_maps = []
    for c in range(NCORES):
        mprev = np.zeros((128, 8), np.float32)
        if c > 0:
            mprev[:, c - 1] = 1.0
        smask = np.zeros((16, 1), np.float32)
        smask[2 * c: 2 * c + 2] = 1.0
        smr = np.zeros((1, 32), np.float32)
        smr[0, 2 * c] = 1.0
        smr[0, 16 + 2 * c + 1] = 1.0
        xs = np.ascontiguousarray(x[c * tcore:(c + 1) * tcore, :].T)
        m = dict(packs)
        m.update(x_in=xs, mprev_in=mprev, smask_in=smask, smr_in=smr)
        in_maps.append(m)
    return in_maps


def kernel(**inputs):
    tcore = T // NCORES
    in_maps = make_inputs(inputs, tcore)
    nc = _get_nc((tcore, 512, 4, 0))
    res = run_bass_kernel_spmd(nc, in_maps, core_ids=list(range(NCORES)))
    out = np.empty((T, D), np.float32)
    for c in range(NCORES):
        out[c * tcore:(c + 1) * tcore, :] = res.results[c]["out_o"].T
    return out.reshape(1, T, D)


# version 20
# speedup vs baseline: 1.0908x; 1.0908x over previous
import numpy as np
import concourse.bass as bass
import concourse.mybir as mybir
from concourse.bass_utils import run_bass_kernel_spmd

F32 = mybir.dt.float32
BF16 = mybir.dt.bfloat16
AF = mybir.ActivationFunctionType
ALU = mybir.AluOpType
AX = mybir.AxisListType

D = 1024
DC = 8
NCORES = 8
T = 16384

ENGINES = ("pe", "act", "dve", "pool", "sp")
SEMS = ("pe", "act", "dve", "pool", "cc",
        "dx", "dq", "dqh", "dcb", "dout", "dcv", "dcq", "dpf",
        "dcq0", "dcq1", "dcq2", "dcq3", "dcq4", "dcq5", "dcq6", "dcq7",
        "dw0", "dw1",
        "ds0", "ds1", "ds2", "ds3", "ds4", "ds5",
        "di0", "di1", "di2", "di3", "dcr0", "dcr1",
        "dcp0", "dcp1")


class Buf:
    __slots__ = ("w", "r", "name")

    def __init__(self, name=""):
        self.w = {}
        self.r = {}
        self.name = name


class Em:
    def __init__(self):
        self.items = {e: [] for e in ENGINES}
        self.cnt = {s: 0 for s in SEMS}
        self.waited = {e: {} for e in ENGINES}

    def _wait(self, eng, sem, val):
        if val <= 0:
            return
        assert val <= self.cnt[sem], (eng, sem, val, self.cnt[sem])
        if self.waited[eng].get(sem, 0) >= val:
            return
        self.waited[eng][sem] = val
        self.items[eng].append(("w", sem, val))

    TOTAL_SEMS = ("dx", "dq", "dqh", "dcb", "dout", "dcv", "dcq", "dpf")

    def _wait_edge(self, eng, sem, val):
        if sem in self.TOTAL_SEMS:
            val = self.cnt[sem]
        self._wait(eng, sem, val)

    def op(self, eng, fn, reads=(), writes=(), dma=False, inc=True, sem=None,
           incval=None):
        assert sem is not None or not dma, "dma ops must name a channel sem"
        if sem is None:
            sem = eng
        if incval is None:
            incval = 16 if dma else 1
        for b in reads:
            for s, c in b.w.items():
                self._wait_edge(eng, s, c)
        for b in writes:
            for s, c in b.w.items():
                self._wait_edge(eng, s, c)
            for s, c in b.r.items():
                self._wait_edge(eng, s, c)
        if inc:
            self.cnt[sem] += incval
            self.items[eng].append(("c", fn, sem, incval))
            cur = (sem, self.cnt[sem])
            for b in writes:
                if b.name.startswith("w_img"):
                    b.w[sem] = self.cnt[sem]
                else:
                    b.w = {sem: self.cnt[sem]}
                b.r = {}
            for b in reads:
                b.r[sem] = cur[1]
            return cur
        self.items[eng].append(("c", fn, None, 0))
        return None

    def fence(self):
        for eng in ENGINES:
            for sem in SEMS:
                self._wait(eng, sem, self.cnt[sem])

    def raw_wait(self, eng, sem, val):
        self._wait(eng, sem, val)


def pack_lhsT(w):
    E, K = w.shape
    w4 = w.reshape(E // 128, 128, K // 128, 128).transpose(0, 3, 2, 1)
    return np.ascontiguousarray(w4.reshape(E, K))


VR_BDW, VR_BPW, VR_FB2, VR_NB2, VR_FB1, VR_NB1 = 0, 1, 2, 3, 4, 5
VR_T0, VR_T1, VR_T2 = 6, 7, 8
VR_GAMMA, VR_CONVG, VR_GRUG, VR_FFG = 9, 10, 11, 12


SUB3 = 9


def build_nc(tcore=2048, tt=512, upto=4, debug=0):
    nt = tcore // tt
    nc = bass.Bass("TRN2", target_bir_lowering=False, num_devices=8)

    x_in = nc.dram_tensor("x_in", [D, tcore], F32, kind="ExternalInput").ap()
    out_o = nc.dram_tensor("out_o", [D, tcore], F32, kind="ExternalOutput").ap()
    wq_p = nc.dram_tensor("wq_p", [3 * D, D], F32, kind="ExternalInput").ap()
    pw_p = nc.dram_tensor("pw_p", [D, D], F32, kind="ExternalInput").ap()
    gru_p = nc.dram_tensor("gru_p", [2 * D, D], F32, kind="ExternalInput").ap()
    ff1_p = nc.dram_tensor("ff1_p", [4 * D, D], F32, kind="ExternalInput").ap()
    ff2_p = nc.dram_tensor("ff2_p", [D, 4 * D], F32, kind="ExternalInput").ap()
    n1_p = nc.dram_tensor("n1_p", [4 * D, D], F32, kind="ExternalInput").ap()
    n2_p = nc.dram_tensor("n2_p", [D, 4 * D], F32, kind="ExternalInput").ap()
    n1_nat = nc.dram_tensor("n1_nat", [4 * D, D], F32, kind="ExternalInput").ap()
    n2_nat = nc.dram_tensor("n2_nat", [D, 4 * D], F32, kind="ExternalInput").ap()
    vrows_in = nc.dram_tensor("vrows_in", [16, 4096], F32, kind="ExternalInput").ap()
    ident_in = nc.dram_tensor("ident_in", [128, 128], F32, kind="ExternalInput").ap()
    mprev_in = nc.dram_tensor("mprev_in", [128, 8], F32, kind="ExternalInput").ap()
    smask_in = nc.dram_tensor("smask_in", [16, 1], F32, kind="ExternalInput").ap()
    smr_in = nc.dram_tensor("smr_in", [1, 32], F32, kind="ExternalInput").ap()

    wimg_q = nc.dram_tensor("wimg_q", [3 * D, D], BF16).ap()
    wimg_pw = nc.dram_tensor("wimg_pw", [D, D], BF16).ap()
    wimg_gru = nc.dram_tensor("wimg_gru", [2 * D, D], BF16).ap()
    wimg_ff1 = nc.dram_tensor("wimg_ff1", [4 * D, D], BF16).ap()
    wimg_ff2 = nc.dram_tensor("wimg_ff2", [D, 4 * D], BF16).ap()
    wimg_s1 = nc.dram_tensor("wimg_s1", [4 * D, D], BF16).ap()
    wimg_s2 = nc.dram_tensor("wimg_s2", [D, 4 * D], BF16).ap()
    qp_scr = nc.dram_tensor("qp_scr", [nt, DC, 128, tt], F32).ap()
    pg_scr = nc.dram_tensor("pg_scr", [nt, DC, 128, tt], F32).ap()
    xold_scr = nc.dram_tensor("xold_scr", [nt, DC, 128, tt], F32).ap()
    cc1 = nc.dram_tensor("cc1", [16, D], F32).ap()
    cc2 = nc.dram_tensor("cc2", [16, D], F32).ap()
    cc3 = nc.dram_tensor("cc3", [16, D], F32).ap()
    scr2a = nc.dram_tensor("scr2a", [1, 2 * D], F32).ap()
    scr2b = nc.dram_tensor("scr2b", [1, 2 * D], F32).ap()
    scr2c = nc.dram_tensor("scr2c", [1, 2 * D], F32).ap()

    from contextlib import ExitStack
    es = ExitStack()
    with es:
        def sb(name, shape, dtype=F32):
            return es.enter_context(nc.sbuf_tensor(name, shape, dtype))

        x_sb = sb("x_sb", [128, DC * tcore])
        U = sb("U", [128, 6688])
        stg = sb("stg", [128, 2 * 1024])
        cv16 = sb("cv16", [128, 4 * 1024], BF16)
        wstr = sb("wstr", [128, 2 * 4096], BF16)
        xnb = sb("xnb", [128, DC * tt], BF16)
        hb16 = sb("hb16", [128, 16384], BF16)
        sqb = sb("sqb", [128, 2 * tt])
        tmp5 = sb("tmp5", [128, 4 * tt])
        corr = sb("corr", [128, 2 * tt])
        vcols = sb("vcols", [128, 512])
        Rg = sb("Rg", [128, 2176])
        Rs = sb("Rs", [128, 3072])
        ident = sb("ident", [128, 128])
        ones_c = sb("ones_c", [128, 1])
        ones_h = sb("ones_h", [128, 1], BF16)
        hcar = sb("hcar", [128, 8])
        pcar = sb("pcar", [128, 8])
        hsel = sb("hsel", [128, 8])
        mprev = sb("mprev", [128, 8])
        smask = sb("smask", [16, 1])
        smr = sb("smr", [1, 32])
        G = sb("G", [128, 128])
        ccst = sb("ccst", [16, D])
        alpha1 = sb("alpha1", [128, 32])
        alpha2 = sb("alpha2", [128, 8])
        uh2 = sb("uh2", [128, 16])
        epsc = sb("epsc", [128, 4])

        ps = [es.enter_context(nc.psum_tensor(f"ps{i}", [128, 512], F32))
              for i in range(8)]
        sems = {s: es.enter_context(nc.semaphore(s)) for s in SEMS}

        em = Em()

        bank = [Buf(f"bank{i}") for i in range(8)]
        b_x = [Buf(f"x{j}") for j in range(nt)]
        b_tA = [Buf(f"tA{i}") for i in range(4)]
        b_tV = [Buf(f"tV{i}") for i in range(4)]
        b_tQ = [Buf(f"tQ{i}") for i in range(4)]
        b_tH = [Buf(f"tH{i}") for i in range(4)]
        b_tq = [Buf(f"tq{i}") for i in range(4)]
        b_fs = [Buf(f"fs{i}") for i in range(6)]
        b_cv16 = [Buf(f"cv16{i}") for i in range(4)]
        b_wstr = [Buf("wstr0"), Buf("wstr1")]
        b_xnb = [Buf(f"xnb{c}") for c in range(DC)]
        b_xnbh = [Buf(f"xnbh{c}") for c in range(DC)]
        b_hbf = [Buf(f"hbf{c}") for c in range(32)]
        b_sqb = [Buf("sqb0"), Buf("sqb1")]
        b_tmp = [Buf(f"tmp{i}") for i in range(4)]
        b_corr = [Buf("corr0"), Buf("corr1")]
        b_uext = [Buf(f"uext{c}") for c in range(DC)]
        b_ycv = Buf("ycv")
        b_ybf = [Buf(f"ybf{c}") for c in range(DC)]
        b_vrows = Buf("vrows")
        b_vcols = Buf("vcols")
        b_Rg = Buf("Rg")
        b_ident = Buf("ident")
        b_ones = Buf("ones")
        b_zeros = Buf("zeros")
        b_hcar = [Buf(f"hcar{c}") for c in range(DC)]
        b_pcar = [Buf(f"pcar{c}") for c in range(DC)]
        b_hsel = Buf("hsel")
        b_mprev = Buf("mprev")
        b_smask = Buf("smask")
        b_smr = Buf("smr")
        b_G = Buf("G")
        b_ccst = Buf("ccst")
        b_rows2 = Buf("rows2")
        b_alpha1 = Buf("alpha1")
        b_alpha2 = Buf("alpha2")
        b_uh2 = Buf("uh2")
        b_epsc = Buf("epsc")
        b_wimg = {k: Buf("w_img_" + k) for k in
                  ("q", "pw", "gru", "ff1", "ff2", "s1", "s2")}
        b_wq_ch = [Buf(f"wq_ch{c}") for c in range(DC)]
        b_qp = [Buf(f"qp{j}") for j in range(nt)]
        b_pg = [Buf(f"pg{j}") for j in range(nt)]
        b_xold = [Buf(f"xold{j}") for j in range(nt)]
        b_cc = {"cc1": Buf("cc1"), "cc2": Buf("cc2"), "cc3": Buf("cc3")}
        b_scr2 = {"cc1": Buf("s2a"), "cc2": Buf("s2b"), "cc3": Buf("s2c")}
        b_out = Buf("out")
        RSB = {}

        xv = x_sb.ap()

        def x_ch(ch, j, n=None):
            n = tt if n is None else n
            return xv[:, ch * tcore + j * tt: ch * tcore + j * tt + n]

        def x_cols(ch, c0, n):
            return xv[:, ch * tcore + c0: ch * tcore + c0 + n]

        Uv = U.ap()
        vrowsv = Uv[0:16, 0:4096]
        UE = DC * (tt + 4)
        R2OFF = max(8 * tt, UE + tt)

        def tsl(base, hoff, ch, n=tt):
            s = ch % 4
            if s < 2:
                return Uv[:, base + s * tt: base + s * tt + n]
            return hbfc[:, hoff + (s - 2) * tt: hoff + (s - 2) * tt + n]

        def tA_v(ch, n=tt):
            return tsl(0, 4 * tt, ch, n)

        def tV_v(ch, n=tt):
            return tsl(2 * tt, 6 * tt, ch, n)

        def tQ_v(ch, n=tt):
            return tsl(4 * tt, 8 * tt, ch, n)

        def tH_v(ch, n=tt):
            return tsl(6 * tt, 10 * tt, ch, n)

        def tq_v(ch, n=tt):
            return hbfc[:, 12 * tt + (ch % 4) * tt: 12 * tt + (ch % 4) * tt + n]

        def u_ch(ch, c0, n):
            return Uv[:, ch * (tt + 4) + c0: ch * (tt + 4) + c0 + n]

        ycvv = Uv[:, UE:UE + tt]
        rows2v = Uv[0:1, R2OFF:R2OFF + 2 * D]
        zerosv = Uv[:, R2OFF:R2OFF + tt]
        stg2v = Uv[:, R2OFF + 2 * D - 1024: R2OFF + 2 * D]

        stgv = stg.ap()
        cv16v = cv16.ap()
        stgb = stg.ap().bitcast(BF16)
        Ub = U.ap().bitcast(BF16)
        FS_OFF = [0, 1024, 2048, 3072, 2 * 5152, 2 * 5152 + 1024]

        def fs_v(slot, n):
            base = stgb if slot < 4 else Ub
            return base[:, FS_OFF[slot]: FS_OFF[slot] + n]
        hbfc = hb16.ap().bitcast(F32)
        wstrv = wstr.ap()
        xnbv = xnb.ap()
        hbfv = hb16.ap()

        def hbf_v(ec):
            return hbfv[:, ec * tt: (ec + 1) * tt]

        def ybf_v(ch):
            return hbfv[:, ch * tt: (ch + 1) * tt]

        sqv = sqb.ap()
        tmpv = tmp5.ap()
        corrv = corr.ap()
        vcolsv = vcols.ap()
        Rgv = Rg.ap()
        Rsv = Rs.ap()
        identv = ident.ap()
        onesv = ones_c.ap()
        oneshv = ones_h.ap()
        sqhv = sqb.ap().bitcast(BF16)
        hcarv, pcarv, hselv = hcar.ap(), pcar.ap(), hsel.ap()
        mprevv, smaskv = mprev.ap(), smask.ap()
        smrv = smr.ap()
        Gv, ccstv = G.ap(), ccst.ap()
        a1v, a2v = alpha1.ap(), alpha2.ap()
        uh2v = uh2.ap()
        epscv = epsc.ap()
        EPS_COL = {1e-30: 0, 1e-5: 1}

        ROW_GAMMA = (0, 0)
        ROW_CONVG = (0, 1024)
        ROW_GRU32 = (0, 0)
        ROW_FF32 = (0, 1024)

        def gain_slice(row, ch):
            base, col = row
            return Rgv[base:base + 1, col + ch * 128: col + (ch + 1) * 128]

        def ones_row():
            return Rgv[0:1, 2048:2048 + 128]

        RS = {"m": (0, 0), "vtmp": (0, 512),
              "invE": (0, 1024), "minvE": (0, 1536), "inv0": (0, 2048),
              "minv0": (0, 2560), "norm": (0, 512), "r": (0, 0)}
        for k in RS:
            RSB[k] = Buf("rs_" + k)
        RSB["norm"] = RSB["vtmp"]
        RSB["r"] = RSB["m"]

        def rs(name, n=512):
            base, col = RS[name]
            return Rsv[base:base + 1, col: col + n]

        def E(eng, fn, reads=(), writes=(), dma=False, inc=True, sem=None,
              incval=None):
            return em.op(eng, fn, reads=reads, writes=writes, dma=dma, inc=inc,
                         sem=sem, incval=incval)

        def matmul_group2(items, bank_buf):
            nit = len(items)
            allreads = []
            for it in items:
                for b in it[3]:
                    if b not in allreads:
                        allreads.append(b)
            for i, (o, l, r, rd) in enumerate(items):
                st, sp = (i == 0), (i == nit - 1)
                fn = (lambda o=o, l=l, r=r, st=st, sp=sp:
                      lambda h: h.matmul(o, l, r, start=st, stop=sp))()
                if i == nit - 1:
                    E("pe", fn, reads=allreads, writes=[bank_buf], inc=True)
                else:
                    for b in rd:
                        for s_, c_ in b.w.items():
                            em._wait_edge("pe", s_, c_)
                    if i == 0:
                        for s_, c_ in bank_buf.w.items():
                            em._wait_edge("pe", s_, c_)
                        for s_, c_ in bank_buf.r.items():
                            em._wait_edge("pe", s_, c_)
                    em.items["pe"].append(("c", fn, None, 0))

        E("sp", lambda h: h.dma_start(out=ident[:], in_=ident_in[:, :]),
          writes=[b_ident], dma=True, sem="dx")
        E("sp", lambda h: h.dma_start(out=vrowsv, in_=vrows_in[:, :]),
          writes=[b_vrows], dma=True, sem="dx")
        E("sp", lambda h: h.dma_start(out=mprev[:], in_=mprev_in[:, :]),
          writes=[b_mprev], dma=True, sem="dx")
        E("sp", lambda h: h.dma_start(out=smask[:], in_=smask_in[:, :]),
          writes=[b_smask], dma=True, sem="dx")
        E("sp", lambda h: h.dma_start(out=smr[:], in_=smr_in[:, :]),
          writes=[b_smr], dma=True, sem="dx")
        for ch in range(DC):
            E("sp", (lambda ch=ch: lambda h: h.dma_start(
                out=xv[:, ch * tcore:(ch + 1) * tcore],
                in_=x_in[ch * 128:(ch + 1) * 128, :]))(),
              writes=[b_x[j] for j in range(nt)], dma=True, sem="dx")
        E("dve", lambda h: h.memset(ones_c[:], 1.0), writes=[b_ones])
        E("dve", lambda h: h.memset(ones_h[:], 1.0), writes=[b_ones])
        E("dve", lambda h: h.memset(epscv[:, 0:1], 1e-30), writes=[b_epsc])
        E("dve", lambda h: h.memset(epscv[:, 1:2], 1e-5), writes=[b_epsc])
        E("dve", lambda h: h.memset(epscv[:, 2:3], -0.5), writes=[b_epsc])
        E("dve", lambda h: h.memset(Rg[:], 0.0), writes=[b_Rg])

        for s in range(32):
            E("pe", (lambda s=s: lambda h: h.transpose(
                ps[6][:, 0:16], vrowsv[0:16, s * 128:(s + 1) * 128],
                identv[0:16, 0:16]))(),
              reads=[b_vrows, b_ident], writes=[bank[6]])
            E("dve", (lambda s=s: lambda h: h.tensor_copy(
                out=vcolsv[:, s * 16:(s + 1) * 16], in_=ps[6][:, 0:16]))(),
              reads=[bank[6]], writes=[b_vcols])

        def vcol(r, ec):
            return vcolsv[:, ec * 16 + r: ec * 16 + r + 1]

        def build_gain(vr, row, addv, mulv):
            for ch in range(DC):
                E("pe", (lambda ch=ch, vr=vr: lambda h: h.transpose(
                    ps[7][0:1, 0:128], vcol(vr, ch), identv[:, :]))(),
                  reads=[b_vcols, b_ident], writes=[bank[7]])
                base, col = row
                dst = Rgv[base:base + 1, col + ch * 128: col + (ch + 1) * 128]
                if addv == 0.0 and mulv == 1.0:
                    E("dve", (lambda dst=dst: lambda h: h.tensor_copy(
                        out=dst, in_=ps[7][0:1, 0:128]))(),
                      reads=[bank[7]], writes=[b_Rg])
                else:
                    E("dve", (lambda dst=dst, addv=addv, mulv=mulv:
                              lambda h: h.tensor_scalar(
                                  out=dst, in0=ps[7][0:1, 0:128], scalar1=addv,
                                  scalar2=mulv, op0=ALU.add, op1=ALU.mult))(),
                      reads=[bank[7]], writes=[b_Rg])

        build_gain(VR_GAMMA, ROW_GAMMA, 0.0, 1.0)
        build_gain(VR_CONVG, ROW_CONVG, 1.0, 1.0)
        _ones_ap = Rgv[0:1, 2048:2048 + 128]
        E("dve", (lambda ap=_ones_ap: lambda h: h.memset(ap, 1.0))(),
          writes=[b_Rg])

        fslot = [0]
        cvi_slot = [0]

        def sign_unit(src_ap, dst_ap, rows0, ncols, img_buf, col0=0):
            st = {}

            def load():
                slot = fslot[0]
                fslot[0] = (fslot[0] + 1) % 6
                st["slot"] = slot
                st["sgv"] = fs_v(slot, ncols)
                E("pool", (lambda sgv=st["sgv"]: lambda h: h.dma_start(
                    out=sgv, in_=src_ap[rows0:rows0 + 128,
                                        col0:col0 + ncols]))(),
                  writes=[b_fs[slot]], dma=True, sem=f"ds{slot}")

            def compute():
                slot, sgv = st["slot"], st["sgv"]
                cslot = cvi_slot[0]
                cvi_slot[0] = (cvi_slot[0] + 1) % 4
                cg = cv16v[:, cslot * 1024: cslot * 1024 + ncols]
                E("act", (lambda cg=cg, sgv=sgv: lambda h: h.activation(
                    out=cg, in_=sgv, func=AF.Sign))(),
                  reads=[b_fs[slot]], writes=[b_cv16[cslot]])

                def write():
                    E("sp", (lambda cg=cg: lambda h: h.dma_start(
                        out=dst_ap[rows0:rows0 + 128, col0:col0 + ncols],
                        in_=cg))(),
                      reads=[b_cv16[cslot]], writes=[img_buf], dma=True,
                      sem=f"di{cslot}")
                return write
            return (load, compute)

        def alpha_unit(nat_ap, rows0, col0, ncols, dst_col, acc, bufa):
            st = {}

            def load():
                slot = fslot[0]
                fslot[0] = (fslot[0] + 1) % 6
                st["slot"] = slot
                st["sgv"] = fs_v(slot, ncols)
                E("pool", (lambda sgv=st["sgv"]: lambda h: h.dma_start(
                    out=sgv, in_=nat_ap[rows0:rows0 + 128,
                                        col0:col0 + ncols]))(),
                  writes=[b_fs[slot]], dma=True, sem=f"ds{slot}")

            def compute():
                slot, sgv = st["slot"], st["sgv"]
                if not acc:
                    E("act", (lambda sgv=sgv: lambda h: h.activation(
                        out=sgv, in_=sgv, func=AF.Abs, accum_out=dst_col))(),
                      reads=[b_fs[slot]], writes=[b_fs[slot], bufa])
                else:
                    t = tmpv[:, 3 * tt: 3 * tt + 1]
                    E("act", (lambda sgv=sgv, t=t: lambda h: h.activation(
                        out=sgv, in_=sgv, func=AF.Abs, accum_out=t))(),
                      reads=[b_fs[slot]], writes=[b_fs[slot], b_tmp[3]])
                    E("dve", (lambda t=t: lambda h: h.tensor_add(
                        out=dst_col, in0=dst_col, in1=t))(),
                      reads=[b_tmp[3], bufa], writes=[bufa])
                return None
            return (load, compute)

        def cast_img(src_ap, dst_ap, img_buf, sem="dcv", rows=None):
            if rows is None:
                sl_ = (slice(None), slice(None))
            else:
                sl_ = (slice(rows[0], rows[1]), slice(None))
            E("pool", (lambda sl_=sl_: lambda h: h.dma_start(
                out=dst_ap[sl_], in_=src_ap[sl_]))(),
              writes=[img_buf], dma=True, sem=sem)

        def cast_pieces(src_ap, dst_ap, img_buf, piece_rows):
            R = src_ap.shape[0]
            out = []
            for r0 in range(0, R, piece_rows):
                r1 = min(r0 + piece_rows, R)
                out.append((lambda r0=r0, r1=r1: cast_img(
                    src_ap, dst_ap, img_buf, rows=(r0, r1))))
            return out

        class Feeder:

            def __init__(self, units):
                self.units = list(units)
                self.li = 0
                self.ci = 0
                self.pending = []

            def take(self, k):
                if self.ci >= len(self.units):
                    while self.pending:
                        self.pending.pop(0)()
                else:
                    while len(self.pending) > 1:
                        self.pending.pop(0)()
                lim = self.li
                self.li = min(self.li + k, len(self.units))
                for u in self.units[lim:self.li]:
                    u[0]()
                for u in self.units[self.ci:lim]:
                    w = u[1]()
                    if w is not None:
                        self.pending.append(w)
                self.ci = lim

            def drain(self):
                while self.ci < len(self.units):
                    self.take(3)
                for w in self.pending:
                    w()
                self.pending = []

        for _c in (0, 1):
            cast_img(wq_p, wimg_q, b_wq_ch[_c], sem=f"dcq{_c}",
                     rows=(_c * 384, (_c + 1) * 384))
        _signs = ([sign_unit(n1_p, wimg_s1, ec * 128, D, b_wimg["s1"])
                   for ec in range(32)]
                  + [sign_unit(n2_p, wimg_s2, ec * 128, 1024, b_wimg["s2"],
                               col0=hh * 1024)
                     for ec in range(8) for hh in range(4)])
        _alphas = ([alpha_unit(n1_nat, ec * 128, 0, D, a1v[:, ec:ec + 1],
                               False, b_alpha1) for ec in range(32)]
                   + [alpha_unit(n2_nat, ec * 128, p * 1024, 1024,
                                 a2v[:, ec:ec + 1], p > 0, b_alpha2)
                      for ec in range(8) for p in range(4)])
        _s1signs = _signs[:32]
        _s2signs = _signs[32:]
        units_s = []
        for i in range(32):
            units_s.append(_s1signs[i])
            units_s.append(_alphas[2 * i])
            units_s.append(_alphas[2 * i + 1])
        units_b = _s2signs

        BK_S1, BK_S2 = 4, 5
        wl_slot = [0]

        def wload(img_ap, img_buf, rows0, k, col0=0, nrow=128):
            slot = wl_slot[0]
            wl_slot[0] ^= 1
            dst = wstrv[:, slot * 4096: slot * 4096 + k]
            if nrow != 128:
                nc_ = nrow // 128
                ima = img_ap[rows0:rows0 + nrow, col0:col0 + k // nc_]
                ima = ima.rearrange("(c p) k -> p c k", p=128)
                dst3 = dst.rearrange("p (c k) -> p c k", c=nc_)
                E("sp", (lambda dst3=dst3, ima=ima: lambda h: h.dma_start(
                    out=dst3, in_=ima))(),
                  reads=[img_buf], writes=[b_wstr[slot]], dma=True,
                  sem=f"dw{slot}")
                return dst, b_wstr[slot]
            E("sp", (lambda dst=dst: lambda h: h.dma_start(
                out=dst, in_=img_ap[rows0:rows0 + 128, col0:col0 + k]))(),
              reads=[img_buf], writes=[b_wstr[slot]], dma=True,
              sem=f"dw{slot}")
            return dst, b_wstr[slot]

        def site_stats(n, src_fn, src_bufs, want_ln, eps_list):
            if want_ln:
                items = [(ps[BK_S1][0:1, 0:n], onesv[:, :], src_fn(ch),
                          [b_ones] + src_bufs) for ch in range(DC)]
                matmul_group2(items, bank[BK_S1])
            for ch in range(DC):
                slot = ch % 2
                sq = sqhv[:, slot * tt: slot * tt + n]
                E("act", (lambda sq=sq, ch=ch: lambda h: h.activation(
                    out=sq, in_=src_fn(ch), func=AF.Square))(),
                  reads=src_bufs, writes=[b_sqb[slot]])
                fn = (lambda sq=sq, ch=ch, n=n: lambda h: h.matmul(
                    ps[BK_S2][0:1, 0:n], oneshv[:, :], sq,
                    start=(ch == 0), stop=(ch == DC - 1)))()
                E("pe", fn, reads=[b_ones, b_sqb[slot]],
                  writes=[bank[BK_S2]], inc=True)
            if want_ln:
                E("dve", (lambda n=n: lambda h: h.tensor_scalar_mul(
                    out=rs("m", n), in0=ps[BK_S1][0:1, 0:n], scalar1=1.0 / D))(),
                  reads=[bank[BK_S1]], writes=[RSB["m"]])
                E("pool", (lambda n=n: lambda h: h.tensor_mul(
                    out=rs("vtmp", n), in0=rs("m", n), in1=rs("m", n)))(),
                  reads=[RSB["m"]], writes=[RSB["vtmp"]])
                E("dve", (lambda n=n: lambda h: h.scalar_tensor_tensor(
                    out=rs("vtmp", n), in0=ps[BK_S2][0:1, 0:n],
                    scalar=1.0 / D, in1=rs("vtmp", n),
                    op0=ALU.mult, op1=ALU.subtract))(),
                  reads=[bank[BK_S2], RSB["vtmp"]], writes=[RSB["vtmp"]])
                for (inv_name, minv_name, eps) in eps_list:
                    ec_ = EPS_COL[eps]
                    E("act", (lambda n=n, ec_=ec_, minv_name=minv_name:
                              lambda h: h.activation(
                                  out=rs(minv_name, n), in_=rs("vtmp", n),
                                  func=AF.Ln, bias=epscv[0:1, ec_:ec_ + 1]))(),
                      reads=[RSB["vtmp"], b_epsc], writes=[RSB[minv_name]])
                    E("act", (lambda n=n, inv_name=inv_name,
                              minv_name=minv_name: lambda h: h.activation(
                                  out=rs(inv_name, n), in_=rs(minv_name, n),
                                  func=AF.Exp, scale=epscv[0:1, 2:3]))(),
                      reads=[RSB[minv_name], b_epsc], writes=[RSB[inv_name]])
                    E("pool", (lambda n=n, inv_name=inv_name,
                               minv_name=minv_name: lambda h: h.tensor_mul(
                                   out=rs(minv_name, n), in0=rs("m", n),
                                   in1=rs(inv_name, n)))(),
                      reads=[RSB["m"], RSB[inv_name]], writes=[RSB[minv_name]])
            else:
                E("dve", (lambda n=n: lambda h: h.tensor_scalar_max(
                    out=rs("norm", n), in0=ps[BK_S2][0:1, 0:n],
                    scalar1=1e-24))(),
                  reads=[bank[BK_S2]], writes=[RSB["norm"]])
                E("act", (lambda n=n: lambda h: h.activation(
                    out=rs("minvE", n), in_=rs("norm", n), func=AF.Ln))(),
                  reads=[RSB["norm"]], writes=[RSB["minvE"]])
                E("act", (lambda n=n: lambda h: h.activation(
                    out=rs("r", n), in_=rs("minvE", n), func=AF.Exp,
                    scale=epscv[0:1, 2:3]))(),
                  reads=[RSB["minvE"], b_epsc], writes=[RSB["r"]])

        def apply_ln(n, src_fn, src_bufs, gain_row, inv_name, minv_name,
                     out_fn, out_bufs, bb=6, bc=7, tmp_i=0):
            for ch in range(DC):
                p2 = 2 * (ch % 2)
                bbx, bcx = 4 + p2, 5 + p2
                g = gain_slice(gain_row, ch) if gain_row is not None else ones_row()
                E("pe", (lambda g=g, n=n, inv_name=inv_name, bbx=bbx:
                         lambda h: h.matmul(ps[bbx][:, 0:n], g, rs(inv_name, n),
                                            start=True, stop=True))(),
                  reads=[b_Rg, RSB[inv_name]], writes=[bank[bbx]])
                E("pe", (lambda g=g, n=n, minv_name=minv_name, bcx=bcx:
                         lambda h: h.matmul(ps[bcx][:, 0:n], g, rs(minv_name, n),
                                            start=True, stop=True))(),
                  reads=[b_Rg, RSB[minv_name]], writes=[bank[bcx]])
                t = tmpv[:, (tmp_i + ch % 2) * tt: (tmp_i + ch % 2) * tt + n]
                E("dve", (lambda t=t, ch=ch, n=n, bbx=bbx: lambda h: h.tensor_mul(
                    out=t, in0=src_fn(ch), in1=ps[bbx][:, 0:n]))(),
                  reads=src_bufs + [bank[bbx]], writes=[b_tmp[tmp_i + ch % 2]])
                E("dve", (lambda t=t, ch=ch, n=n, bcx=bcx: lambda h: h.tensor_sub(
                    out=out_fn(ch), in0=t, in1=ps[bcx][:, 0:n]))(),
                  reads=[b_tmp[tmp_i + ch % 2], bank[bcx]], writes=[out_bufs[ch]])

        def apply_rms(n, src_fn, src_bufs, gain_row, out_fn, out_bufs, bb=6):
            for ch in range(DC):
                bbx = 4 + (ch % 4)
                g = gain_slice(gain_row, ch)
                E("pe", (lambda g=g, n=n, bbx=bbx: lambda h: h.matmul(
                    ps[bbx][:, 0:n], g, rs("r", n), start=True, stop=True))(),
                  reads=[b_Rg, RSB["r"]], writes=[bank[bbx]])
                E("dve", (lambda ch=ch, n=n, bbx=bbx: lambda h: h.tensor_mul(
                    out=out_fn(ch), in0=src_fn(ch), in1=ps[bbx][:, 0:n]))(),
                  reads=src_bufs + [bank[bbx]], writes=[out_bufs[ch]])

        def exchange_prep(colA_fn, colA_bufs, colB_fn, colB_bufs):
            al_w = [b_rows2, b_zeros, b_fs[4], b_fs[5]]
            for ch in range(DC):
                bk_ = 6 if ch < 4 else 5
                co = (ch % 4) * 128
                E("pe", (lambda ch=ch: lambda h: h.transpose(
                    ps[4][0:1, 0:128], colA_fn(ch), identv[:, :]))(),
                  reads=colA_bufs + [b_ident], writes=[bank[4]])
                E("dve", (lambda ch=ch: lambda h: h.tensor_copy(
                    out=rows2v[0:1, ch * 128:(ch + 1) * 128],
                    in_=ps[4][0:1, 0:128]))(),
                  reads=[bank[4]], writes=al_w)
                E("pe", (lambda ch=ch: lambda h: h.transpose(
                    ps[7][0:1, 0:128], colB_fn(ch), identv[:, :]))(),
                  reads=colB_bufs + [b_ident], writes=[bank[7]])
                E("dve", (lambda ch=ch: lambda h: h.tensor_copy(
                    out=rows2v[0:1, D + ch * 128: D + (ch + 1) * 128],
                    in_=ps[7][0:1, 0:128]))(),
                  reads=[bank[7]], writes=al_w)
                E("pe", (lambda ch=ch, bk_=bk_, co=co: lambda h: h.matmul(
                    ps[bk_][0:16, co:co + 128], smrv[0:1, 0:16],
                    rows2v[0:1, ch * 128:(ch + 1) * 128],
                    start=True, stop=False))(),
                  reads=[b_smr, b_rows2], writes=[bank[bk_]])
                E("pe", (lambda ch=ch, bk_=bk_, co=co: lambda h: h.matmul(
                    ps[bk_][0:16, co:co + 128], smrv[0:1, 16:32],
                    rows2v[0:1, D + ch * 128: D + (ch + 1) * 128],
                    start=False, stop=True))(),
                  reads=[b_smr, b_rows2], writes=[bank[bk_]])

        def exchange_send(tag, cc_ap):
            E("dve", lambda h: h.tensor_copy(
                out=ccstv[0:16, 0:512], in_=ps[6][0:16, 0:512]),
              reads=[bank[6]], writes=[b_ccst])
            E("dve", lambda h: h.tensor_copy(
                out=ccstv[0:16, 512:1024], in_=ps[5][0:16, 0:512]),
              reads=[bank[5]], writes=[b_ccst])
            E("sp", (lambda cc_ap=cc_ap: lambda h: h.dma_start(
                out=cc_ap[:, :], in_=ccstv))(),
              reads=[b_ccst], writes=[b_cc[tag]], dma=True, sem="dcb")
            flat = cc_ap.rearrange("a b -> (a b)").rearrange("(o f) -> o f",
                                                             o=1)
            E("pool", (lambda flat=flat: lambda h: h.collective_compute(
                "AllReduce", ALU.add, replica_groups=[list(range(8))],
                ins=[flat], outs=[flat]))(),
              reads=[b_cc[tag]], writes=[b_cc[tag]], sem="cc", incval=1)

        def exchange_recv(tag, cc_ap):
            E("sp", (lambda cc_ap=cc_ap: lambda h: h.dma_start(
                out=ccst[:], in_=cc_ap[:, :]))(),
              reads=[b_cc[tag]], writes=[b_ccst], dma=True, sem="dcb")
            for ch in range(DC):
                E("pe", (lambda ch=ch: lambda h: h.transpose(
                    ps[7][:, 0:16], ccstv[0:16, ch * 128:(ch + 1) * 128],
                    identv[0:16, 0:16]))(),
                  reads=[b_ccst, b_ident], writes=[bank[7]])
                E("dve", (lambda ch=ch: lambda h: h.tensor_copy(
                    out=Gv[:, ch * 16:(ch + 1) * 16], in_=ps[7][:, 0:16]))(),
                  reads=[bank[7]], writes=[b_G])

        qpf_bufs = [b_tA[0], b_tA[1], b_tV[0], b_tV[1],
                    b_tQ[0], b_tQ[1], b_tH[0], b_tH[1]]

        def qpf_v(ch):
            return Uv[:, ch * tt: (ch + 1) * tt]

        def prefetch_corr(scr, scr_buf):
            for ch in range(DC):
                E("pool", (lambda ch=ch, scr=scr: lambda h: h.dma_start(
                    out=qpf_v(ch), in_=scr[0, ch, :, :]))(),
                  reads=[scr_buf], writes=[qpf_bufs[ch], b_uext[ch]],
                  dma=True, sem="dpf")

        def combine_carries():
            Hc = tmpv[:, 0:8]
            E("dve", lambda h: h.memset(hsel[:], 0.0), writes=[b_hsel])
            Gr = Gv.rearrange("p (c s) -> p c s", s=16)
            for k in range(7):
                Ak = Gr[:, :, 2 * k:2 * k + 1]
                bk_ = Gr[:, :, 2 * k + 1:2 * k + 2]
                if k == 0:
                    E("dve", (lambda bk_=bk_: lambda h: h.tensor_copy(
                        out=Hc, in_=bk_))(),
                      reads=[b_G], writes=[b_tmp[0]])
                else:
                    E("dve", (lambda Ak=Ak: lambda h: h.tensor_mul(
                        out=Hc, in0=Hc, in1=Ak))(),
                      reads=[b_G, b_tmp[0]], writes=[b_tmp[0]])
                    E("dve", (lambda bk_=bk_: lambda h: h.tensor_add(
                        out=Hc, in0=Hc, in1=bk_))(),
                      reads=[b_G, b_tmp[0]], writes=[b_tmp[0]])
                E("dve", (lambda k=k: lambda h: h.scalar_tensor_tensor(
                    out=hsel[:], in0=Hc, scalar=mprevv[:, k:k + 1],
                    in1=hsel[:], op0=ALU.mult, op1=ALU.add))(),
                  reads=[b_tmp[0], b_mprev, b_hsel], writes=[b_hsel])

        em.fence()

        E("dve", lambda h: h.memset(hcar[:], 0.0), writes=b_hcar)
        E("dve", lambda h: h.memset(pcar[:], 1.0), writes=b_pcar)
        E("dve", lambda h: h.memset(zerosv, 0.0), writes=[b_zeros])
        sfeed = Feeder(units_s)
        sfeed.take(3)

        cpieces_a = (cast_pieces(pw_p, wimg_pw, b_wimg["pw"], 128)
                     + cast_pieces(gru_p, wimg_gru, b_wimg["gru"], 128))
        cpieces_b = (cast_pieces(ff1_p, wimg_ff1, b_wimg["ff1"], 128)
                     + cast_pieces(ff2_p, wimg_ff2, b_wimg["ff2"], 32))
        cpa_i = [0]
        cpb_i = [0]

        def feed_casts_a(k):
            for fn in cpieces_a[cpa_i[0]:cpa_i[0] + k]:
                fn()
            cpa_i[0] = min(cpa_i[0] + k, len(cpieces_a))

        def feed_casts_b(k):
            for fn in cpieces_b[cpb_i[0]:cpb_i[0] + k]:
                fn()
            cpb_i[0] = min(cpb_i[0] + k, len(cpieces_b))

        def xnb_v(j, ch):
            base = xnbv if j % 2 == 0 else hbfv
            return base[:, ch * tt:(ch + 1) * tt]

        def xnb_b(j):
            return b_xnb if j % 2 == 0 else b_xnbh

        def a_stats(j):
            site_stats(tt, lambda ch, j=j: x_ch(ch, j), [b_x[j]], True,
                       [("invE", "minvE", 1e-30)])

        def a_apply(j):
            apply_ln(tt, lambda ch, j=j: x_ch(ch, j), [b_x[j]], ROW_GAMMA,
                     "invE", "minvE", lambda ch, j=j: xnb_v(j, ch), xnb_b(j))

        a_stats(0)
        a_apply(0)
        for j in range(nt):
            qk_rot = [0]
            for ch in range(DC):
                sl = ch % 4
                wv3, wb3 = wload(wimg_q, b_wq_ch[ch], ch * 384, 3 * D,
                                 nrow=384)
                trip = [(0, "kv"), (1, "a"), (2, "q")]
                for (ti, kind) in trip:
                    bi = qk_rot[0] % 4
                    qk_rot[0] += 1
                    wv = wv3[:, ti * D: (ti + 1) * D]
                    items = [(ps[bi][:, 0:tt], wv[:, kc * 128:(kc + 1) * 128],
                              xnb_v(j, kc),
                              [wb3, xnb_b(j)[kc]]) for kc in range(DC)]
                    matmul_group2(items, bank[bi])
                    if kind == "kv":
                        E("act", (lambda ch=ch, bi=bi: lambda h: h.activation(
                            out=tV_v(ch), in_=ps[bi][:, 0:tt],
                            func=AF.Copy))(),
                          reads=[bank[bi]], writes=[b_tV[sl]])
                    elif kind == "a":
                        E("act", (lambda ch=ch, bi=bi: lambda h: h.activation(
                            out=tA_v(ch), in_=ps[bi][:, 0:tt],
                            func=AF.Sigmoid))(),
                          reads=[bank[bi]], writes=[b_tA[sl]])
                    else:
                        E("act", (lambda ch=ch, bi=bi: lambda h: h.activation(
                            out=tQ_v(ch), in_=ps[bi][:, 0:tt], func=AF.Copy))(),
                          reads=[bank[bi]], writes=[b_tQ[sl]])
                E("dve", (lambda ch=ch: lambda h: h.tensor_tensor_scan(
                    out=tH_v(ch), data0=tA_v(ch), data1=tV_v(ch),
                    initial=hcarv[:, ch:ch + 1], op0=ALU.mult, op1=ALU.add))(),
                  reads=[b_tA[sl], b_tV[sl], b_hcar[ch]], writes=[b_tH[sl]])
                E("dve", (lambda ch=ch: lambda h: h.tensor_copy(
                    out=hcarv[:, ch:ch + 1], in_=tH_v(ch)[:, tt - 1:tt]))(),
                  reads=[b_tH[sl]], writes=[b_hcar[ch]])
                E("dve", (lambda ch=ch: lambda h: h.tensor_tensor_scan(
                    out=tV_v(ch), data0=tA_v(ch), data1=zerosv[:, :],
                    initial=pcarv[:, ch:ch + 1], op0=ALU.mult, op1=ALU.add))(),
                  reads=[b_tA[sl], b_zeros, b_pcar[ch], b_tV[sl]],
                  writes=[b_tV[sl]])
                E("dve", (lambda ch=ch: lambda h: h.tensor_copy(
                    out=pcarv[:, ch:ch + 1], in_=tV_v(ch)[:, tt - 1:tt]))(),
                  reads=[b_tV[sl]], writes=[b_pcar[ch]])
                tq = tq_v(ch)
                E("pool", (lambda ch=ch, tq=tq: lambda h: h.tensor_mul(
                    out=tq, in0=tQ_v(ch), in1=tH_v(ch)))(),
                  reads=[b_tQ[sl], b_tH[sl]], writes=[b_tq[sl]])
                E("dve", (lambda ch=ch, j=j, tq=tq: lambda h: h.tensor_add(
                    out=x_ch(ch, j), in0=x_ch(ch, j), in1=tq))(),
                  reads=[b_tq[sl], b_x[j]], writes=[b_x[j]])
                E("pool", (lambda ch=ch: lambda h: h.tensor_mul(
                    out=tH_v(ch), in0=tQ_v(ch), in1=tV_v(ch)))(),
                  reads=[b_tQ[sl], b_tV[sl]], writes=[b_tH[sl]])
                E("pool", (lambda j=j, ch=ch: lambda h: h.dma_start(
                    out=qp_scr[j, ch, :, :], in_=tH_v(ch)))(),
                  reads=[b_tH[sl]], writes=[b_qp[j]], dma=True, sem="dq")
                if j == 0 and ch < 6:
                    cast_img(wq_p, wimg_q, b_wq_ch[ch + 2], sem=f"dcq{ch + 2}",
                             rows=((ch + 2) * 384, (ch + 3) * 384))
                if ch == 3 and j + 1 < nt:
                    a_stats(j + 1)
                if ch == 6 and j + 1 < nt:
                    a_apply(j + 1)
                sfeed.take(3)
                if j > 0 or nt == 1:
                    feed_casts_a(1)

        sfeed.drain()
        feed_casts_a(len(cpieces_a))
        E("dve", lambda h: h.tensor_scalar_mul(
            out=a1v[:, :], in0=a1v[:, :], scalar1=1.0 / D),
          reads=[b_alpha1], writes=[b_alpha1])
        E("dve", lambda h: h.tensor_scalar_mul(
            out=a2v[:, :], in0=a2v[:, :], scalar1=1.0 / (4 * D)),
          reads=[b_alpha2], writes=[b_alpha2])

        exchange_prep(lambda ch: pcarv[:, ch:ch + 1], list(b_pcar),
                      lambda ch: hcarv[:, ch:ch + 1], list(b_hcar))
        exchange_send("cc1", cc1)
        prefetch_corr(qp_scr, b_qp[0])
        exchange_recv("cc1", cc1)
        combine_carries()

        def corr_tile(scr, j, pref=False):
            for ch in range(DC):
                sl = ch % 2
                if pref and j == 0:
                    E("dve", (lambda j=j, ch=ch: lambda h:
                              h.scalar_tensor_tensor(
                                  out=x_ch(ch, j), in0=qpf_v(ch),
                                  scalar=hselv[:, ch:ch + 1], in1=x_ch(ch, j),
                                  op0=ALU.mult, op1=ALU.add))(),
                      reads=[qpf_bufs[ch], b_uext[ch], b_hsel, b_x[j]],
                      writes=[b_x[j]])
                    continue
                cr = corrv[:, sl * tt: (sl + 1) * tt]
                E("pool", (lambda j=j, ch=ch, cr=cr, scr=scr:
                           lambda h: h.dma_start(
                               out=cr, in_=scr[j, ch, :, :]))(),
                  reads=[b_qp[j] if scr is qp_scr else b_pg[j]],
                  writes=[b_corr[sl]], dma=True, sem=f"dcp{sl}")
                E("dve", (lambda j=j, ch=ch, cr=cr:
                          lambda h: h.scalar_tensor_tensor(
                              out=x_ch(ch, j), in0=cr,
                              scalar=hselv[:, ch:ch + 1], in1=x_ch(ch, j),
                              op0=ALU.mult, op1=ALU.add))(),
                  reads=[b_corr[sl], b_hsel, b_x[j]], writes=[b_x[j]])

        def emit_mlp(kind, w1_img, w1_buf, w2_img, w2_buf, b1_col, epilogue,
                     alpha, bunits=None, gain_row=None, pre_tile=None,
                     cast_fn=None):
            bfeed = Feeder(bunits) if bunits else None

            def head_stats(j):
                if kind == "ln":
                    site_stats(tt, lambda ch, j=j: x_ch(ch, j), [b_x[j]], True,
                               [("invE", "minvE", 1e-5)])
                else:
                    site_stats(tt, lambda ch, j=j: x_ch(ch, j), [b_x[j]],
                               False, [])

            def head_apply(j):
                if kind == "ln":
                    apply_ln(tt, lambda ch, j=j: x_ch(ch, j), [b_x[j]], None,
                             "invE", "minvE",
                             lambda ch: xnbv[:, ch * tt:(ch + 1) * tt], b_xnb)
                else:
                    apply_rms(tt, lambda ch, j=j: x_ch(ch, j), [b_x[j]],
                              gain_row,
                              lambda ch: xnbv[:, ch * tt:(ch + 1) * tt], b_xnb)

            if pre_tile is not None:
                pre_tile(0)
            head_stats(0)
            head_apply(0)
            for j in range(nt):
                for ec in range(32):
                    bi = ec % 2
                    wv, wb = wload(w1_img, w1_buf, ec * 128, D)
                    items = [(ps[bi][:, 0:tt], wv[:, kc * 128:(kc + 1) * 128],
                              xnbv[:, kc * tt:(kc + 1) * tt],
                              [wb, b_xnb[kc]]) for kc in range(DC)]
                    matmul_group2(items, bank[bi])
                    bi_c = b1_col(ec)
                    rd = [bank[bi], b_vcols] + ([b_alpha1] if alpha else [])
                    if alpha:
                        sc = a1v[:, ec:ec + 1]
                        E("act", (lambda ec=ec, bi=bi, sc=sc, bi_c=bi_c:
                                  lambda h: h.activation(
                                      out=hbf_v(ec), in_=ps[bi][:, 0:tt],
                                      func=AF.Gelu, scale=sc, bias=bi_c))(),
                          reads=rd, writes=[b_hbf[ec]])
                    else:
                        E("act", (lambda ec=ec, bi=bi, bi_c=bi_c:
                                  lambda h: h.activation(
                                      out=hbf_v(ec), in_=ps[bi][:, 0:tt],
                                      func=AF.Gelu, bias=bi_c))(),
                          reads=rd, writes=[b_hbf[ec]])
                    if bfeed:
                        bfeed.take(1)
                    if cast_fn:
                        cast_fn()
                    if ec == 14 and j + 1 < nt and pre_tile is not None:
                        pre_tile(j + 1)
                    if ec == 16 and j + 1 < nt:
                        head_stats(j + 1)
                for dc in range(DC):
                    bi = 2 + dc % 2
                    wv1, wb1 = wload(w2_img, w2_buf, dc * 128, 4096)
                    items = [(ps[bi][:, 0:tt], wv1[:, kc * 128:(kc + 1) * 128],
                              hbf_v(kc), [wb1, b_hbf[kc]]) for kc in range(32)]
                    matmul_group2(items, bank[bi])
                    epilogue(j, dc, ps[bi][:, 0:tt], bank[bi])
                    if bfeed:
                        bfeed.take(1)
                    if cast_fn:
                        cast_fn()
                    if dc == 2 and j + 1 < nt:
                        head_apply(j + 1)
            if bfeed:
                bfeed.drain()

        def ffn_epilogue(store):
            def ep(j, dc, psum_ap, bk):
                sl = dc % 2
                t = tmpv[:, sl * tt: (sl + 1) * tt]
                E("dve", (lambda psum_ap=psum_ap, dc=dc, t=t:
                          lambda h: h.tensor_scalar(
                              out=t, in0=psum_ap, scalar1=a2v[:, dc:dc + 1],
                              scalar2=vcol(VR_NB2, dc), op0=ALU.mult,
                              op1=ALU.add))(),
                  reads=[bk, b_alpha2, b_vcols], writes=[b_tmp[sl]])
                E("dve", (lambda j=j, dc=dc, t=t: lambda h: h.tensor_add(
                    out=x_ch(dc, j), in0=x_ch(dc, j), in1=t))(),
                  reads=[b_tmp[sl], b_x[j]], writes=[b_x[j]])
                if store:
                    E("pool", (lambda j=j, dc=dc: lambda h: h.dma_start(
                        out=out_o.rearrange("(c p) t -> p c t", p=128)
                        [:, dc, j * tt:(j + 1) * tt],
                        in_=x_ch(dc, j)))(),
                      reads=[b_x[j]], writes=[b_out], dma=True, sem="dout")
            return ep

        def _bcasts(j):
            corr_tile(qp_scr, j, pref=(j == 0))

        em.fence()
        if upto >= 2:
            emit_mlp("ln", wimg_s1, b_wimg["s1"], wimg_s2, b_wimg["s2"],
                     lambda ec: vcol(VR_NB1, ec), ffn_epilogue(False), True,
                     bunits=units_b, pre_tile=_bcasts,
                     cast_fn=lambda: feed_casts_b(1))
        elif upto == 1:
            pass
        else:
            for j in range(nt):
                corr_tile(qp_scr, j)
        feed_casts_b(len(cpieces_b))

        class _EndPhase3(Exception):
            pass
        try:
         if upto >= 3:
            build_gain(VR_GRUG, ROW_GRU32, 1.0, 32.0)
            site_stats(2, lambda ch: x_cols(ch, tcore - 2, 2), [b_x[nt - 1]],
                       True, [("inv0", "minv0", 1e-30)])
            for ch in range(DC):
                g = gain_slice(ROW_CONVG, ch)
                E("pe", (lambda g=g: lambda h: h.matmul(
                    ps[6][:, 0:2], g, rs("inv0", 2), start=True, stop=True))(),
                  reads=[b_Rg, RSB["inv0"]], writes=[bank[6]])
                E("pe", (lambda g=g: lambda h: h.matmul(
                    ps[7][:, 0:2], g, rs("minv0", 2), start=True, stop=True))(),
                  reads=[b_Rg, RSB["minv0"]], writes=[bank[7]])
                t = tmpv[:, 0:2]
                E("dve", (lambda ch=ch, t=t: lambda h: h.tensor_mul(
                    out=t, in0=x_cols(ch, tcore - 2, 2), in1=ps[6][:, 0:2]))(),
                  reads=[b_x[nt - 1], bank[6]], writes=[b_tmp[0]])
                E("dve", (lambda ch=ch, t=t: lambda h: h.tensor_sub(
                    out=uh2v[:, ch * 2: ch * 2 + 2], in0=t,
                    in1=ps[7][:, 0:2]))(),
                  reads=[b_tmp[0], bank[7]], writes=[b_uh2])
            em.fence()
            exchange_prep(lambda ch: uh2v[:, ch * 2: ch * 2 + 1], [b_uh2],
                          lambda ch: uh2v[:, ch * 2 + 1: ch * 2 + 2], [b_uh2])
            exchange_send("cc2", cc2)
            _s3 = SUB3

            uer = Uv[:, 0:DC * (tt + 4)].rearrange("p (c t) -> p c t", c=DC)
            Gr = Gv.rearrange("p (c s) -> p c s", s=16)

            def a3_stats(j):
                for ch in range(DC):
                    E("sp", (lambda j=j, ch=ch: lambda h: h.dma_start(
                        out=xold_scr[j, ch, :, :], in_=x_ch(ch, j)))(),
                      reads=[b_x[j]], writes=[b_xold[j]], dma=True, sem="dqh")
                site_stats(tt, lambda ch, j=j: x_ch(ch, j), [b_x[j]], True,
                           [("invE", "minvE", 1e-5), ("inv0", "minv0", 1e-30)])

            def a3_uln(j):
                for ch in range(DC):
                    g2 = gain_slice(ROW_CONVG, ch)
                    E("pe", (lambda g2=g2: lambda h: h.matmul(
                        ps[6][:, 0:tt], g2, rs("inv0", tt),
                        start=True, stop=True))(),
                      reads=[b_Rg, RSB["inv0"]], writes=[bank[6]])
                    E("pe", (lambda g2=g2: lambda h: h.matmul(
                        ps[7][:, 0:tt], g2, rs("minv0", tt),
                        start=True, stop=True))(),
                      reads=[b_Rg, RSB["minv0"]], writes=[bank[7]])
                    t0 = tmpv[:, 0:tt]
                    E("dve", (lambda ch=ch, j=j, t0=t0: lambda h: h.tensor_mul(
                        out=t0, in0=x_ch(ch, j), in1=ps[6][:, 0:tt]))(),
                      reads=[b_x[j], bank[6]], writes=[b_tmp[0]])
                    E("dve", (lambda ch=ch, t0=t0: lambda h: h.tensor_sub(
                        out=u_ch(ch, 2, tt), in0=t0, in1=ps[7][:, 0:tt]))(),
                      reads=[b_tmp[0], bank[7]], writes=[b_uext[ch]])
                    o = ones_row()
                    E("pe", (lambda o=o: lambda h: h.matmul(
                        ps[4][:, 0:tt], o, rs("invE", tt),
                        start=True, stop=True))(),
                      reads=[b_Rg, RSB["invE"]], writes=[bank[4]])
                    E("pe", (lambda o=o: lambda h: h.matmul(
                        ps[5][:, 0:tt], o, rs("minvE", tt),
                        start=True, stop=True))(),
                      reads=[b_Rg, RSB["minvE"]], writes=[bank[5]])
                    t1 = tmpv[:, tt:2 * tt]
                    E("dve", (lambda ch=ch, j=j, t1=t1: lambda h: h.tensor_mul(
                        out=t1, in0=x_ch(ch, j), in1=ps[4][:, 0:tt]))(),
                      reads=[b_x[j], bank[4], b_xold[j]], writes=[b_tmp[1]])
                    E("dve", (lambda ch=ch, j=j, t1=t1: lambda h: h.tensor_sub(
                        out=x_ch(ch, j), in0=t1, in1=ps[5][:, 0:tt]))(),
                      reads=[b_tmp[1], bank[5]], writes=[b_x[j]])

            def a3_taps_pw(j):
                for ch in range(DC):
                    E("dve", (lambda ch=ch: lambda h: h.tensor_scalar(
                        out=ycvv, in0=u_ch(ch, 2, tt),
                        scalar1=vcol(VR_T2, ch), scalar2=vcol(VR_BDW, ch),
                        op0=ALU.mult, op1=ALU.add))(),
                      reads=[b_uext[ch], b_vcols], writes=[b_ycv])
                    E("dve", (lambda ch=ch: lambda h: h.scalar_tensor_tensor(
                        out=ycvv, in0=u_ch(ch, 1, tt),
                        scalar=vcol(VR_T1, ch), in1=ycvv,
                        op0=ALU.mult, op1=ALU.add))(),
                      reads=[b_uext[ch], b_vcols, b_ycv], writes=[b_ycv])
                    E("dve", (lambda ch=ch: lambda h: h.scalar_tensor_tensor(
                        out=ybf_v(ch), in0=u_ch(ch, 0, tt),
                        scalar=vcol(VR_T0, ch), in1=ycvv,
                        op0=ALU.mult, op1=ALU.add))(),
                      reads=[b_uext[ch], b_vcols, b_ycv], writes=[b_ybf[ch]])
                    E("dve", (lambda ch=ch: lambda h: h.tensor_copy(
                        out=u_ch(ch, 0, 2), in_=u_ch(ch, tt, 2)))(),
                      reads=[b_uext[ch]], writes=[b_uext[ch]])
                for ec in range(DC):
                    bi = 2 + ec % 2
                    wv, wb = wload(wimg_pw, b_wimg["pw"], ec * 128, D)
                    items = [(ps[bi][:, 0:tt], wv[:, kc * 128:(kc + 1) * 128],
                              ybf_v(kc), [wb, b_ybf[kc]]) for kc in range(DC)]
                    matmul_group2(items, bank[bi])
                    sl = ec % 2
                    t = tmpv[:, (2 + sl) * tt: (3 + sl) * tt]
                    E("dve", (lambda bi=bi, ec=ec, t=t:
                              lambda h: h.tensor_scalar_add(
                                  out=t, in0=ps[bi][:, 0:tt],
                                  scalar1=vcol(VR_BPW, ec)))(),
                      reads=[bank[bi], b_vcols], writes=[b_tmp[2 + sl]])
                    E("dve", (lambda j=j, ec=ec, t=t: lambda h: h.tensor_add(
                        out=x_ch(ec, j), in0=x_ch(ec, j), in1=t))(),
                      reads=[b_tmp[2 + sl], b_x[j]], writes=[b_x[j]])
                    if ec == 2 and j + 1 < nt and _s3 >= 3:
                        a3_stats(j + 1)

            if _s3 >= 3:
                a3_stats(0)
                a3_uln(0)
            exchange_recv("cc2", cc2)
            for par in range(2):
                dst = uer[:, :, par:par + 1]
                E("dve", (lambda dst=dst: lambda h: h.memset(dst, 0.0))(),
                  writes=list(b_uext))
            for k in range(7):
                for par in range(2):
                    dst = uer[:, :, par:par + 1]
                    src_ = Gr[:, :, 2 * k + par:2 * k + par + 1]
                    E("dve", (lambda dst=dst, src_=src_, k=k:
                              lambda h: h.scalar_tensor_tensor(
                                  out=dst, in0=src_, scalar=mprevv[:, k:k + 1],
                                  in1=dst, op0=ALU.mult, op1=ALU.add))(),
                      reads=[b_G, b_mprev] + list(b_uext),
                      writes=list(b_uext))
            if _s3 >= 3:
                a3_taps_pw(0)
                for j in range(1, nt):
                    a3_uln(j)
                    a3_taps_pw(j)

            em.fence()

            E("dve", lambda h: h.memset(hcar[:], 0.0), writes=b_hcar)
            E("dve", lambda h: h.memset(pcar[:], 1.0), writes=b_pcar)
            E("dve", lambda h: h.memset(zerosv, 0.0), writes=[b_zeros])
            for j in range(nt if _s3 >= 4 else 0):
                site_stats(tt, lambda ch, j=j: x_ch(ch, j), [b_x[j]],
                           False, [])
                apply_rms(tt, lambda ch, j=j: x_ch(ch, j), [b_x[j]],
                          ROW_GRU32,
                          lambda ch: xnbv[:, ch * tt:(ch + 1) * tt], b_xnb)
                for ch in range(DC):
                    sl = ch % 4
                    sl2 = ch % 2
                    bh = 2 + sl2
                    bg = sl2
                    wvg, wbg = wload(wimg_gru, b_wimg["gru"], ch * 256, 2 * D,
                                     nrow=256)
                    items = [(ps[bh][:, 0:tt], wvg[:, kc * 128:(kc + 1) * 128],
                              xnbv[:, kc * tt:(kc + 1) * tt],
                              [wbg, b_xnb[kc]]) for kc in range(DC)]
                    matmul_group2(items, bank[bh])
                    items = [(ps[bg][:, 0:tt],
                              wvg[:, D + kc * 128: D + (kc + 1) * 128],
                              xnbv[:, kc * tt:(kc + 1) * tt],
                              [wbg, b_xnb[kc]]) for kc in range(DC)]
                    matmul_group2(items, bank[bg])
                    s_t = tmpv[:, sl2 * tt: (sl2 + 1) * tt]
                    sv_t = tmpv[:, (2 + sl2) * tt: (3 + sl2) * tt]
                    E("act", (lambda bh=bh, s_t=s_t: lambda h: h.activation(
                        out=s_t, in_=ps[bh][:, 0:tt], func=AF.Sigmoid))(),
                      reads=[bank[bh]], writes=[b_tmp[sl2]])
                    E("dve", (lambda bh=bh, ch=ch: lambda h: h.tensor_scalar_add(
                        out=tV_v(ch), in0=ps[bh][:, 0:tt], scalar1=0.5))(),
                      reads=[bank[bh], b_tmp[sl2]], writes=[b_tV[sl]])
                    E("act", (lambda bg=bg, ch=ch: lambda h: h.activation(
                        out=tA_v(ch), in_=ps[bg][:, 0:tt], func=AF.Sigmoid,
                        scale=-1.0))(),
                      reads=[bank[bg]], writes=[b_tA[sl]])
                    E("act", (lambda bg=bg, sv_t=sv_t: lambda h: h.activation(
                        out=sv_t, in_=ps[bg][:, 0:tt], func=AF.Sigmoid))(),
                      reads=[bank[bg]], writes=[b_tmp[2 + sl2]])
                    E("dve", (lambda ch=ch, s_t=s_t: lambda h: h.tensor_max(
                        out=tV_v(ch), in0=tV_v(ch), in1=s_t))(),
                      reads=[b_tmp[sl2], b_tV[sl]], writes=[b_tV[sl]])
                    E("pool", (lambda ch=ch, sv_t=sv_t: lambda h: h.tensor_mul(
                        out=tV_v(ch), in0=tV_v(ch), in1=sv_t))(),
                      reads=[b_tmp[2 + sl2], b_tV[sl]], writes=[b_tV[sl]])
                    E("dve", (lambda ch=ch: lambda h: h.tensor_tensor_scan(
                        out=tH_v(ch), data0=tA_v(ch), data1=tV_v(ch),
                        initial=hcarv[:, ch:ch + 1], op0=ALU.mult,
                        op1=ALU.add))(),
                      reads=[b_tA[sl], b_tV[sl], b_hcar[ch]], writes=[b_tH[sl]])
                    E("dve", (lambda ch=ch: lambda h: h.tensor_copy(
                        out=hcarv[:, ch:ch + 1], in_=tH_v(ch)[:, tt - 1:tt]))(),
                      reads=[b_tH[sl]], writes=[b_hcar[ch]])
                    E("dve", (lambda ch=ch: lambda h: h.tensor_tensor_scan(
                        out=tV_v(ch), data0=tA_v(ch), data1=zerosv[:, :],
                        initial=pcarv[:, ch:ch + 1], op0=ALU.mult,
                        op1=ALU.add))(),
                      reads=[b_tA[sl], b_zeros, b_pcar[ch], b_tV[sl]],
                      writes=[b_tV[sl]])
                    E("dve", (lambda ch=ch: lambda h: h.tensor_copy(
                        out=pcarv[:, ch:ch + 1], in_=tV_v(ch)[:, tt - 1:tt]))(),
                      reads=[b_tV[sl]], writes=[b_pcar[ch]])
                    E("dve", (lambda ch=ch, j=j: lambda h: h.tensor_add(
                        out=x_ch(ch, j), in0=x_ch(ch, j), in1=tH_v(ch)))(),
                      reads=[b_tH[sl], b_x[j]], writes=[b_x[j]])
                    E("pool", (lambda j=j, ch=ch: lambda h: h.dma_start(
                        out=pg_scr[j, ch, :, :], in_=tV_v(ch)))(),
                      reads=[b_tV[sl]], writes=[b_pg[j]], dma=True, sem="dq")
            if _s3 < 5:
                raise _EndPhase3()
            exchange_prep(lambda ch: pcarv[:, ch:ch + 1], list(b_pcar),
                          lambda ch: hcarv[:, ch:ch + 1], list(b_hcar))
            exchange_send("cc3", cc3)
            prefetch_corr(pg_scr, b_pg[0])
            build_gain(VR_FFG, ROW_FF32, 1.0, 32.0)
            exchange_recv("cc3", cc3)
            combine_carries()

            em.fence()

            def ff_epilogue(j, dc, psum_ap, bk):
                sl = dc % 2
                t = tmpv[:, sl * tt: (sl + 1) * tt]
                E("dve", (lambda psum_ap=psum_ap, dc=dc, t=t:
                          lambda h: h.tensor_scalar_add(
                              out=t, in0=psum_ap, scalar1=vcol(VR_FB2, dc)))(),
                  reads=[bk, b_vcols], writes=[b_tmp[sl]])
                E("dve", (lambda j=j, dc=dc, t=t: lambda h: h.tensor_add(
                    out=x_ch(dc, j), in0=x_ch(dc, j), in1=t))(),
                  reads=[b_tmp[sl], b_x[j]], writes=[b_x[j]])
                xo = corrv[:, sl * tt: (sl + 1) * tt]
                E("pool", (lambda j=j, dc=dc, xo=xo: lambda h: h.dma_start(
                    out=xo, in_=xold_scr[j, dc, :, :]))(),
                  reads=[b_xold[j]], writes=[b_corr[sl]], dma=True,
                  sem=f"dcp{sl}")
                E("dve", (lambda j=j, dc=dc, xo=xo: lambda h: h.tensor_add(
                    out=x_ch(dc, j), in0=x_ch(dc, j), in1=xo))(),
                  reads=[b_corr[sl], b_x[j]], writes=[b_x[j]])

            emit_mlp("rms", wimg_ff1, b_wimg["ff1"], wimg_ff2, b_wimg["ff2"],
                     lambda ec: vcol(VR_FB1, ec), ff_epilogue, False,
                     gain_row=ROW_FF32,
                     pre_tile=lambda j: corr_tile(pg_scr, j,
                                                   pref=(j == 0)))

        except _EndPhase3:
            pass
        if upto >= 4:
            emit_mlp("ln", wimg_s1, b_wimg["s1"], wimg_s2, b_wimg["s2"],
                     lambda ec: vcol(VR_NB1, ec), ffn_epilogue(True), True)
        else:
            for j in range(nt):
                for dc in range(DC):
                    E("pool", (lambda j=j, dc=dc: lambda h: h.dma_start(
                        out=out_o.rearrange("(c p) t -> p c t", p=128)
                        [:, dc, j * tt:(j + 1) * tt],
                        in_=x_ch(dc, j)))(),
                      reads=[b_x[j]], writes=[b_out], dma=True, sem="dout")

        for s_ in ("dout", "dq", "dqh", "di0", "di1", "dcb", "dcv"):
            em.raw_wait("sp", s_, em.cnt[s_])

        with nc.Block() as block:
            def mk(eng):
                def run(h):
                    for it in em.items[eng]:
                        if it[0] == "w":
                            h.wait_ge(sems[it[1]], it[2])
                        else:
                            inst = it[1](h)
                            if it[2] is not None:
                                inst.then_inc(sems[it[2]], it[3])
                return run
            block.tensor(mk("pe"))
            block.scalar(mk("act"))
            block.vector(mk("dve"))
            block.gpsimd(mk("pool"))
            block.sync(mk("sp"))

    return nc


_NC_CACHE = {}


def _get_nc(key):
    if key not in _NC_CACHE:
        _NC_CACHE[key] = build_nc(*key)
    return _NC_CACHE[key]


def make_inputs(full_inputs, tcore):
    ii = full_inputs
    Tloc = tcore * NCORES
    x = np.asarray(ii["L0_attn_a_x"], dtype=np.float32).reshape(-1, D)[:Tloc]

    vrows = np.zeros((16, 4096), np.float32)
    vrows[VR_BDW, :D] = np.asarray(ii["L0_attn_b_conv_dw_b"], np.float32)
    vrows[VR_BPW, :D] = np.asarray(ii["L0_attn_b_conv_pw_b"], np.float32)
    vrows[VR_FB2, :D] = np.asarray(ii["L0_attn_b_ff_b2"], np.float32)
    vrows[VR_NB2, :D] = np.asarray(ii["L0_ffn_b2"], np.float32)
    vrows[VR_FB1, :] = np.asarray(ii["L0_attn_b_ff_b1"], np.float32)
    vrows[VR_NB1, :] = np.asarray(ii["L0_ffn_b1"], np.float32)
    cdw = np.asarray(ii["L0_attn_b_conv_dw_w"], np.float32)[:, 0, :]
    vrows[VR_T0, :D] = cdw[:, 0]
    vrows[VR_T1, :D] = cdw[:, 1]
    vrows[VR_T2, :D] = cdw[:, 2]
    vrows[VR_GAMMA, :D] = np.asarray(ii["L0_attn_a_gamma"], np.float32)
    vrows[VR_CONVG, :D] = np.asarray(ii["L0_attn_b_conv_norm_g"], np.float32)
    vrows[VR_GRUG, :D] = np.asarray(ii["L0_attn_b_gru_norm_g"], np.float32)
    vrows[VR_FFG, :D] = np.asarray(ii["L0_attn_b_ff_norm_g"], np.float32)

    wq_packed = pack_lhsT(np.asarray(ii["L0_attn_a_w_qkva"], np.float32))
    wq4 = wq_packed.reshape(24, 128, 1024)
    order = []
    for ch in range(8):
        order += [8 + ch, 16 + ch, ch]
    wq_tripped = np.ascontiguousarray(wq4[order].reshape(24 * 128, 1024))
    gru_packed = pack_lhsT(np.asarray(ii["L0_attn_b_gru_w"], np.float32))
    g4 = gru_packed.reshape(16, 128, 1024)
    gorder = []
    for ch in range(8):
        gorder += [ch, 8 + ch]
    gru_paired = np.ascontiguousarray(g4[gorder].reshape(16 * 128, 1024))
    packs = dict(
        wq_p=wq_tripped,
        pw_p=pack_lhsT(np.asarray(ii["L0_attn_b_conv_pw_w"], np.float32)),
        gru_p=gru_paired,
        ff1_p=pack_lhsT(np.asarray(ii["L0_attn_b_ff_w1"], np.float32)),
        ff2_p=pack_lhsT(np.asarray(ii["L0_attn_b_ff_w2"], np.float32)),
        n1_p=pack_lhsT(np.asarray(ii["L0_ffn_w1"], np.float32)),
        n2_p=pack_lhsT(np.asarray(ii["L0_ffn_w2"], np.float32)),
        n1_nat=np.ascontiguousarray(np.asarray(ii["L0_ffn_w1"], np.float32)),
        n2_nat=np.ascontiguousarray(np.asarray(ii["L0_ffn_w2"], np.float32)),
        vrows_in=vrows,
        ident_in=np.eye(128, dtype=np.float32),
    )

    in_maps = []
    for c in range(NCORES):
        mprev = np.zeros((128, 8), np.float32)
        if c > 0:
            mprev[:, c - 1] = 1.0
        smask = np.zeros((16, 1), np.float32)
        smask[2 * c: 2 * c + 2] = 1.0
        smr = np.zeros((1, 32), np.float32)
        smr[0, 2 * c] = 1.0
        smr[0, 16 + 2 * c + 1] = 1.0
        xs = np.ascontiguousarray(x[c * tcore:(c + 1) * tcore, :].T)
        m = dict(packs)
        m.update(x_in=xs, mprev_in=mprev, smask_in=smask, smr_in=smr)
        in_maps.append(m)
    return in_maps


def kernel(**inputs):
    tcore = T // NCORES
    in_maps = make_inputs(inputs, tcore)
    nc = _get_nc((tcore, 512, 4, 0))
    res = run_bass_kernel_spmd(nc, in_maps, core_ids=list(range(NCORES)))
    out = np.empty((T, D), np.float32)
    for c in range(NCORES):
        out[c * tcore:(c + 1) * tcore, :] = res.results[c]["out_o"].T
    return out.reshape(1, T, D)


# version 21
# speedup vs baseline: 1.0992x; 1.0077x over previous
import numpy as np
import concourse.bass as bass
import concourse.mybir as mybir
from concourse.bass_utils import run_bass_kernel_spmd

F32 = mybir.dt.float32
BF16 = mybir.dt.bfloat16
AF = mybir.ActivationFunctionType
ALU = mybir.AluOpType
AX = mybir.AxisListType

D = 1024
DC = 8
NCORES = 8
T = 16384

ENGINES = ("pe", "act", "dve", "pool", "sp")
SEMS = ("pe", "act", "dve", "pool", "cc",
        "dx", "dq", "dqh", "dcb", "dout", "dcv", "dcq", "dpf",
        "dcq0", "dcq1", "dcq2", "dcq3", "dcq4", "dcq5", "dcq6", "dcq7",
        "dw0", "dw1",
        "ds0", "ds1", "ds2", "ds3", "ds4", "ds5",
        "di0", "di1", "di2", "di3", "dcr0", "dcr1",
        "dcp0", "dcp1")


class Buf:
    __slots__ = ("w", "r", "name")

    def __init__(self, name=""):
        self.w = {}
        self.r = {}
        self.name = name


class Em:
    def __init__(self):
        self.items = {e: [] for e in ENGINES}
        self.cnt = {s: 0 for s in SEMS}
        self.waited = {e: {} for e in ENGINES}

    def _wait(self, eng, sem, val):
        if val <= 0:
            return
        assert val <= self.cnt[sem], (eng, sem, val, self.cnt[sem])
        if self.waited[eng].get(sem, 0) >= val:
            return
        self.waited[eng][sem] = val
        self.items[eng].append(("w", sem, val))

    TOTAL_SEMS = ("dx", "dq", "dqh", "dcb", "dout", "dcv", "dcq", "dpf")

    def _wait_edge(self, eng, sem, val):
        if sem in self.TOTAL_SEMS:
            val = self.cnt[sem]
        self._wait(eng, sem, val)

    def op(self, eng, fn, reads=(), writes=(), dma=False, inc=True, sem=None,
           incval=None):
        assert sem is not None or not dma, "dma ops must name a channel sem"
        if sem is None:
            sem = eng
        if incval is None:
            incval = 16 if dma else 1
        for b in reads:
            for s, c in b.w.items():
                self._wait_edge(eng, s, c)
        for b in writes:
            for s, c in b.w.items():
                self._wait_edge(eng, s, c)
            for s, c in b.r.items():
                self._wait_edge(eng, s, c)
        if inc:
            self.cnt[sem] += incval
            self.items[eng].append(("c", fn, sem, incval))
            cur = (sem, self.cnt[sem])
            for b in writes:
                if b.name.startswith("w_img"):
                    b.w[sem] = self.cnt[sem]
                else:
                    b.w = {sem: self.cnt[sem]}
                b.r = {}
            for b in reads:
                b.r[sem] = cur[1]
            return cur
        self.items[eng].append(("c", fn, None, 0))
        return None

    def fence(self):
        for eng in ENGINES:
            for sem in SEMS:
                self._wait(eng, sem, self.cnt[sem])

    def raw_wait(self, eng, sem, val):
        self._wait(eng, sem, val)


def pack_lhsT(w):
    E, K = w.shape
    w4 = w.reshape(E // 128, 128, K // 128, 128).transpose(0, 3, 2, 1)
    return np.ascontiguousarray(w4.reshape(E, K))


VR_BDW, VR_BPW, VR_FB2, VR_NB2, VR_FB1, VR_NB1 = 0, 1, 2, 3, 4, 5
VR_T0, VR_T1, VR_T2 = 6, 7, 8
VR_GAMMA, VR_CONVG, VR_GRUG, VR_FFG = 9, 10, 11, 12


SUB3 = 9


def build_nc(tcore=2048, tt=512, upto=4, debug=0):
    nt = tcore // tt
    nc = bass.Bass("TRN2", target_bir_lowering=False, num_devices=8)

    x_in = nc.dram_tensor("x_in", [D, tcore], F32, kind="ExternalInput").ap()
    out_o = nc.dram_tensor("out_o", [D, tcore], F32, kind="ExternalOutput").ap()
    wq_p = nc.dram_tensor("wq_p", [3 * D, D], F32, kind="ExternalInput").ap()
    pw_p = nc.dram_tensor("pw_p", [D, D], F32, kind="ExternalInput").ap()
    gru_p = nc.dram_tensor("gru_p", [2 * D, D], F32, kind="ExternalInput").ap()
    ff1_p = nc.dram_tensor("ff1_p", [4 * D, D], F32, kind="ExternalInput").ap()
    ff2_p = nc.dram_tensor("ff2_p", [D, 4 * D], F32, kind="ExternalInput").ap()
    n1_p = nc.dram_tensor("n1_p", [4 * D, D], F32, kind="ExternalInput").ap()
    n2_p = nc.dram_tensor("n2_p", [D, 4 * D], F32, kind="ExternalInput").ap()
    n1_nat = nc.dram_tensor("n1_nat", [4 * D, D], F32, kind="ExternalInput").ap()
    n2_nat = nc.dram_tensor("n2_nat", [D, 4 * D], F32, kind="ExternalInput").ap()
    vrows_in = nc.dram_tensor("vrows_in", [16, 4096], F32, kind="ExternalInput").ap()
    ident_in = nc.dram_tensor("ident_in", [128, 128], F32, kind="ExternalInput").ap()
    mprev_in = nc.dram_tensor("mprev_in", [128, 8], F32, kind="ExternalInput").ap()
    smask_in = nc.dram_tensor("smask_in", [16, 1], F32, kind="ExternalInput").ap()
    smr_in = nc.dram_tensor("smr_in", [1, 32], F32, kind="ExternalInput").ap()

    wimg_q = nc.dram_tensor("wimg_q", [3 * D, D], BF16).ap()
    wimg_pw = nc.dram_tensor("wimg_pw", [D, D], BF16).ap()
    wimg_gru = nc.dram_tensor("wimg_gru", [2 * D, D], BF16).ap()
    wimg_ff1 = nc.dram_tensor("wimg_ff1", [4 * D, D], BF16).ap()
    wimg_ff2 = nc.dram_tensor("wimg_ff2", [D, 4 * D], BF16).ap()
    wimg_s1 = nc.dram_tensor("wimg_s1", [4 * D, D], BF16).ap()
    wimg_s2 = nc.dram_tensor("wimg_s2", [D, 4 * D], BF16).ap()
    qp_scr = nc.dram_tensor("qp_scr", [nt, DC, 128, tt], F32).ap()
    pg_scr = nc.dram_tensor("pg_scr", [nt, DC, 128, tt], F32).ap()
    xold_scr = nc.dram_tensor("xold_scr", [nt, DC, 128, tt], F32).ap()
    cc1 = nc.dram_tensor("cc1", [16, D], F32).ap()
    cc2 = nc.dram_tensor("cc2", [16, D], F32).ap()
    cc3 = nc.dram_tensor("cc3", [16, D], F32).ap()
    scr2a = nc.dram_tensor("scr2a", [1, 2 * D], F32).ap()
    scr2b = nc.dram_tensor("scr2b", [1, 2 * D], F32).ap()
    scr2c = nc.dram_tensor("scr2c", [1, 2 * D], F32).ap()

    from contextlib import ExitStack
    es = ExitStack()
    with es:
        def sb(name, shape, dtype=F32):
            return es.enter_context(nc.sbuf_tensor(name, shape, dtype))

        x_sb = sb("x_sb", [128, DC * tcore])
        U = sb("U", [128, 6688])
        stg = sb("stg", [128, 2 * 1024])
        cv16 = sb("cv16", [128, 4 * 1024], BF16)
        wstr = sb("wstr", [128, 2 * 4096], BF16)
        xnb = sb("xnb", [128, DC * tt], BF16)
        hb16 = sb("hb16", [128, 16384], BF16)
        sqb = sb("sqb", [128, 2 * tt])
        tmp5 = sb("tmp5", [128, 4 * tt])
        corr = sb("corr", [128, 2 * tt])
        vcols = sb("vcols", [128, 512])
        Rg = sb("Rg", [128, 2176])
        Rs = sb("Rs", [128, 3072])
        ident = sb("ident", [128, 128])
        ones_c = sb("ones_c", [128, 1])
        ones_h = sb("ones_h", [128, 1], BF16)
        hcar = sb("hcar", [128, 8])
        pcar = sb("pcar", [128, 8])
        hsel = sb("hsel", [128, 8])
        mprev = sb("mprev", [128, 8])
        smask = sb("smask", [16, 1])
        smr = sb("smr", [1, 32])
        G = sb("G", [128, 128])
        ccst = sb("ccst", [16, D])
        alpha1 = sb("alpha1", [128, 32])
        alpha2 = sb("alpha2", [128, 8])
        uh2 = sb("uh2", [128, 16])
        epsc = sb("epsc", [128, 4])

        ps = [es.enter_context(nc.psum_tensor(f"ps{i}", [128, 512], F32))
              for i in range(8)]
        sems = {s: es.enter_context(nc.semaphore(s)) for s in SEMS}

        em = Em()

        bank = [Buf(f"bank{i}") for i in range(8)]
        b_x = [Buf(f"x{j}") for j in range(nt)]
        b_tA = [Buf(f"tA{i}") for i in range(4)]
        b_tV = [Buf(f"tV{i}") for i in range(4)]
        b_tQ = [Buf(f"tQ{i}") for i in range(4)]
        b_tH = [Buf(f"tH{i}") for i in range(4)]
        b_tq = [Buf(f"tq{i}") for i in range(4)]
        b_fs = [Buf(f"fs{i}") for i in range(6)]
        b_cv16 = [Buf(f"cv16{i}") for i in range(4)]
        b_wstr = [Buf("wstr0"), Buf("wstr1")]
        b_xnb = [Buf(f"xnb{c}") for c in range(DC)]
        b_xnbh = [Buf(f"xnbh{c}") for c in range(DC)]
        b_hbf = [Buf(f"hbf{c}") for c in range(32)]
        b_sqb = [Buf("sqb0"), Buf("sqb1")]
        b_tmp = [Buf(f"tmp{i}") for i in range(4)]
        b_corr = [Buf("corr0"), Buf("corr1")]
        b_uext = [Buf(f"uext{c}") for c in range(DC)]
        b_ycv = Buf("ycv")
        b_ybf = [Buf(f"ybf{c}") for c in range(DC)]
        b_vrows = Buf("vrows")
        b_vcols = Buf("vcols")
        b_Rg = Buf("Rg")
        b_ident = Buf("ident")
        b_ones = Buf("ones")
        b_zeros = Buf("zeros")
        b_hcar = [Buf(f"hcar{c}") for c in range(DC)]
        b_pcar = [Buf(f"pcar{c}") for c in range(DC)]
        b_hsel = Buf("hsel")
        b_mprev = Buf("mprev")
        b_smask = Buf("smask")
        b_smr = Buf("smr")
        b_G = Buf("G")
        b_ccst = Buf("ccst")
        b_rows2 = Buf("rows2")
        b_alpha1 = Buf("alpha1")
        b_alpha2 = Buf("alpha2")
        b_uh2 = Buf("uh2")
        b_epsc = Buf("epsc")
        b_wimg = {k: Buf("w_img_" + k) for k in
                  ("q", "pw", "gru", "ff1", "ff2", "s1", "s2")}
        b_wq_ch = [Buf(f"wq_ch{c}") for c in range(DC)]
        b_qp = [Buf(f"qp{j}") for j in range(nt)]
        b_pg = [Buf(f"pg{j}") for j in range(nt)]
        b_xold = [Buf(f"xold{j}") for j in range(nt)]
        b_cc = {"cc1": Buf("cc1"), "cc2": Buf("cc2"), "cc3": Buf("cc3")}
        b_scr2 = {"cc1": Buf("s2a"), "cc2": Buf("s2b"), "cc3": Buf("s2c")}
        b_out = Buf("out")
        RSB = {}

        xv = x_sb.ap()

        def x_ch(ch, j, n=None):
            n = tt if n is None else n
            return xv[:, ch * tcore + j * tt: ch * tcore + j * tt + n]

        def x_cols(ch, c0, n):
            return xv[:, ch * tcore + c0: ch * tcore + c0 + n]

        Uv = U.ap()
        vrowsv = Uv[0:16, 0:4096]
        UE = DC * (tt + 4)
        R2OFF = max(8 * tt, UE + tt)

        def tsl(base, hoff, ch, n=tt):
            s = ch % 4
            if s < 2:
                return Uv[:, base + s * tt: base + s * tt + n]
            return hbfc[:, hoff + (s - 2) * tt: hoff + (s - 2) * tt + n]

        def tA_v(ch, n=tt):
            return tsl(0, 4 * tt, ch, n)

        def tV_v(ch, n=tt):
            return tsl(2 * tt, 6 * tt, ch, n)

        def tQ_v(ch, n=tt):
            return tsl(4 * tt, 8 * tt, ch, n)

        def tH_v(ch, n=tt):
            return tsl(6 * tt, 10 * tt, ch, n)

        def tq_v(ch, n=tt):
            return hbfc[:, 12 * tt + (ch % 4) * tt: 12 * tt + (ch % 4) * tt + n]

        def u_ch(ch, c0, n):
            return Uv[:, ch * (tt + 4) + c0: ch * (tt + 4) + c0 + n]

        ycvv = Uv[:, UE:UE + tt]
        rows2v = Uv[0:1, R2OFF:R2OFF + 2 * D]
        zerosv = Uv[:, R2OFF:R2OFF + tt]
        stg2v = Uv[:, R2OFF + 2 * D - 1024: R2OFF + 2 * D]

        stgv = stg.ap()
        cv16v = cv16.ap()
        stgb = stg.ap().bitcast(BF16)
        Ub = U.ap().bitcast(BF16)
        FS_OFF = [0, 1024, 2048, 3072, 2 * 5152, 2 * 5152 + 1024]

        def fs_v(slot, n):
            base = stgb if slot < 4 else Ub
            return base[:, FS_OFF[slot]: FS_OFF[slot] + n]
        hbfc = hb16.ap().bitcast(F32)
        wstrv = wstr.ap()
        xnbv = xnb.ap()
        hbfv = hb16.ap()

        def hbf_v(ec):
            return hbfv[:, ec * tt: (ec + 1) * tt]

        def ybf_v(ch):
            return hbfv[:, ch * tt: (ch + 1) * tt]

        sqv = sqb.ap()
        tmpv = tmp5.ap()
        corrv = corr.ap()
        vcolsv = vcols.ap()
        Rgv = Rg.ap()
        Rsv = Rs.ap()
        identv = ident.ap()
        onesv = ones_c.ap()
        oneshv = ones_h.ap()
        sqhv = sqb.ap().bitcast(BF16)
        hcarv, pcarv, hselv = hcar.ap(), pcar.ap(), hsel.ap()
        mprevv, smaskv = mprev.ap(), smask.ap()
        smrv = smr.ap()
        Gv, ccstv = G.ap(), ccst.ap()
        a1v, a2v = alpha1.ap(), alpha2.ap()
        uh2v = uh2.ap()
        epscv = epsc.ap()
        EPS_COL = {1e-30: 0, 1e-5: 1}

        ROW_GAMMA = (0, 0)
        ROW_CONVG = (0, 1024)
        ROW_GRU32 = (0, 0)
        ROW_FF32 = (0, 1024)

        def gain_slice(row, ch):
            base, col = row
            return Rgv[base:base + 1, col + ch * 128: col + (ch + 1) * 128]

        def ones_row():
            return Rgv[0:1, 2048:2048 + 128]

        RS = {"m": (0, 0), "vtmp": (0, 512),
              "invE": (0, 1024), "minvE": (0, 1536), "inv0": (0, 2048),
              "minv0": (0, 2560), "norm": (0, 512), "r": (0, 0)}
        for k in RS:
            RSB[k] = Buf("rs_" + k)
        RSB["norm"] = RSB["vtmp"]
        RSB["r"] = RSB["m"]

        def rs(name, n=512):
            base, col = RS[name]
            return Rsv[base:base + 1, col: col + n]

        def E(eng, fn, reads=(), writes=(), dma=False, inc=True, sem=None,
              incval=None):
            return em.op(eng, fn, reads=reads, writes=writes, dma=dma, inc=inc,
                         sem=sem, incval=incval)

        def matmul_group2(items, bank_buf):
            nit = len(items)
            allreads = []
            for it in items:
                for b in it[3]:
                    if b not in allreads:
                        allreads.append(b)
            for i, (o, l, r, rd) in enumerate(items):
                st, sp = (i == 0), (i == nit - 1)
                fn = (lambda o=o, l=l, r=r, st=st, sp=sp:
                      lambda h: h.matmul(o, l, r, start=st, stop=sp))()
                if i == nit - 1:
                    E("pe", fn, reads=allreads, writes=[bank_buf], inc=True)
                else:
                    for b in rd:
                        for s_, c_ in b.w.items():
                            em._wait_edge("pe", s_, c_)
                    if i == 0:
                        for s_, c_ in bank_buf.w.items():
                            em._wait_edge("pe", s_, c_)
                        for s_, c_ in bank_buf.r.items():
                            em._wait_edge("pe", s_, c_)
                    em.items["pe"].append(("c", fn, None, 0))

        E("sp", lambda h: h.dma_start(out=ident[:], in_=ident_in[:, :]),
          writes=[b_ident], dma=True, sem="dx")
        E("sp", lambda h: h.dma_start(out=vrowsv, in_=vrows_in[:, :]),
          writes=[b_vrows], dma=True, sem="dx")
        E("sp", lambda h: h.dma_start(out=mprev[:], in_=mprev_in[:, :]),
          writes=[b_mprev], dma=True, sem="dx")
        E("sp", lambda h: h.dma_start(out=smask[:], in_=smask_in[:, :]),
          writes=[b_smask], dma=True, sem="dx")
        E("sp", lambda h: h.dma_start(out=smr[:], in_=smr_in[:, :]),
          writes=[b_smr], dma=True, sem="dx")
        for ch in range(DC):
            E("sp", (lambda ch=ch: lambda h: h.dma_start(
                out=xv[:, ch * tcore:(ch + 1) * tcore],
                in_=x_in[ch * 128:(ch + 1) * 128, :]))(),
              writes=[b_x[j] for j in range(nt)], dma=True, sem="dx")
        E("dve", lambda h: h.memset(ones_c[:], 1.0), writes=[b_ones])
        E("dve", lambda h: h.memset(ones_h[:], 1.0), writes=[b_ones])
        E("dve", lambda h: h.memset(epscv[:, 0:1], 1e-30), writes=[b_epsc])
        E("dve", lambda h: h.memset(epscv[:, 1:2], 1e-5), writes=[b_epsc])
        E("dve", lambda h: h.memset(epscv[:, 2:3], -0.5), writes=[b_epsc])
        E("dve", lambda h: h.memset(Rg[:], 0.0), writes=[b_Rg])

        for s in range(32):
            E("pe", (lambda s=s: lambda h: h.transpose(
                ps[6][:, 0:16], vrowsv[0:16, s * 128:(s + 1) * 128],
                identv[0:16, 0:16]))(),
              reads=[b_vrows, b_ident], writes=[bank[6]])
            E("dve", (lambda s=s: lambda h: h.tensor_copy(
                out=vcolsv[:, s * 16:(s + 1) * 16], in_=ps[6][:, 0:16]))(),
              reads=[bank[6]], writes=[b_vcols])

        def vcol(r, ec):
            return vcolsv[:, ec * 16 + r: ec * 16 + r + 1]

        def build_gain(vr, row, addv, mulv):
            for ch in range(DC):
                E("pe", (lambda ch=ch, vr=vr: lambda h: h.transpose(
                    ps[7][0:1, 0:128], vcol(vr, ch), identv[:, :]))(),
                  reads=[b_vcols, b_ident], writes=[bank[7]])
                base, col = row
                dst = Rgv[base:base + 1, col + ch * 128: col + (ch + 1) * 128]
                if addv == 0.0 and mulv == 1.0:
                    E("dve", (lambda dst=dst: lambda h: h.tensor_copy(
                        out=dst, in_=ps[7][0:1, 0:128]))(),
                      reads=[bank[7]], writes=[b_Rg])
                else:
                    E("dve", (lambda dst=dst, addv=addv, mulv=mulv:
                              lambda h: h.tensor_scalar(
                                  out=dst, in0=ps[7][0:1, 0:128], scalar1=addv,
                                  scalar2=mulv, op0=ALU.add, op1=ALU.mult))(),
                      reads=[bank[7]], writes=[b_Rg])

        build_gain(VR_GAMMA, ROW_GAMMA, 0.0, 1.0)
        build_gain(VR_CONVG, ROW_CONVG, 1.0, 1.0)
        _ones_ap = Rgv[0:1, 2048:2048 + 128]
        E("dve", (lambda ap=_ones_ap: lambda h: h.memset(ap, 1.0))(),
          writes=[b_Rg])

        fslot = [0]
        cvi_slot = [0]

        def sign_unit(src_ap, dst_ap, rows0, ncols, img_buf, col0=0):
            st = {}

            def load():
                slot = fslot[0]
                fslot[0] = (fslot[0] + 1) % 6
                st["slot"] = slot
                st["sgv"] = fs_v(slot, ncols)
                E("pool", (lambda sgv=st["sgv"]: lambda h: h.dma_start(
                    out=sgv, in_=src_ap[rows0:rows0 + 128,
                                        col0:col0 + ncols]))(),
                  writes=[b_fs[slot]], dma=True, sem=f"ds{slot}")

            def compute():
                slot, sgv = st["slot"], st["sgv"]
                cslot = cvi_slot[0]
                cvi_slot[0] = (cvi_slot[0] + 1) % 4
                cg = cv16v[:, cslot * 1024: cslot * 1024 + ncols]
                E("act", (lambda cg=cg, sgv=sgv: lambda h: h.activation(
                    out=cg, in_=sgv, func=AF.Sign))(),
                  reads=[b_fs[slot]], writes=[b_cv16[cslot]])

                def write():
                    E("sp", (lambda cg=cg: lambda h: h.dma_start(
                        out=dst_ap[rows0:rows0 + 128, col0:col0 + ncols],
                        in_=cg))(),
                      reads=[b_cv16[cslot]], writes=[img_buf], dma=True,
                      sem=f"di{cslot}")
                return write
            return (load, compute)

        def alpha_unit(nat_ap, rows0, col0, ncols, dst_col, acc, bufa):
            st = {}

            def load():
                slot = fslot[0]
                fslot[0] = (fslot[0] + 1) % 6
                st["slot"] = slot
                st["sgv"] = fs_v(slot, ncols)
                E("pool", (lambda sgv=st["sgv"]: lambda h: h.dma_start(
                    out=sgv, in_=nat_ap[rows0:rows0 + 128,
                                        col0:col0 + ncols]))(),
                  writes=[b_fs[slot]], dma=True, sem=f"ds{slot}")

            def compute():
                slot, sgv = st["slot"], st["sgv"]
                if not acc:
                    E("act", (lambda sgv=sgv: lambda h: h.activation(
                        out=sgv, in_=sgv, func=AF.Abs, accum_out=dst_col))(),
                      reads=[b_fs[slot]], writes=[b_fs[slot], bufa])
                else:
                    t = tmpv[:, 3 * tt: 3 * tt + 1]
                    E("act", (lambda sgv=sgv, t=t: lambda h: h.activation(
                        out=sgv, in_=sgv, func=AF.Abs, accum_out=t))(),
                      reads=[b_fs[slot]], writes=[b_fs[slot], b_tmp[3]])
                    E("dve", (lambda t=t: lambda h: h.tensor_add(
                        out=dst_col, in0=dst_col, in1=t))(),
                      reads=[b_tmp[3], bufa], writes=[bufa])
                return None
            return (load, compute)

        def cast_img(src_ap, dst_ap, img_buf, sem="dcv", rows=None):
            if rows is None:
                sl_ = (slice(None), slice(None))
            else:
                sl_ = (slice(rows[0], rows[1]), slice(None))
            E("pool", (lambda sl_=sl_: lambda h: h.dma_start(
                out=dst_ap[sl_], in_=src_ap[sl_]))(),
              writes=[img_buf], dma=True, sem=sem)

        def cast_pieces(src_ap, dst_ap, img_buf, piece_rows):
            R = src_ap.shape[0]
            out = []
            for r0 in range(0, R, piece_rows):
                r1 = min(r0 + piece_rows, R)
                out.append((lambda r0=r0, r1=r1: cast_img(
                    src_ap, dst_ap, img_buf, rows=(r0, r1))))
            return out

        class Feeder:

            def __init__(self, units):
                self.units = list(units)
                self.li = 0
                self.ci = 0
                self.pending = []

            def take(self, k):
                if self.ci >= len(self.units):
                    while self.pending:
                        self.pending.pop(0)()
                else:
                    while len(self.pending) > 1:
                        self.pending.pop(0)()
                lim = self.li
                self.li = min(self.li + k, len(self.units))
                for u in self.units[lim:self.li]:
                    u[0]()
                for u in self.units[self.ci:lim]:
                    w = u[1]()
                    if w is not None:
                        self.pending.append(w)
                self.ci = lim

            def drain(self):
                while self.ci < len(self.units):
                    self.take(3)
                for w in self.pending:
                    w()
                self.pending = []

        for _c in (0, 1):
            cast_img(wq_p, wimg_q, b_wq_ch[_c], sem=f"dcq{_c}",
                     rows=(_c * 384, (_c + 1) * 384))
        _signs = ([sign_unit(n1_p, wimg_s1, ec * 128, D, b_wimg["s1"])
                   for ec in range(32)]
                  + [sign_unit(n2_p, wimg_s2, ec * 128, 1024, b_wimg["s2"],
                               col0=hh * 1024)
                     for ec in range(8) for hh in range(4)])
        _alphas = ([alpha_unit(n1_nat, ec * 128, 0, D, a1v[:, ec:ec + 1],
                               False, b_alpha1) for ec in range(32)]
                   + [alpha_unit(n2_nat, ec * 128, p * 1024, 1024,
                                 a2v[:, ec:ec + 1], p > 0, b_alpha2)
                      for ec in range(8) for p in range(4)])
        _s1signs = _signs[:32]
        _s2signs = _signs[32:]
        units_s = []
        for i in range(32):
            units_s.append(_s1signs[i])
            units_s.append(_alphas[2 * i])
            units_s.append(_alphas[2 * i + 1])
        units_b = _s2signs

        BK_S1, BK_S2 = 4, 5
        wl_slot = [0]

        def wload(img_ap, img_buf, rows0, k, col0=0, nrow=128):
            slot = wl_slot[0]
            wl_slot[0] ^= 1
            dst = wstrv[:, slot * 4096: slot * 4096 + k]
            if nrow != 128:
                nc_ = nrow // 128
                ima = img_ap[rows0:rows0 + nrow, col0:col0 + k // nc_]
                ima = ima.rearrange("(c p) k -> p c k", p=128)
                dst3 = dst.rearrange("p (c k) -> p c k", c=nc_)
                E("sp", (lambda dst3=dst3, ima=ima: lambda h: h.dma_start(
                    out=dst3, in_=ima))(),
                  reads=[img_buf], writes=[b_wstr[slot]], dma=True,
                  sem=f"dw{slot}")
                return dst, b_wstr[slot]
            E("sp", (lambda dst=dst: lambda h: h.dma_start(
                out=dst, in_=img_ap[rows0:rows0 + 128, col0:col0 + k]))(),
              reads=[img_buf], writes=[b_wstr[slot]], dma=True,
              sem=f"dw{slot}")
            return dst, b_wstr[slot]

        def site_stats(n, src_fn, src_bufs, want_ln, eps_list):
            if want_ln:
                items = [(ps[BK_S1][0:1, 0:n], onesv[:, :], src_fn(ch),
                          [b_ones] + src_bufs) for ch in range(DC)]
                matmul_group2(items, bank[BK_S1])
            for ch in range(DC):
                slot = ch % 2
                sq = sqhv[:, slot * tt: slot * tt + n]
                E("act", (lambda sq=sq, ch=ch: lambda h: h.activation(
                    out=sq, in_=src_fn(ch), func=AF.Square))(),
                  reads=src_bufs, writes=[b_sqb[slot]])
                fn = (lambda sq=sq, ch=ch, n=n: lambda h: h.matmul(
                    ps[BK_S2][0:1, 0:n], oneshv[:, :], sq,
                    start=(ch == 0), stop=(ch == DC - 1)))()
                E("pe", fn, reads=[b_ones, b_sqb[slot]],
                  writes=[bank[BK_S2]], inc=True)
            if want_ln:
                E("dve", (lambda n=n: lambda h: h.tensor_scalar_mul(
                    out=rs("m", n), in0=ps[BK_S1][0:1, 0:n], scalar1=1.0 / D))(),
                  reads=[bank[BK_S1]], writes=[RSB["m"]])
                E("pool", (lambda n=n: lambda h: h.tensor_mul(
                    out=rs("vtmp", n), in0=rs("m", n), in1=rs("m", n)))(),
                  reads=[RSB["m"]], writes=[RSB["vtmp"]])
                E("dve", (lambda n=n: lambda h: h.scalar_tensor_tensor(
                    out=rs("vtmp", n), in0=ps[BK_S2][0:1, 0:n],
                    scalar=1.0 / D, in1=rs("vtmp", n),
                    op0=ALU.mult, op1=ALU.subtract))(),
                  reads=[bank[BK_S2], RSB["vtmp"]], writes=[RSB["vtmp"]])
                for (inv_name, minv_name, eps) in eps_list:
                    ec_ = EPS_COL[eps]
                    E("act", (lambda n=n, ec_=ec_, minv_name=minv_name:
                              lambda h: h.activation(
                                  out=rs(minv_name, n), in_=rs("vtmp", n),
                                  func=AF.Ln, bias=epscv[0:1, ec_:ec_ + 1]))(),
                      reads=[RSB["vtmp"], b_epsc], writes=[RSB[minv_name]])
                    E("act", (lambda n=n, inv_name=inv_name,
                              minv_name=minv_name: lambda h: h.activation(
                                  out=rs(inv_name, n), in_=rs(minv_name, n),
                                  func=AF.Exp, scale=epscv[0:1, 2:3]))(),
                      reads=[RSB[minv_name], b_epsc], writes=[RSB[inv_name]])
                    E("pool", (lambda n=n, inv_name=inv_name,
                               minv_name=minv_name: lambda h: h.tensor_mul(
                                   out=rs(minv_name, n), in0=rs("m", n),
                                   in1=rs(inv_name, n)))(),
                      reads=[RSB["m"], RSB[inv_name]], writes=[RSB[minv_name]])
            else:
                E("dve", (lambda n=n: lambda h: h.tensor_scalar_max(
                    out=rs("norm", n), in0=ps[BK_S2][0:1, 0:n],
                    scalar1=1e-24))(),
                  reads=[bank[BK_S2]], writes=[RSB["norm"]])
                E("act", (lambda n=n: lambda h: h.activation(
                    out=rs("minvE", n), in_=rs("norm", n), func=AF.Ln))(),
                  reads=[RSB["norm"]], writes=[RSB["minvE"]])
                E("act", (lambda n=n: lambda h: h.activation(
                    out=rs("r", n), in_=rs("minvE", n), func=AF.Exp,
                    scale=epscv[0:1, 2:3]))(),
                  reads=[RSB["minvE"], b_epsc], writes=[RSB["r"]])

        def apply_ln(n, src_fn, src_bufs, gain_row, inv_name, minv_name,
                     out_fn, out_bufs, bb=6, bc=7, tmp_i=0):
            for ch in range(DC):
                p2 = 2 * (ch % 2)
                bbx, bcx = 4 + p2, 5 + p2
                g = gain_slice(gain_row, ch) if gain_row is not None else ones_row()
                E("pe", (lambda g=g, n=n, inv_name=inv_name, bbx=bbx:
                         lambda h: h.matmul(ps[bbx][:, 0:n], g, rs(inv_name, n),
                                            start=True, stop=True))(),
                  reads=[b_Rg, RSB[inv_name]], writes=[bank[bbx]])
                E("pe", (lambda g=g, n=n, minv_name=minv_name, bcx=bcx:
                         lambda h: h.matmul(ps[bcx][:, 0:n], g, rs(minv_name, n),
                                            start=True, stop=True))(),
                  reads=[b_Rg, RSB[minv_name]], writes=[bank[bcx]])
                t = tmpv[:, (tmp_i + ch % 2) * tt: (tmp_i + ch % 2) * tt + n]
                E("dve", (lambda t=t, ch=ch, n=n, bbx=bbx: lambda h: h.tensor_mul(
                    out=t, in0=src_fn(ch), in1=ps[bbx][:, 0:n]))(),
                  reads=src_bufs + [bank[bbx]], writes=[b_tmp[tmp_i + ch % 2]])
                E("dve", (lambda t=t, ch=ch, n=n, bcx=bcx: lambda h: h.tensor_sub(
                    out=out_fn(ch), in0=t, in1=ps[bcx][:, 0:n]))(),
                  reads=[b_tmp[tmp_i + ch % 2], bank[bcx]], writes=[out_bufs[ch]])

        def apply_rms(n, src_fn, src_bufs, gain_row, out_fn, out_bufs, bb=6):
            for ch in range(DC):
                bbx = 4 + (ch % 4)
                g = gain_slice(gain_row, ch)
                E("pe", (lambda g=g, n=n, bbx=bbx: lambda h: h.matmul(
                    ps[bbx][:, 0:n], g, rs("r", n), start=True, stop=True))(),
                  reads=[b_Rg, RSB["r"]], writes=[bank[bbx]])
                E("dve", (lambda ch=ch, n=n, bbx=bbx: lambda h: h.tensor_mul(
                    out=out_fn(ch), in0=src_fn(ch), in1=ps[bbx][:, 0:n]))(),
                  reads=src_bufs + [bank[bbx]], writes=[out_bufs[ch]])

        def exchange_prep(colA_fn, colA_bufs, colB_fn, colB_bufs):
            al_w = [b_rows2, b_zeros, b_fs[4], b_fs[5]]
            for ch in range(DC):
                bk_ = 6 if ch < 4 else 5
                co = (ch % 4) * 128
                E("pe", (lambda ch=ch: lambda h: h.transpose(
                    ps[4][0:1, 0:128], colA_fn(ch), identv[:, :]))(),
                  reads=colA_bufs + [b_ident], writes=[bank[4]])
                E("dve", (lambda ch=ch: lambda h: h.tensor_copy(
                    out=rows2v[0:1, ch * 128:(ch + 1) * 128],
                    in_=ps[4][0:1, 0:128]))(),
                  reads=[bank[4]], writes=al_w)
                E("pe", (lambda ch=ch: lambda h: h.transpose(
                    ps[7][0:1, 0:128], colB_fn(ch), identv[:, :]))(),
                  reads=colB_bufs + [b_ident], writes=[bank[7]])
                E("dve", (lambda ch=ch: lambda h: h.tensor_copy(
                    out=rows2v[0:1, D + ch * 128: D + (ch + 1) * 128],
                    in_=ps[7][0:1, 0:128]))(),
                  reads=[bank[7]], writes=al_w)
                E("pe", (lambda ch=ch, bk_=bk_, co=co: lambda h: h.matmul(
                    ps[bk_][0:16, co:co + 128], smrv[0:1, 0:16],
                    rows2v[0:1, ch * 128:(ch + 1) * 128],
                    start=True, stop=False))(),
                  reads=[b_smr, b_rows2], writes=[bank[bk_]])
                E("pe", (lambda ch=ch, bk_=bk_, co=co: lambda h: h.matmul(
                    ps[bk_][0:16, co:co + 128], smrv[0:1, 16:32],
                    rows2v[0:1, D + ch * 128: D + (ch + 1) * 128],
                    start=False, stop=True))(),
                  reads=[b_smr, b_rows2], writes=[bank[bk_]])

        def exchange_send(tag, cc_ap):
            E("dve", lambda h: h.tensor_copy(
                out=ccstv[0:16, 0:512], in_=ps[6][0:16, 0:512]),
              reads=[bank[6]], writes=[b_ccst])
            E("dve", lambda h: h.tensor_copy(
                out=ccstv[0:16, 512:1024], in_=ps[5][0:16, 0:512]),
              reads=[bank[5]], writes=[b_ccst])
            E("sp", (lambda cc_ap=cc_ap: lambda h: h.dma_start(
                out=cc_ap[:, :], in_=ccstv))(),
              reads=[b_ccst], writes=[b_cc[tag]], dma=True, sem="dcb")
            flat = cc_ap.rearrange("a b -> (a b)").rearrange("(o f) -> o f",
                                                             o=1)
            E("pool", (lambda flat=flat: lambda h: h.collective_compute(
                "AllReduce", ALU.add, replica_groups=[list(range(8))],
                ins=[flat], outs=[flat]))(),
              reads=[b_cc[tag]], writes=[b_cc[tag]], sem="cc", incval=1)

        def exchange_recv(tag, cc_ap):
            E("sp", (lambda cc_ap=cc_ap: lambda h: h.dma_start(
                out=ccst[:], in_=cc_ap[:, :]))(),
              reads=[b_cc[tag]], writes=[b_ccst], dma=True, sem="dcb")
            for ch in range(DC):
                E("pe", (lambda ch=ch: lambda h: h.transpose(
                    ps[7][:, 0:16], ccstv[0:16, ch * 128:(ch + 1) * 128],
                    identv[0:16, 0:16]))(),
                  reads=[b_ccst, b_ident], writes=[bank[7]])
                E("dve", (lambda ch=ch: lambda h: h.tensor_copy(
                    out=Gv[:, ch * 16:(ch + 1) * 16], in_=ps[7][:, 0:16]))(),
                  reads=[bank[7]], writes=[b_G])

        qpf_bufs = [b_tA[0], b_tA[1], b_tV[0], b_tV[1],
                    b_tQ[0], b_tQ[1], b_tH[0], b_tH[1]]

        def qpf_v(ch):
            return Uv[:, ch * tt: (ch + 1) * tt]

        def prefetch_corr(scr, scr_buf):
            for ch in range(DC):
                E("pool", (lambda ch=ch, scr=scr: lambda h: h.dma_start(
                    out=qpf_v(ch), in_=scr[0, ch, :, :]))(),
                  reads=[scr_buf], writes=[qpf_bufs[ch], b_uext[ch]],
                  dma=True, sem="dpf")

        def combine_carries():
            Hc = tmpv[:, 0:8]
            E("dve", lambda h: h.memset(hsel[:], 0.0), writes=[b_hsel])
            Gr = Gv.rearrange("p (c s) -> p c s", s=16)
            for k in range(7):
                Ak = Gr[:, :, 2 * k:2 * k + 1]
                bk_ = Gr[:, :, 2 * k + 1:2 * k + 2]
                if k == 0:
                    E("dve", (lambda bk_=bk_: lambda h: h.tensor_copy(
                        out=Hc, in_=bk_))(),
                      reads=[b_G], writes=[b_tmp[0]])
                else:
                    E("dve", (lambda Ak=Ak: lambda h: h.tensor_mul(
                        out=Hc, in0=Hc, in1=Ak))(),
                      reads=[b_G, b_tmp[0]], writes=[b_tmp[0]])
                    E("dve", (lambda bk_=bk_: lambda h: h.tensor_add(
                        out=Hc, in0=Hc, in1=bk_))(),
                      reads=[b_G, b_tmp[0]], writes=[b_tmp[0]])
                E("dve", (lambda k=k: lambda h: h.scalar_tensor_tensor(
                    out=hsel[:], in0=Hc, scalar=mprevv[:, k:k + 1],
                    in1=hsel[:], op0=ALU.mult, op1=ALU.add))(),
                  reads=[b_tmp[0], b_mprev, b_hsel], writes=[b_hsel])

        em.fence()

        E("dve", lambda h: h.memset(hcar[:], 0.0), writes=b_hcar)
        E("dve", lambda h: h.memset(pcar[:], 1.0), writes=b_pcar)
        E("dve", lambda h: h.memset(zerosv, 0.0), writes=[b_zeros])
        sfeed = Feeder(units_s)
        sfeed.take(3)

        cpieces_a = (cast_pieces(pw_p, wimg_pw, b_wimg["pw"], 128)
                     + cast_pieces(gru_p, wimg_gru, b_wimg["gru"], 128))
        cpieces_b = (cast_pieces(ff1_p, wimg_ff1, b_wimg["ff1"], 128)
                     + cast_pieces(ff2_p, wimg_ff2, b_wimg["ff2"], 32))
        cpa_i = [0]
        cpb_i = [0]

        def feed_casts_a(k):
            for fn in cpieces_a[cpa_i[0]:cpa_i[0] + k]:
                fn()
            cpa_i[0] = min(cpa_i[0] + k, len(cpieces_a))

        def feed_casts_b(k):
            for fn in cpieces_b[cpb_i[0]:cpb_i[0] + k]:
                fn()
            cpb_i[0] = min(cpb_i[0] + k, len(cpieces_b))

        def xnb_v(j, ch):
            base = xnbv if j % 2 == 0 else hbfv
            return base[:, ch * tt:(ch + 1) * tt]

        def xnb_b(j):
            return b_xnb if j % 2 == 0 else b_xnbh

        def a_stats(j):
            site_stats(tt, lambda ch, j=j: x_ch(ch, j), [b_x[j]], True,
                       [("invE", "minvE", 1e-30)])

        def a_apply(j):
            apply_ln(tt, lambda ch, j=j: x_ch(ch, j), [b_x[j]], ROW_GAMMA,
                     "invE", "minvE", lambda ch, j=j: xnb_v(j, ch), xnb_b(j))

        a_stats(0)
        a_apply(0)
        for j in range(nt):
            qk_rot = [0]
            for ch in range(DC):
                sl = ch % 4
                wv3, wb3 = wload(wimg_q, b_wq_ch[ch], ch * 384, 3 * D,
                                 nrow=384)
                trip = [(0, "kv"), (1, "a"), (2, "q")]
                for (ti, kind) in trip:
                    bi = qk_rot[0] % 4
                    qk_rot[0] += 1
                    wv = wv3[:, ti * D: (ti + 1) * D]
                    items = [(ps[bi][:, 0:tt], wv[:, kc * 128:(kc + 1) * 128],
                              xnb_v(j, kc),
                              [wb3, xnb_b(j)[kc]]) for kc in range(DC)]
                    matmul_group2(items, bank[bi])
                    if kind == "kv":
                        E("act", (lambda ch=ch, bi=bi: lambda h: h.activation(
                            out=tV_v(ch), in_=ps[bi][:, 0:tt],
                            func=AF.Copy))(),
                          reads=[bank[bi]], writes=[b_tV[sl]])
                    elif kind == "a":
                        E("act", (lambda ch=ch, bi=bi: lambda h: h.activation(
                            out=tA_v(ch), in_=ps[bi][:, 0:tt],
                            func=AF.Sigmoid))(),
                          reads=[bank[bi]], writes=[b_tA[sl]])
                    else:
                        E("act", (lambda ch=ch, bi=bi: lambda h: h.activation(
                            out=tQ_v(ch), in_=ps[bi][:, 0:tt], func=AF.Copy))(),
                          reads=[bank[bi]], writes=[b_tQ[sl]])
                E("dve", (lambda ch=ch: lambda h: h.tensor_tensor_scan(
                    out=tH_v(ch), data0=tA_v(ch), data1=tV_v(ch),
                    initial=hcarv[:, ch:ch + 1], op0=ALU.mult, op1=ALU.add))(),
                  reads=[b_tA[sl], b_tV[sl], b_hcar[ch]], writes=[b_tH[sl]])
                E("dve", (lambda ch=ch: lambda h: h.tensor_copy(
                    out=hcarv[:, ch:ch + 1], in_=tH_v(ch)[:, tt - 1:tt]))(),
                  reads=[b_tH[sl]], writes=[b_hcar[ch]])
                E("dve", (lambda ch=ch: lambda h: h.tensor_tensor_scan(
                    out=tV_v(ch), data0=tA_v(ch), data1=zerosv[:, :],
                    initial=pcarv[:, ch:ch + 1], op0=ALU.mult, op1=ALU.add))(),
                  reads=[b_tA[sl], b_zeros, b_pcar[ch], b_tV[sl]],
                  writes=[b_tV[sl]])
                E("dve", (lambda ch=ch: lambda h: h.tensor_copy(
                    out=pcarv[:, ch:ch + 1], in_=tV_v(ch)[:, tt - 1:tt]))(),
                  reads=[b_tV[sl]], writes=[b_pcar[ch]])
                tq = tq_v(ch)
                E("pool", (lambda ch=ch, tq=tq: lambda h: h.tensor_mul(
                    out=tq, in0=tQ_v(ch), in1=tH_v(ch)))(),
                  reads=[b_tQ[sl], b_tH[sl]], writes=[b_tq[sl]])
                E("dve", (lambda ch=ch, j=j, tq=tq: lambda h: h.tensor_add(
                    out=x_ch(ch, j), in0=x_ch(ch, j), in1=tq))(),
                  reads=[b_tq[sl], b_x[j]], writes=[b_x[j]])
                E("pool", (lambda ch=ch: lambda h: h.tensor_mul(
                    out=tH_v(ch), in0=tQ_v(ch), in1=tV_v(ch)))(),
                  reads=[b_tQ[sl], b_tV[sl]], writes=[b_tH[sl]])
                E("pool", (lambda j=j, ch=ch: lambda h: h.dma_start(
                    out=qp_scr[j, ch, :, :], in_=tH_v(ch)))(),
                  reads=[b_tH[sl]], writes=[b_qp[j]], dma=True, sem="dq")
                if j == 0 and ch < 6:
                    cast_img(wq_p, wimg_q, b_wq_ch[ch + 2], sem=f"dcq{ch + 2}",
                             rows=((ch + 2) * 384, (ch + 3) * 384))
                if ch == 3 and j + 1 < nt:
                    a_stats(j + 1)
                if ch == 6 and j + 1 < nt:
                    a_apply(j + 1)
                sfeed.take(3)
                if j > 0 or nt == 1:
                    feed_casts_a(1)

        sfeed.drain()
        feed_casts_a(len(cpieces_a))
        E("dve", lambda h: h.tensor_scalar_mul(
            out=a1v[:, :], in0=a1v[:, :], scalar1=1.0 / D),
          reads=[b_alpha1], writes=[b_alpha1])
        E("dve", lambda h: h.tensor_scalar_mul(
            out=a2v[:, :], in0=a2v[:, :], scalar1=1.0 / (4 * D)),
          reads=[b_alpha2], writes=[b_alpha2])

        exchange_prep(lambda ch: pcarv[:, ch:ch + 1], list(b_pcar),
                      lambda ch: hcarv[:, ch:ch + 1], list(b_hcar))
        exchange_send("cc1", cc1)
        prefetch_corr(qp_scr, b_qp[0])
        exchange_recv("cc1", cc1)
        combine_carries()

        def corr_tile(scr, j, pref=False):
            for ch in range(DC):
                sl = ch % 2
                if pref and j == 0:
                    E("dve", (lambda j=j, ch=ch: lambda h:
                              h.scalar_tensor_tensor(
                                  out=x_ch(ch, j), in0=qpf_v(ch),
                                  scalar=hselv[:, ch:ch + 1], in1=x_ch(ch, j),
                                  op0=ALU.mult, op1=ALU.add))(),
                      reads=[qpf_bufs[ch], b_uext[ch], b_hsel, b_x[j]],
                      writes=[b_x[j]])
                    continue
                cr = corrv[:, sl * tt: (sl + 1) * tt]
                E("pool", (lambda j=j, ch=ch, cr=cr, scr=scr:
                           lambda h: h.dma_start(
                               out=cr, in_=scr[j, ch, :, :]))(),
                  reads=[b_qp[j] if scr is qp_scr else b_pg[j]],
                  writes=[b_corr[sl]], dma=True, sem=f"dcp{sl}")
                E("dve", (lambda j=j, ch=ch, cr=cr:
                          lambda h: h.scalar_tensor_tensor(
                              out=x_ch(ch, j), in0=cr,
                              scalar=hselv[:, ch:ch + 1], in1=x_ch(ch, j),
                              op0=ALU.mult, op1=ALU.add))(),
                  reads=[b_corr[sl], b_hsel, b_x[j]], writes=[b_x[j]])

        def emit_mlp(kind, w1_img, w1_buf, w2_img, w2_buf, b1_col, epilogue,
                     alpha, bunits=None, gain_row=None, pre_tile=None,
                     cast_fn=None):
            bfeed = Feeder(bunits) if bunits else None

            def head_stats(j):
                if kind == "ln":
                    site_stats(tt, lambda ch, j=j: x_ch(ch, j), [b_x[j]], True,
                               [("invE", "minvE", 1e-5)])
                else:
                    site_stats(tt, lambda ch, j=j: x_ch(ch, j), [b_x[j]],
                               False, [])

            def head_apply(j):
                if kind == "ln":
                    apply_ln(tt, lambda ch, j=j: x_ch(ch, j), [b_x[j]], None,
                             "invE", "minvE",
                             lambda ch: xnbv[:, ch * tt:(ch + 1) * tt], b_xnb)
                else:
                    apply_rms(tt, lambda ch, j=j: x_ch(ch, j), [b_x[j]],
                              gain_row,
                              lambda ch: xnbv[:, ch * tt:(ch + 1) * tt], b_xnb)

            if pre_tile is not None:
                pre_tile(0)
            head_stats(0)
            head_apply(0)
            for j in range(nt):
                for ec in range(32):
                    bi = ec % 2
                    wv, wb = wload(w1_img, w1_buf, ec * 128, D)
                    items = [(ps[bi][:, 0:tt], wv[:, kc * 128:(kc + 1) * 128],
                              xnbv[:, kc * tt:(kc + 1) * tt],
                              [wb, b_xnb[kc]]) for kc in range(DC)]
                    matmul_group2(items, bank[bi])
                    bi_c = b1_col(ec)
                    rd = [bank[bi], b_vcols] + ([b_alpha1] if alpha else [])
                    if alpha:
                        sc = a1v[:, ec:ec + 1]
                        E("act", (lambda ec=ec, bi=bi, sc=sc, bi_c=bi_c:
                                  lambda h: h.activation(
                                      out=hbf_v(ec), in_=ps[bi][:, 0:tt],
                                      func=AF.Gelu, scale=sc, bias=bi_c))(),
                          reads=rd, writes=[b_hbf[ec]])
                    else:
                        E("act", (lambda ec=ec, bi=bi, bi_c=bi_c:
                                  lambda h: h.activation(
                                      out=hbf_v(ec), in_=ps[bi][:, 0:tt],
                                      func=AF.Gelu, bias=bi_c))(),
                          reads=rd, writes=[b_hbf[ec]])
                    if bfeed:
                        bfeed.take(1)
                    if cast_fn:
                        cast_fn()
                    if ec == 14 and j + 1 < nt and pre_tile is not None:
                        pre_tile(j + 1)
                    if ec == 16 and j + 1 < nt:
                        head_stats(j + 1)
                for dc in range(DC):
                    bi = 2 + dc % 2
                    wv1, wb1 = wload(w2_img, w2_buf, dc * 128, 4096)
                    items = [(ps[bi][:, 0:tt], wv1[:, kc * 128:(kc + 1) * 128],
                              hbf_v(kc), [wb1, b_hbf[kc]]) for kc in range(32)]
                    matmul_group2(items, bank[bi])
                    epilogue(j, dc, ps[bi][:, 0:tt], bank[bi])
                    if bfeed:
                        bfeed.take(1)
                    if cast_fn:
                        cast_fn()
                    if dc == 2 and j + 1 < nt:
                        head_apply(j + 1)
            if bfeed:
                bfeed.drain()

        def ffn_epilogue(store):
            def ep(j, dc, psum_ap, bk):
                sl = dc % 2
                t = tmpv[:, sl * tt: (sl + 1) * tt]
                E("dve", (lambda psum_ap=psum_ap, dc=dc, t=t:
                          lambda h: h.tensor_scalar(
                              out=t, in0=psum_ap, scalar1=a2v[:, dc:dc + 1],
                              scalar2=vcol(VR_NB2, dc), op0=ALU.mult,
                              op1=ALU.add))(),
                  reads=[bk, b_alpha2, b_vcols], writes=[b_tmp[sl]])
                E("dve", (lambda j=j, dc=dc, t=t: lambda h: h.tensor_add(
                    out=x_ch(dc, j), in0=x_ch(dc, j), in1=t))(),
                  reads=[b_tmp[sl], b_x[j]], writes=[b_x[j]])
                if store:
                    E("pool", (lambda j=j, dc=dc: lambda h: h.dma_start(
                        out=out_o.rearrange("(c p) t -> p c t", p=128)
                        [:, dc, j * tt:(j + 1) * tt],
                        in_=x_ch(dc, j)))(),
                      reads=[b_x[j]], writes=[b_out], dma=True, sem="dout")
            return ep

        def _bcasts(j):
            corr_tile(qp_scr, j, pref=(j == 0))

        em.fence()
        if upto >= 2:
            emit_mlp("ln", wimg_s1, b_wimg["s1"], wimg_s2, b_wimg["s2"],
                     lambda ec: vcol(VR_NB1, ec), ffn_epilogue(False), True,
                     bunits=units_b, pre_tile=_bcasts,
                     cast_fn=lambda: feed_casts_b(1))
        elif upto == 1:
            pass
        else:
            for j in range(nt):
                corr_tile(qp_scr, j)
        feed_casts_b(len(cpieces_b))

        class _EndPhase3(Exception):
            pass
        try:
         if upto >= 3:
            build_gain(VR_GRUG, ROW_GRU32, 1.0, 32.0)
            site_stats(2, lambda ch: x_cols(ch, tcore - 2, 2), [b_x[nt - 1]],
                       True, [("inv0", "minv0", 1e-30)])
            for ch in range(DC):
                g = gain_slice(ROW_CONVG, ch)
                E("pe", (lambda g=g: lambda h: h.matmul(
                    ps[6][:, 0:2], g, rs("inv0", 2), start=True, stop=True))(),
                  reads=[b_Rg, RSB["inv0"]], writes=[bank[6]])
                E("pe", (lambda g=g: lambda h: h.matmul(
                    ps[7][:, 0:2], g, rs("minv0", 2), start=True, stop=True))(),
                  reads=[b_Rg, RSB["minv0"]], writes=[bank[7]])
                t = tmpv[:, 0:2]
                E("dve", (lambda ch=ch, t=t: lambda h: h.tensor_mul(
                    out=t, in0=x_cols(ch, tcore - 2, 2), in1=ps[6][:, 0:2]))(),
                  reads=[b_x[nt - 1], bank[6]], writes=[b_tmp[0]])
                E("dve", (lambda ch=ch, t=t: lambda h: h.tensor_sub(
                    out=uh2v[:, ch * 2: ch * 2 + 2], in0=t,
                    in1=ps[7][:, 0:2]))(),
                  reads=[b_tmp[0], bank[7]], writes=[b_uh2])
            em.fence()
            exchange_prep(lambda ch: uh2v[:, ch * 2: ch * 2 + 1], [b_uh2],
                          lambda ch: uh2v[:, ch * 2 + 1: ch * 2 + 2], [b_uh2])
            exchange_send("cc2", cc2)
            _s3 = SUB3

            uer = Uv[:, 0:DC * (tt + 4)].rearrange("p (c t) -> p c t", c=DC)
            Gr = Gv.rearrange("p (c s) -> p c s", s=16)

            def a3_stats(j):
                for ch in range(DC):
                    E("sp", (lambda j=j, ch=ch: lambda h: h.dma_start(
                        out=xold_scr[j, ch, :, :], in_=x_ch(ch, j)))(),
                      reads=[b_x[j]], writes=[b_xold[j]], dma=True, sem="dqh")
                site_stats(tt, lambda ch, j=j: x_ch(ch, j), [b_x[j]], True,
                           [("invE", "minvE", 1e-5), ("inv0", "minv0", 1e-30)])

            def a3_uln(j):
                for ch in range(DC):
                    g2 = gain_slice(ROW_CONVG, ch)
                    E("pe", (lambda g2=g2: lambda h: h.matmul(
                        ps[6][:, 0:tt], g2, rs("inv0", tt),
                        start=True, stop=True))(),
                      reads=[b_Rg, RSB["inv0"]], writes=[bank[6]])
                    E("pe", (lambda g2=g2: lambda h: h.matmul(
                        ps[7][:, 0:tt], g2, rs("minv0", tt),
                        start=True, stop=True))(),
                      reads=[b_Rg, RSB["minv0"]], writes=[bank[7]])
                    t0 = tmpv[:, 0:tt]
                    E("dve", (lambda ch=ch, j=j, t0=t0: lambda h: h.tensor_mul(
                        out=t0, in0=x_ch(ch, j), in1=ps[6][:, 0:tt]))(),
                      reads=[b_x[j], bank[6]], writes=[b_tmp[0]])
                    E("dve", (lambda ch=ch, t0=t0: lambda h: h.tensor_sub(
                        out=u_ch(ch, 2, tt), in0=t0, in1=ps[7][:, 0:tt]))(),
                      reads=[b_tmp[0], bank[7]], writes=[b_uext[ch]])
                    o = ones_row()
                    E("pe", (lambda o=o: lambda h: h.matmul(
                        ps[4][:, 0:tt], o, rs("invE", tt),
                        start=True, stop=True))(),
                      reads=[b_Rg, RSB["invE"]], writes=[bank[4]])
                    E("pe", (lambda o=o: lambda h: h.matmul(
                        ps[5][:, 0:tt], o, rs("minvE", tt),
                        start=True, stop=True))(),
                      reads=[b_Rg, RSB["minvE"]], writes=[bank[5]])
                    t1 = tmpv[:, tt:2 * tt]
                    E("dve", (lambda ch=ch, j=j, t1=t1: lambda h: h.tensor_mul(
                        out=t1, in0=x_ch(ch, j), in1=ps[4][:, 0:tt]))(),
                      reads=[b_x[j], bank[4], b_xold[j]], writes=[b_tmp[1]])
                    E("dve", (lambda ch=ch, j=j, t1=t1: lambda h: h.tensor_sub(
                        out=x_ch(ch, j), in0=t1, in1=ps[5][:, 0:tt]))(),
                      reads=[b_tmp[1], bank[5]], writes=[b_x[j]])

            def a3_taps_pw(j):
                for ch in range(DC):
                    E("dve", (lambda ch=ch: lambda h: h.tensor_scalar(
                        out=ycvv, in0=u_ch(ch, 2, tt),
                        scalar1=vcol(VR_T2, ch), scalar2=vcol(VR_BDW, ch),
                        op0=ALU.mult, op1=ALU.add))(),
                      reads=[b_uext[ch], b_vcols], writes=[b_ycv])
                    E("dve", (lambda ch=ch: lambda h: h.scalar_tensor_tensor(
                        out=ycvv, in0=u_ch(ch, 1, tt),
                        scalar=vcol(VR_T1, ch), in1=ycvv,
                        op0=ALU.mult, op1=ALU.add))(),
                      reads=[b_uext[ch], b_vcols, b_ycv], writes=[b_ycv])
                    E("dve", (lambda ch=ch: lambda h: h.scalar_tensor_tensor(
                        out=ybf_v(ch), in0=u_ch(ch, 0, tt),
                        scalar=vcol(VR_T0, ch), in1=ycvv,
                        op0=ALU.mult, op1=ALU.add))(),
                      reads=[b_uext[ch], b_vcols, b_ycv], writes=[b_ybf[ch]])
                    E("dve", (lambda ch=ch: lambda h: h.tensor_copy(
                        out=u_ch(ch, 0, 2), in_=u_ch(ch, tt, 2)))(),
                      reads=[b_uext[ch]], writes=[b_uext[ch]])
                for ec in range(DC):
                    bi = 2 + ec % 2
                    wv, wb = wload(wimg_pw, b_wimg["pw"], ec * 128, D)
                    items = [(ps[bi][:, 0:tt], wv[:, kc * 128:(kc + 1) * 128],
                              ybf_v(kc), [wb, b_ybf[kc]]) for kc in range(DC)]
                    matmul_group2(items, bank[bi])
                    sl = ec % 2
                    t = tmpv[:, (2 + sl) * tt: (3 + sl) * tt]
                    E("dve", (lambda bi=bi, ec=ec, t=t:
                              lambda h: h.tensor_scalar_add(
                                  out=t, in0=ps[bi][:, 0:tt],
                                  scalar1=vcol(VR_BPW, ec)))(),
                      reads=[bank[bi], b_vcols], writes=[b_tmp[2 + sl]])
                    E("dve", (lambda j=j, ec=ec, t=t: lambda h: h.tensor_add(
                        out=x_ch(ec, j), in0=x_ch(ec, j), in1=t))(),
                      reads=[b_tmp[2 + sl], b_x[j]], writes=[b_x[j]])
                    if ec == 2 and j + 1 < nt and _s3 >= 3:
                        a3_stats(j + 1)

            if _s3 >= 3:
                a3_stats(0)
                a3_uln(0)
            exchange_recv("cc2", cc2)
            for par in range(2):
                dst = uer[:, :, par:par + 1]
                E("dve", (lambda dst=dst: lambda h: h.memset(dst, 0.0))(),
                  writes=list(b_uext))
            for k in range(7):
                for par in range(2):
                    dst = uer[:, :, par:par + 1]
                    src_ = Gr[:, :, 2 * k + par:2 * k + par + 1]
                    E("dve", (lambda dst=dst, src_=src_, k=k:
                              lambda h: h.scalar_tensor_tensor(
                                  out=dst, in0=src_, scalar=mprevv[:, k:k + 1],
                                  in1=dst, op0=ALU.mult, op1=ALU.add))(),
                      reads=[b_G, b_mprev] + list(b_uext),
                      writes=list(b_uext))
            if _s3 >= 3:
                a3_taps_pw(0)
                for j in range(1, nt):
                    a3_uln(j)
                    a3_taps_pw(j)

            em.fence()

            E("dve", lambda h: h.memset(hcar[:], 0.0), writes=b_hcar)
            E("dve", lambda h: h.memset(pcar[:], 1.0), writes=b_pcar)
            E("dve", lambda h: h.memset(zerosv, 0.0), writes=[b_zeros])
            def g_stats(j):
                site_stats(tt, lambda ch, j=j: x_ch(ch, j), [b_x[j]],
                           False, [])

            def g_apply(j):
                apply_rms(tt, lambda ch, j=j: x_ch(ch, j), [b_x[j]],
                          ROW_GRU32,
                          lambda ch, j=j: xnb_v(j, ch), xnb_b(j))

            if _s3 >= 4 and nt > 0:
                g_stats(0)
                g_apply(0)
            for j in range(nt if _s3 >= 4 else 0):
                for ch in range(DC):
                    sl = ch % 4
                    sl2 = ch % 2
                    bh = 2 + sl2
                    bg = sl2
                    wvg, wbg = wload(wimg_gru, b_wimg["gru"], ch * 256, 2 * D,
                                     nrow=256)
                    items = [(ps[bh][:, 0:tt], wvg[:, kc * 128:(kc + 1) * 128],
                              xnb_v(j, kc),
                              [wbg, xnb_b(j)[kc]]) for kc in range(DC)]
                    matmul_group2(items, bank[bh])
                    items = [(ps[bg][:, 0:tt],
                              wvg[:, D + kc * 128: D + (kc + 1) * 128],
                              xnb_v(j, kc),
                              [wbg, xnb_b(j)[kc]]) for kc in range(DC)]
                    matmul_group2(items, bank[bg])
                    s_t = tmpv[:, sl2 * tt: (sl2 + 1) * tt]
                    sv_t = tmpv[:, (2 + sl2) * tt: (3 + sl2) * tt]
                    E("act", (lambda bh=bh, s_t=s_t: lambda h: h.activation(
                        out=s_t, in_=ps[bh][:, 0:tt], func=AF.Sigmoid))(),
                      reads=[bank[bh]], writes=[b_tmp[sl2]])
                    E("dve", (lambda bh=bh, ch=ch: lambda h: h.tensor_scalar_add(
                        out=tV_v(ch), in0=ps[bh][:, 0:tt], scalar1=0.5))(),
                      reads=[bank[bh], b_tmp[sl2]], writes=[b_tV[sl]])
                    E("act", (lambda bg=bg, ch=ch: lambda h: h.activation(
                        out=tA_v(ch), in_=ps[bg][:, 0:tt], func=AF.Sigmoid,
                        scale=-1.0))(),
                      reads=[bank[bg]], writes=[b_tA[sl]])
                    E("act", (lambda bg=bg, sv_t=sv_t: lambda h: h.activation(
                        out=sv_t, in_=ps[bg][:, 0:tt], func=AF.Sigmoid))(),
                      reads=[bank[bg]], writes=[b_tmp[2 + sl2]])
                    E("dve", (lambda ch=ch, s_t=s_t: lambda h: h.tensor_max(
                        out=tV_v(ch), in0=tV_v(ch), in1=s_t))(),
                      reads=[b_tmp[sl2], b_tV[sl]], writes=[b_tV[sl]])
                    E("pool", (lambda ch=ch, sv_t=sv_t: lambda h: h.tensor_mul(
                        out=tV_v(ch), in0=tV_v(ch), in1=sv_t))(),
                      reads=[b_tmp[2 + sl2], b_tV[sl]], writes=[b_tV[sl]])
                    E("dve", (lambda ch=ch: lambda h: h.tensor_tensor_scan(
                        out=tH_v(ch), data0=tA_v(ch), data1=tV_v(ch),
                        initial=hcarv[:, ch:ch + 1], op0=ALU.mult,
                        op1=ALU.add))(),
                      reads=[b_tA[sl], b_tV[sl], b_hcar[ch]], writes=[b_tH[sl]])
                    E("dve", (lambda ch=ch: lambda h: h.tensor_copy(
                        out=hcarv[:, ch:ch + 1], in_=tH_v(ch)[:, tt - 1:tt]))(),
                      reads=[b_tH[sl]], writes=[b_hcar[ch]])
                    E("dve", (lambda ch=ch: lambda h: h.tensor_tensor_scan(
                        out=tV_v(ch), data0=tA_v(ch), data1=zerosv[:, :],
                        initial=pcarv[:, ch:ch + 1], op0=ALU.mult,
                        op1=ALU.add))(),
                      reads=[b_tA[sl], b_zeros, b_pcar[ch], b_tV[sl]],
                      writes=[b_tV[sl]])
                    E("dve", (lambda ch=ch: lambda h: h.tensor_copy(
                        out=pcarv[:, ch:ch + 1], in_=tV_v(ch)[:, tt - 1:tt]))(),
                      reads=[b_tV[sl]], writes=[b_pcar[ch]])
                    E("dve", (lambda ch=ch, j=j: lambda h: h.tensor_add(
                        out=x_ch(ch, j), in0=x_ch(ch, j), in1=tH_v(ch)))(),
                      reads=[b_tH[sl], b_x[j]], writes=[b_x[j]])
                    E("pool", (lambda j=j, ch=ch: lambda h: h.dma_start(
                        out=pg_scr[j, ch, :, :], in_=tV_v(ch)))(),
                      reads=[b_tV[sl]], writes=[b_pg[j]], dma=True, sem="dq")
                    if ch == 3 and j + 1 < nt:
                        g_stats(j + 1)
                    if ch == 6 and j + 1 < nt:
                        g_apply(j + 1)
            if _s3 < 5:
                raise _EndPhase3()
            exchange_prep(lambda ch: pcarv[:, ch:ch + 1], list(b_pcar),
                          lambda ch: hcarv[:, ch:ch + 1], list(b_hcar))
            exchange_send("cc3", cc3)
            prefetch_corr(pg_scr, b_pg[0])
            build_gain(VR_FFG, ROW_FF32, 1.0, 32.0)
            exchange_recv("cc3", cc3)
            combine_carries()

            em.fence()

            def ff_epilogue(j, dc, psum_ap, bk):
                sl = dc % 2
                t = tmpv[:, sl * tt: (sl + 1) * tt]
                E("dve", (lambda psum_ap=psum_ap, dc=dc, t=t:
                          lambda h: h.tensor_scalar_add(
                              out=t, in0=psum_ap, scalar1=vcol(VR_FB2, dc)))(),
                  reads=[bk, b_vcols], writes=[b_tmp[sl]])
                E("dve", (lambda j=j, dc=dc, t=t: lambda h: h.tensor_add(
                    out=x_ch(dc, j), in0=x_ch(dc, j), in1=t))(),
                  reads=[b_tmp[sl], b_x[j]], writes=[b_x[j]])
                xo = corrv[:, sl * tt: (sl + 1) * tt]
                E("pool", (lambda j=j, dc=dc, xo=xo: lambda h: h.dma_start(
                    out=xo, in_=xold_scr[j, dc, :, :]))(),
                  reads=[b_xold[j]], writes=[b_corr[sl]], dma=True,
                  sem=f"dcp{sl}")
                E("dve", (lambda j=j, dc=dc, xo=xo: lambda h: h.tensor_add(
                    out=x_ch(dc, j), in0=x_ch(dc, j), in1=xo))(),
                  reads=[b_corr[sl], b_x[j]], writes=[b_x[j]])

            emit_mlp("rms", wimg_ff1, b_wimg["ff1"], wimg_ff2, b_wimg["ff2"],
                     lambda ec: vcol(VR_FB1, ec), ff_epilogue, False,
                     gain_row=ROW_FF32,
                     pre_tile=lambda j: corr_tile(pg_scr, j,
                                                   pref=(j == 0)))

        except _EndPhase3:
            pass
        if upto >= 4:
            emit_mlp("ln", wimg_s1, b_wimg["s1"], wimg_s2, b_wimg["s2"],
                     lambda ec: vcol(VR_NB1, ec), ffn_epilogue(True), True)
        else:
            for j in range(nt):
                for dc in range(DC):
                    E("pool", (lambda j=j, dc=dc: lambda h: h.dma_start(
                        out=out_o.rearrange("(c p) t -> p c t", p=128)
                        [:, dc, j * tt:(j + 1) * tt],
                        in_=x_ch(dc, j)))(),
                      reads=[b_x[j]], writes=[b_out], dma=True, sem="dout")

        for s_ in ("dout", "dq", "dqh", "di0", "di1", "dcb", "dcv"):
            em.raw_wait("sp", s_, em.cnt[s_])

        with nc.Block() as block:
            def mk(eng):
                def run(h):
                    for it in em.items[eng]:
                        if it[0] == "w":
                            h.wait_ge(sems[it[1]], it[2])
                        else:
                            inst = it[1](h)
                            if it[2] is not None:
                                inst.then_inc(sems[it[2]], it[3])
                return run
            block.tensor(mk("pe"))
            block.scalar(mk("act"))
            block.vector(mk("dve"))
            block.gpsimd(mk("pool"))
            block.sync(mk("sp"))

    return nc


_NC_CACHE = {}


def _get_nc(key):
    if key not in _NC_CACHE:
        _NC_CACHE[key] = build_nc(*key)
    return _NC_CACHE[key]


def make_inputs(full_inputs, tcore):
    ii = full_inputs
    Tloc = tcore * NCORES
    x = np.asarray(ii["L0_attn_a_x"], dtype=np.float32).reshape(-1, D)[:Tloc]

    vrows = np.zeros((16, 4096), np.float32)
    vrows[VR_BDW, :D] = np.asarray(ii["L0_attn_b_conv_dw_b"], np.float32)
    vrows[VR_BPW, :D] = np.asarray(ii["L0_attn_b_conv_pw_b"], np.float32)
    vrows[VR_FB2, :D] = np.asarray(ii["L0_attn_b_ff_b2"], np.float32)
    vrows[VR_NB2, :D] = np.asarray(ii["L0_ffn_b2"], np.float32)
    vrows[VR_FB1, :] = np.asarray(ii["L0_attn_b_ff_b1"], np.float32)
    vrows[VR_NB1, :] = np.asarray(ii["L0_ffn_b1"], np.float32)
    cdw = np.asarray(ii["L0_attn_b_conv_dw_w"], np.float32)[:, 0, :]
    vrows[VR_T0, :D] = cdw[:, 0]
    vrows[VR_T1, :D] = cdw[:, 1]
    vrows[VR_T2, :D] = cdw[:, 2]
    vrows[VR_GAMMA, :D] = np.asarray(ii["L0_attn_a_gamma"], np.float32)
    vrows[VR_CONVG, :D] = np.asarray(ii["L0_attn_b_conv_norm_g"], np.float32)
    vrows[VR_GRUG, :D] = np.asarray(ii["L0_attn_b_gru_norm_g"], np.float32)
    vrows[VR_FFG, :D] = np.asarray(ii["L0_attn_b_ff_norm_g"], np.float32)

    wq_packed = pack_lhsT(np.asarray(ii["L0_attn_a_w_qkva"], np.float32))
    wq4 = wq_packed.reshape(24, 128, 1024)
    order = []
    for ch in range(8):
        order += [8 + ch, 16 + ch, ch]
    wq_tripped = np.ascontiguousarray(wq4[order].reshape(24 * 128, 1024))
    gru_packed = pack_lhsT(np.asarray(ii["L0_attn_b_gru_w"], np.float32))
    g4 = gru_packed.reshape(16, 128, 1024)
    gorder = []
    for ch in range(8):
        gorder += [ch, 8 + ch]
    gru_paired = np.ascontiguousarray(g4[gorder].reshape(16 * 128, 1024))
    packs = dict(
        wq_p=wq_tripped,
        pw_p=pack_lhsT(np.asarray(ii["L0_attn_b_conv_pw_w"], np.float32)),
        gru_p=gru_paired,
        ff1_p=pack_lhsT(np.asarray(ii["L0_attn_b_ff_w1"], np.float32)),
        ff2_p=pack_lhsT(np.asarray(ii["L0_attn_b_ff_w2"], np.float32)),
        n1_p=pack_lhsT(np.asarray(ii["L0_ffn_w1"], np.float32)),
        n2_p=pack_lhsT(np.asarray(ii["L0_ffn_w2"], np.float32)),
        n1_nat=np.ascontiguousarray(np.asarray(ii["L0_ffn_w1"], np.float32)),
        n2_nat=np.ascontiguousarray(np.asarray(ii["L0_ffn_w2"], np.float32)),
        vrows_in=vrows,
        ident_in=np.eye(128, dtype=np.float32),
    )

    in_maps = []
    for c in range(NCORES):
        mprev = np.zeros((128, 8), np.float32)
        if c > 0:
            mprev[:, c - 1] = 1.0
        smask = np.zeros((16, 1), np.float32)
        smask[2 * c: 2 * c + 2] = 1.0
        smr = np.zeros((1, 32), np.float32)
        smr[0, 2 * c] = 1.0
        smr[0, 16 + 2 * c + 1] = 1.0
        xs = np.ascontiguousarray(x[c * tcore:(c + 1) * tcore, :].T)
        m = dict(packs)
        m.update(x_in=xs, mprev_in=mprev, smask_in=smask, smr_in=smr)
        in_maps.append(m)
    return in_maps


def kernel(**inputs):
    tcore = T // NCORES
    in_maps = make_inputs(inputs, tcore)
    nc = _get_nc((tcore, 512, 4, 0))
    res = run_bass_kernel_spmd(nc, in_maps, core_ids=list(range(NCORES)))
    out = np.empty((T, D), np.float32)
    for c in range(NCORES):
        out[c * tcore:(c + 1) * tcore, :] = res.results[c]["out_o"].T
    return out.reshape(1, T, D)


# version 23
# speedup vs baseline: 1.1004x; 1.0011x over previous
import numpy as np
import concourse.bass as bass
import concourse.mybir as mybir
from concourse.bass_utils import run_bass_kernel_spmd

F32 = mybir.dt.float32
BF16 = mybir.dt.bfloat16
AF = mybir.ActivationFunctionType
ALU = mybir.AluOpType
AX = mybir.AxisListType

D = 1024
DC = 8
NCORES = 8
T = 16384

ENGINES = ("pe", "act", "dve", "pool", "sp")
SEMS = ("pe", "act", "dve", "pool", "cc",
        "dx", "dq", "dqh", "dcb", "dout", "dcv", "dcq", "dpf",
        "dcq0", "dcq1", "dcq2", "dcq3", "dcq4", "dcq5", "dcq6", "dcq7",
        "dw0", "dw1",
        "ds0", "ds1", "ds2", "ds3", "ds4", "ds5",
        "di0", "di1", "di2", "di3", "dcr0", "dcr1",
        "dcp0", "dcp1")


class Buf:
    __slots__ = ("w", "r", "name")

    def __init__(self, name=""):
        self.w = {}
        self.r = {}
        self.name = name


class Em:
    def __init__(self):
        self.items = {e: [] for e in ENGINES}
        self.cnt = {s: 0 for s in SEMS}
        self.waited = {e: {} for e in ENGINES}

    def _wait(self, eng, sem, val):
        if val <= 0:
            return
        assert val <= self.cnt[sem], (eng, sem, val, self.cnt[sem])
        if self.waited[eng].get(sem, 0) >= val:
            return
        self.waited[eng][sem] = val
        self.items[eng].append(("w", sem, val))

    TOTAL_SEMS = ("dx", "dq", "dqh", "dcb", "dout", "dcv", "dcq", "dpf")

    def _wait_edge(self, eng, sem, val):
        if sem in self.TOTAL_SEMS:
            val = self.cnt[sem]
        self._wait(eng, sem, val)

    def op(self, eng, fn, reads=(), writes=(), dma=False, inc=True, sem=None,
           incval=None):
        assert sem is not None or not dma, "dma ops must name a channel sem"
        if sem is None:
            sem = eng
        if incval is None:
            incval = 16 if dma else 1
        for b in reads:
            for s, c in b.w.items():
                self._wait_edge(eng, s, c)
        for b in writes:
            for s, c in b.w.items():
                self._wait_edge(eng, s, c)
            for s, c in b.r.items():
                self._wait_edge(eng, s, c)
        if inc:
            self.cnt[sem] += incval
            self.items[eng].append(("c", fn, sem, incval))
            cur = (sem, self.cnt[sem])
            for b in writes:
                if b.name.startswith("w_img"):
                    b.w[sem] = self.cnt[sem]
                else:
                    b.w = {sem: self.cnt[sem]}
                b.r = {}
            for b in reads:
                b.r[sem] = cur[1]
            return cur
        self.items[eng].append(("c", fn, None, 0))
        return None

    def fence(self):
        for eng in ENGINES:
            for sem in SEMS:
                self._wait(eng, sem, self.cnt[sem])

    def raw_wait(self, eng, sem, val):
        self._wait(eng, sem, val)


def pack_lhsT(w):
    E, K = w.shape
    w4 = w.reshape(E // 128, 128, K // 128, 128).transpose(0, 3, 2, 1)
    return np.ascontiguousarray(w4.reshape(E, K))


VR_BDW, VR_BPW, VR_FB2, VR_NB2, VR_FB1, VR_NB1 = 0, 1, 2, 3, 4, 5
VR_T0, VR_T1, VR_T2 = 6, 7, 8
VR_GAMMA, VR_CONVG, VR_GRUG, VR_FFG = 9, 10, 11, 12


SUB3 = 9


def build_nc(tcore=2048, tt=512, upto=4, debug=0):
    nt = tcore // tt
    nc = bass.Bass("TRN2", target_bir_lowering=False, num_devices=8)

    x_in = nc.dram_tensor("x_in", [D, tcore], F32, kind="ExternalInput").ap()
    out_o = nc.dram_tensor("out_o", [D, tcore], F32, kind="ExternalOutput").ap()
    wq_p = nc.dram_tensor("wq_p", [3 * D, D], F32, kind="ExternalInput").ap()
    pw_p = nc.dram_tensor("pw_p", [D, D], F32, kind="ExternalInput").ap()
    gru_p = nc.dram_tensor("gru_p", [2 * D, D], F32, kind="ExternalInput").ap()
    ff1_p = nc.dram_tensor("ff1_p", [4 * D, D], F32, kind="ExternalInput").ap()
    ff2_p = nc.dram_tensor("ff2_p", [D, 4 * D], F32, kind="ExternalInput").ap()
    n1_p = nc.dram_tensor("n1_p", [4 * D, D], F32, kind="ExternalInput").ap()
    n2_p = nc.dram_tensor("n2_p", [D, 4 * D], F32, kind="ExternalInput").ap()
    n1_nat = nc.dram_tensor("n1_nat", [4 * D, D], F32, kind="ExternalInput").ap()
    n2_nat = nc.dram_tensor("n2_nat", [D, 4 * D], F32, kind="ExternalInput").ap()
    vrows_in = nc.dram_tensor("vrows_in", [16, 4096], F32, kind="ExternalInput").ap()
    ident_in = nc.dram_tensor("ident_in", [128, 128], F32, kind="ExternalInput").ap()
    mprev_in = nc.dram_tensor("mprev_in", [128, 8], F32, kind="ExternalInput").ap()
    smask_in = nc.dram_tensor("smask_in", [16, 1], F32, kind="ExternalInput").ap()
    smr_in = nc.dram_tensor("smr_in", [1, 32], F32, kind="ExternalInput").ap()

    wimg_q = nc.dram_tensor("wimg_q", [3 * D, D], BF16).ap()
    wimg_pw = nc.dram_tensor("wimg_pw", [D, D], BF16).ap()
    wimg_gru = nc.dram_tensor("wimg_gru", [2 * D, D], BF16).ap()
    wimg_ff1 = nc.dram_tensor("wimg_ff1", [4 * D, D], BF16).ap()
    wimg_ff2 = nc.dram_tensor("wimg_ff2", [D, 4 * D], BF16).ap()
    wimg_s1 = nc.dram_tensor("wimg_s1", [4 * D, D], BF16).ap()
    wimg_s2 = nc.dram_tensor("wimg_s2", [D, 4 * D], BF16).ap()
    qp_scr = nc.dram_tensor("qp_scr", [nt, DC, 128, tt], F32).ap()
    pg_scr = nc.dram_tensor("pg_scr", [nt, DC, 128, tt], F32).ap()
    xold_scr = nc.dram_tensor("xold_scr", [nt, DC, 128, tt], F32).ap()
    cc1 = nc.dram_tensor("cc1", [16, D], F32).ap()
    cc2 = nc.dram_tensor("cc2", [16, D], F32).ap()
    cc3 = nc.dram_tensor("cc3", [16, D], F32).ap()
    scr2a = nc.dram_tensor("scr2a", [1, 2 * D], F32).ap()
    scr2b = nc.dram_tensor("scr2b", [1, 2 * D], F32).ap()
    scr2c = nc.dram_tensor("scr2c", [1, 2 * D], F32).ap()

    from contextlib import ExitStack
    es = ExitStack()
    with es:
        def sb(name, shape, dtype=F32):
            return es.enter_context(nc.sbuf_tensor(name, shape, dtype))

        x_sb = sb("x_sb", [128, DC * tcore])
        U = sb("U", [128, 6688])
        stg = sb("stg", [128, 2 * 1024])
        cv16 = sb("cv16", [128, 4 * 1024], BF16)
        wstr = sb("wstr", [128, 2 * 4096], BF16)
        xnb = sb("xnb", [128, DC * tt], BF16)
        hb16 = sb("hb16", [128, 16384], BF16)
        sqb = sb("sqb", [128, 2 * tt])
        tmp5 = sb("tmp5", [128, 4 * tt])
        corr = sb("corr", [128, 2 * tt])
        vcols = sb("vcols", [128, 512])
        Rg = sb("Rg", [128, 2176])
        Rs = sb("Rs", [128, 3072])
        ident = sb("ident", [128, 128])
        ones_c = sb("ones_c", [128, 1])
        ones_h = sb("ones_h", [128, 1], BF16)
        hcar = sb("hcar", [128, 8])
        pcar = sb("pcar", [128, 8])
        hsel = sb("hsel", [128, 8])
        mprev = sb("mprev", [128, 8])
        smask = sb("smask", [16, 1])
        smr = sb("smr", [1, 32])
        G = sb("G", [128, 128])
        ccst = sb("ccst", [16, D])
        alpha1 = sb("alpha1", [128, 32])
        alpha2 = sb("alpha2", [128, 8])
        uh2 = sb("uh2", [128, 16])
        epsc = sb("epsc", [128, 4])

        ps = [es.enter_context(nc.psum_tensor(f"ps{i}", [128, 512], F32))
              for i in range(8)]
        sems = {s: es.enter_context(nc.semaphore(s)) for s in SEMS}

        em = Em()

        bank = [Buf(f"bank{i}") for i in range(8)]
        b_x = [Buf(f"x{j}") for j in range(nt)]
        b_tA = [Buf(f"tA{i}") for i in range(4)]
        b_tV = [Buf(f"tV{i}") for i in range(4)]
        b_tQ = [Buf(f"tQ{i}") for i in range(4)]
        b_tH = [Buf(f"tH{i}") for i in range(4)]
        b_tq = [Buf(f"tq{i}") for i in range(4)]
        b_fs = [Buf(f"fs{i}") for i in range(6)]
        b_cv16 = [Buf(f"cv16{i}") for i in range(4)]
        b_wstr = [Buf("wstr0"), Buf("wstr1")]
        b_xnb = [Buf(f"xnb{c}") for c in range(DC)]
        b_xnbh = [Buf(f"xnbh{c}") for c in range(DC)]
        b_hbf = [Buf(f"hbf{c}") for c in range(32)]
        b_sqb = [Buf("sqb0"), Buf("sqb1")]
        b_tmp = [Buf(f"tmp{i}") for i in range(4)]
        b_corr = [Buf("corr0"), Buf("corr1")]
        b_uext = [Buf(f"uext{c}") for c in range(DC)]
        b_ycv = Buf("ycv")
        b_ybf = [Buf(f"ybf{c}") for c in range(DC)]
        b_vrows = Buf("vrows")
        b_vcols = Buf("vcols")
        b_Rg = Buf("Rg")
        b_ident = Buf("ident")
        b_ones = Buf("ones")
        b_zeros = Buf("zeros")
        b_hcar = [Buf(f"hcar{c}") for c in range(DC)]
        b_pcar = [Buf(f"pcar{c}") for c in range(DC)]
        b_hsel = Buf("hsel")
        b_mprev = Buf("mprev")
        b_smask = Buf("smask")
        b_smr = Buf("smr")
        b_G = Buf("G")
        b_ccst = Buf("ccst")
        b_rows2 = Buf("rows2")
        b_alpha1 = Buf("alpha1")
        b_alpha2 = Buf("alpha2")
        b_uh2 = Buf("uh2")
        b_epsc = Buf("epsc")
        b_wimg = {k: Buf("w_img_" + k) for k in
                  ("q", "pw", "gru", "ff1", "ff2", "s1", "s2")}
        b_wq_ch = [Buf(f"wq_ch{c}") for c in range(DC)]
        b_qp = [Buf(f"qp{j}") for j in range(nt)]
        b_pg = [Buf(f"pg{j}") for j in range(nt)]
        b_xold = [Buf(f"xold{j}") for j in range(nt)]
        b_cc = {"cc1": Buf("cc1"), "cc2": Buf("cc2"), "cc3": Buf("cc3")}
        b_scr2 = {"cc1": Buf("s2a"), "cc2": Buf("s2b"), "cc3": Buf("s2c")}
        b_out = Buf("out")
        RSB = {}

        xv = x_sb.ap()

        def x_ch(ch, j, n=None):
            n = tt if n is None else n
            return xv[:, ch * tcore + j * tt: ch * tcore + j * tt + n]

        def x_cols(ch, c0, n):
            return xv[:, ch * tcore + c0: ch * tcore + c0 + n]

        Uv = U.ap()
        vrowsv = Uv[0:16, 0:4096]
        UE = DC * (tt + 4)
        R2OFF = max(8 * tt, UE + tt)

        def tsl(base, hoff, ch, n=tt):
            s = ch % 4
            if s < 2:
                return Uv[:, base + s * tt: base + s * tt + n]
            return hbfc[:, hoff + (s - 2) * tt: hoff + (s - 2) * tt + n]

        def tA_v(ch, n=tt):
            return tsl(0, 4 * tt, ch, n)

        def tV_v(ch, n=tt):
            return tsl(2 * tt, 6 * tt, ch, n)

        def tQ_v(ch, n=tt):
            return tsl(4 * tt, 8 * tt, ch, n)

        def tH_v(ch, n=tt):
            return tsl(6 * tt, 10 * tt, ch, n)

        def tq_v(ch, n=tt):
            return hbfc[:, 12 * tt + (ch % 4) * tt: 12 * tt + (ch % 4) * tt + n]

        def u_ch(ch, c0, n):
            return Uv[:, ch * (tt + 4) + c0: ch * (tt + 4) + c0 + n]

        ycvv = Uv[:, UE:UE + tt]
        rows2v = Uv[0:1, R2OFF:R2OFF + 2 * D]
        zerosv = Uv[:, R2OFF:R2OFF + tt]
        stg2v = Uv[:, R2OFF + 2 * D - 1024: R2OFF + 2 * D]

        stgv = stg.ap()
        cv16v = cv16.ap()
        stgb = stg.ap().bitcast(BF16)
        Ub = U.ap().bitcast(BF16)
        FS_OFF = [0, 1024, 2048, 3072, 2 * 5152, 2 * 5152 + 1024]

        def fs_v(slot, n):
            base = stgb if slot < 4 else Ub
            return base[:, FS_OFF[slot]: FS_OFF[slot] + n]
        hbfc = hb16.ap().bitcast(F32)
        wstrv = wstr.ap()
        xnbv = xnb.ap()
        hbfv = hb16.ap()

        def hbf_v(ec):
            return hbfv[:, ec * tt: (ec + 1) * tt]

        def ybf_v(ch):
            return hbfv[:, ch * tt: (ch + 1) * tt]

        sqv = sqb.ap()
        tmpv = tmp5.ap()
        corrv = corr.ap()
        vcolsv = vcols.ap()
        Rgv = Rg.ap()
        Rsv = Rs.ap()
        identv = ident.ap()
        onesv = ones_c.ap()
        oneshv = ones_h.ap()
        sqhv = sqb.ap().bitcast(BF16)
        hcarv, pcarv, hselv = hcar.ap(), pcar.ap(), hsel.ap()
        mprevv, smaskv = mprev.ap(), smask.ap()
        smrv = smr.ap()
        Gv, ccstv = G.ap(), ccst.ap()
        a1v, a2v = alpha1.ap(), alpha2.ap()
        uh2v = uh2.ap()
        epscv = epsc.ap()
        EPS_COL = {1e-30: 0, 1e-5: 1}

        ROW_GAMMA = (0, 0)
        ROW_CONVG = (0, 1024)
        ROW_GRU32 = (0, 0)
        ROW_FF32 = (0, 1024)

        def gain_slice(row, ch):
            base, col = row
            return Rgv[base:base + 1, col + ch * 128: col + (ch + 1) * 128]

        def ones_row():
            return Rgv[0:1, 2048:2048 + 128]

        RS = {"m": (0, 0), "vtmp": (0, 512),
              "invE": (0, 1024), "minvE": (0, 1536), "inv0": (0, 2048),
              "minv0": (0, 2560), "norm": (0, 512), "r": (0, 0)}
        for k in RS:
            RSB[k] = Buf("rs_" + k)
        RSB["norm"] = RSB["vtmp"]
        RSB["r"] = RSB["m"]

        def rs(name, n=512):
            base, col = RS[name]
            return Rsv[base:base + 1, col: col + n]

        def E(eng, fn, reads=(), writes=(), dma=False, inc=True, sem=None,
              incval=None):
            return em.op(eng, fn, reads=reads, writes=writes, dma=dma, inc=inc,
                         sem=sem, incval=incval)

        def matmul_group2(items, bank_buf):
            nit = len(items)
            allreads = []
            for it in items:
                for b in it[3]:
                    if b not in allreads:
                        allreads.append(b)
            for i, (o, l, r, rd) in enumerate(items):
                st, sp = (i == 0), (i == nit - 1)
                fn = (lambda o=o, l=l, r=r, st=st, sp=sp:
                      lambda h: h.matmul(o, l, r, start=st, stop=sp))()
                if i == nit - 1:
                    E("pe", fn, reads=allreads, writes=[bank_buf], inc=True)
                else:
                    for b in rd:
                        for s_, c_ in b.w.items():
                            em._wait_edge("pe", s_, c_)
                    if i == 0:
                        for s_, c_ in bank_buf.w.items():
                            em._wait_edge("pe", s_, c_)
                        for s_, c_ in bank_buf.r.items():
                            em._wait_edge("pe", s_, c_)
                    em.items["pe"].append(("c", fn, None, 0))

        E("sp", lambda h: h.dma_start(out=ident[:], in_=ident_in[:, :]),
          writes=[b_ident], dma=True, sem="dx")
        E("sp", lambda h: h.dma_start(out=vrowsv, in_=vrows_in[:, :]),
          writes=[b_vrows], dma=True, sem="dx")
        E("sp", lambda h: h.dma_start(out=mprev[:], in_=mprev_in[:, :]),
          writes=[b_mprev], dma=True, sem="dx")
        E("sp", lambda h: h.dma_start(out=smask[:], in_=smask_in[:, :]),
          writes=[b_smask], dma=True, sem="dx")
        E("sp", lambda h: h.dma_start(out=smr[:], in_=smr_in[:, :]),
          writes=[b_smr], dma=True, sem="dx")
        for ch in range(DC):
            E("sp", (lambda ch=ch: lambda h: h.dma_start(
                out=xv[:, ch * tcore:(ch + 1) * tcore],
                in_=x_in[ch * 128:(ch + 1) * 128, :]))(),
              writes=[b_x[j] for j in range(nt)], dma=True, sem="dx")
        E("dve", lambda h: h.memset(ones_c[:], 1.0), writes=[b_ones])
        E("dve", lambda h: h.memset(ones_h[:], 1.0), writes=[b_ones])
        E("dve", lambda h: h.memset(epscv[:, 0:1], 1e-30), writes=[b_epsc])
        E("dve", lambda h: h.memset(epscv[:, 1:2], 1e-5), writes=[b_epsc])
        E("dve", lambda h: h.memset(epscv[:, 2:3], -0.5), writes=[b_epsc])
        E("dve", lambda h: h.memset(Rg[:], 0.0), writes=[b_Rg])

        for s in range(32):
            E("pe", (lambda s=s: lambda h: h.transpose(
                ps[6][:, 0:16], vrowsv[0:16, s * 128:(s + 1) * 128],
                identv[0:16, 0:16]))(),
              reads=[b_vrows, b_ident], writes=[bank[6]])
            E("dve", (lambda s=s: lambda h: h.tensor_copy(
                out=vcolsv[:, s * 16:(s + 1) * 16], in_=ps[6][:, 0:16]))(),
              reads=[bank[6]], writes=[b_vcols])

        def vcol(r, ec):
            return vcolsv[:, ec * 16 + r: ec * 16 + r + 1]

        def build_gain(vr, row, addv, mulv):
            for ch in range(DC):
                E("pe", (lambda ch=ch, vr=vr: lambda h: h.transpose(
                    ps[7][0:1, 0:128], vcol(vr, ch), identv[:, :]))(),
                  reads=[b_vcols, b_ident], writes=[bank[7]])
                base, col = row
                dst = Rgv[base:base + 1, col + ch * 128: col + (ch + 1) * 128]
                if addv == 0.0 and mulv == 1.0:
                    E("dve", (lambda dst=dst: lambda h: h.tensor_copy(
                        out=dst, in_=ps[7][0:1, 0:128]))(),
                      reads=[bank[7]], writes=[b_Rg])
                else:
                    E("dve", (lambda dst=dst, addv=addv, mulv=mulv:
                              lambda h: h.tensor_scalar(
                                  out=dst, in0=ps[7][0:1, 0:128], scalar1=addv,
                                  scalar2=mulv, op0=ALU.add, op1=ALU.mult))(),
                      reads=[bank[7]], writes=[b_Rg])

        build_gain(VR_GAMMA, ROW_GAMMA, 0.0, 1.0)
        build_gain(VR_CONVG, ROW_CONVG, 1.0, 1.0)
        _ones_ap = Rgv[0:1, 2048:2048 + 128]
        E("dve", (lambda ap=_ones_ap: lambda h: h.memset(ap, 1.0))(),
          writes=[b_Rg])

        fslot = [0]
        cvi_slot = [0]

        def sign_unit(src_ap, dst_ap, rows0, ncols, img_buf, col0=0):
            st = {}

            def load():
                slot = fslot[0]
                fslot[0] = (fslot[0] + 1) % 6
                st["slot"] = slot
                st["sgv"] = fs_v(slot, ncols)
                E("pool", (lambda sgv=st["sgv"]: lambda h: h.dma_start(
                    out=sgv, in_=src_ap[rows0:rows0 + 128,
                                        col0:col0 + ncols]))(),
                  writes=[b_fs[slot]], dma=True, sem=f"ds{slot}")

            def compute():
                slot, sgv = st["slot"], st["sgv"]
                cslot = cvi_slot[0]
                cvi_slot[0] = (cvi_slot[0] + 1) % 4
                cg = cv16v[:, cslot * 1024: cslot * 1024 + ncols]
                E("act", (lambda cg=cg, sgv=sgv: lambda h: h.activation(
                    out=cg, in_=sgv, func=AF.Sign))(),
                  reads=[b_fs[slot]], writes=[b_cv16[cslot]])

                def write():
                    E("sp", (lambda cg=cg: lambda h: h.dma_start(
                        out=dst_ap[rows0:rows0 + 128, col0:col0 + ncols],
                        in_=cg))(),
                      reads=[b_cv16[cslot]], writes=[img_buf], dma=True,
                      sem=f"di{cslot}")
                return write
            return (load, compute)

        def alpha_unit(nat_ap, rows0, col0, ncols, dst_col, acc, bufa):
            st = {}

            def load():
                slot = fslot[0]
                fslot[0] = (fslot[0] + 1) % 6
                st["slot"] = slot
                st["sgv"] = fs_v(slot, ncols)
                E("pool", (lambda sgv=st["sgv"]: lambda h: h.dma_start(
                    out=sgv, in_=nat_ap[rows0:rows0 + 128,
                                        col0:col0 + ncols]))(),
                  writes=[b_fs[slot]], dma=True, sem=f"ds{slot}")

            def compute():
                slot, sgv = st["slot"], st["sgv"]
                if not acc:
                    E("act", (lambda sgv=sgv: lambda h: h.activation(
                        out=sgv, in_=sgv, func=AF.Abs, accum_out=dst_col))(),
                      reads=[b_fs[slot]], writes=[b_fs[slot], bufa])
                else:
                    t = tmpv[:, 3 * tt: 3 * tt + 1]
                    E("act", (lambda sgv=sgv, t=t: lambda h: h.activation(
                        out=sgv, in_=sgv, func=AF.Abs, accum_out=t))(),
                      reads=[b_fs[slot]], writes=[b_fs[slot], b_tmp[3]])
                    E("dve", (lambda t=t: lambda h: h.tensor_add(
                        out=dst_col, in0=dst_col, in1=t))(),
                      reads=[b_tmp[3], bufa], writes=[bufa])
                return None
            return (load, compute)

        def cast_img(src_ap, dst_ap, img_buf, sem="dcv", rows=None):
            if rows is None:
                sl_ = (slice(None), slice(None))
            else:
                sl_ = (slice(rows[0], rows[1]), slice(None))
            E("pool", (lambda sl_=sl_: lambda h: h.dma_start(
                out=dst_ap[sl_], in_=src_ap[sl_]))(),
              writes=[img_buf], dma=True, sem=sem)

        def cast_pieces(src_ap, dst_ap, img_buf, piece_rows):
            R = src_ap.shape[0]
            out = []
            for r0 in range(0, R, piece_rows):
                r1 = min(r0 + piece_rows, R)
                out.append((lambda r0=r0, r1=r1: cast_img(
                    src_ap, dst_ap, img_buf, rows=(r0, r1))))
            return out

        class Feeder:

            def __init__(self, units):
                self.units = list(units)
                self.li = 0
                self.ci = 0
                self.pending = []

            def take(self, k):
                if self.ci >= len(self.units):
                    while self.pending:
                        self.pending.pop(0)()
                else:
                    while len(self.pending) > 1:
                        self.pending.pop(0)()
                lim = self.li
                self.li = min(self.li + k, len(self.units))
                for u in self.units[lim:self.li]:
                    u[0]()
                for u in self.units[self.ci:lim]:
                    w = u[1]()
                    if w is not None:
                        self.pending.append(w)
                self.ci = lim

            def drain(self):
                while self.ci < len(self.units):
                    self.take(3)
                for w in self.pending:
                    w()
                self.pending = []

        for _c in (0, 1):
            cast_img(wq_p, wimg_q, b_wq_ch[_c], sem=f"dcq{_c}",
                     rows=(_c * 384, (_c + 1) * 384))
        _signs = ([sign_unit(n1_p, wimg_s1, ec * 128, D, b_wimg["s1"])
                   for ec in range(32)]
                  + [sign_unit(n2_p, wimg_s2, ec * 128, 1024, b_wimg["s2"],
                               col0=hh * 1024)
                     for ec in range(8) for hh in range(4)])
        _alphas = ([alpha_unit(n1_nat, ec * 128, 0, D, a1v[:, ec:ec + 1],
                               False, b_alpha1) for ec in range(32)]
                   + [alpha_unit(n2_nat, ec * 128, p * 1024, 1024,
                                 a2v[:, ec:ec + 1], p > 0, b_alpha2)
                      for ec in range(8) for p in range(4)])
        _s1signs = _signs[:32]
        _s2signs = _signs[32:]
        units_s = []
        for i in range(32):
            units_s.append(_s1signs[i])
            units_s.append(_alphas[2 * i])
            units_s.append(_alphas[2 * i + 1])
        units_b = _s2signs

        BK_S1, BK_S2 = 4, 5
        wl_slot = [0]

        def wload(img_ap, img_buf, rows0, k, col0=0, nrow=128):
            slot = wl_slot[0]
            wl_slot[0] ^= 1
            dst = wstrv[:, slot * 4096: slot * 4096 + k]
            if nrow != 128:
                nc_ = nrow // 128
                ima = img_ap[rows0:rows0 + nrow, col0:col0 + k // nc_]
                ima = ima.rearrange("(c p) k -> p c k", p=128)
                dst3 = dst.rearrange("p (c k) -> p c k", c=nc_)
                E("sp", (lambda dst3=dst3, ima=ima: lambda h: h.dma_start(
                    out=dst3, in_=ima))(),
                  reads=[img_buf], writes=[b_wstr[slot]], dma=True,
                  sem=f"dw{slot}")
                return dst, b_wstr[slot]
            E("sp", (lambda dst=dst: lambda h: h.dma_start(
                out=dst, in_=img_ap[rows0:rows0 + 128, col0:col0 + k]))(),
              reads=[img_buf], writes=[b_wstr[slot]], dma=True,
              sem=f"dw{slot}")
            return dst, b_wstr[slot]

        def site_stats(n, src_fn, src_bufs, want_ln, eps_list):
            if want_ln:
                items = [(ps[BK_S1][0:1, 0:n], onesv[:, :], src_fn(ch),
                          [b_ones] + src_bufs) for ch in range(DC)]
                matmul_group2(items, bank[BK_S1])
            for ch in range(DC):
                slot = ch % 2
                sq = sqhv[:, slot * tt: slot * tt + n]
                E("act", (lambda sq=sq, ch=ch: lambda h: h.activation(
                    out=sq, in_=src_fn(ch), func=AF.Square))(),
                  reads=src_bufs, writes=[b_sqb[slot]])
                fn = (lambda sq=sq, ch=ch, n=n: lambda h: h.matmul(
                    ps[BK_S2][0:1, 0:n], oneshv[:, :], sq,
                    start=(ch == 0), stop=(ch == DC - 1)))()
                E("pe", fn, reads=[b_ones, b_sqb[slot]],
                  writes=[bank[BK_S2]], inc=True)
            if want_ln:
                E("dve", (lambda n=n: lambda h: h.tensor_scalar_mul(
                    out=rs("m", n), in0=ps[BK_S1][0:1, 0:n], scalar1=1.0 / D))(),
                  reads=[bank[BK_S1]], writes=[RSB["m"]])
                E("pool", (lambda n=n: lambda h: h.tensor_mul(
                    out=rs("vtmp", n), in0=rs("m", n), in1=rs("m", n)))(),
                  reads=[RSB["m"]], writes=[RSB["vtmp"]])
                E("dve", (lambda n=n: lambda h: h.scalar_tensor_tensor(
                    out=rs("vtmp", n), in0=ps[BK_S2][0:1, 0:n],
                    scalar=1.0 / D, in1=rs("vtmp", n),
                    op0=ALU.mult, op1=ALU.subtract))(),
                  reads=[bank[BK_S2], RSB["vtmp"]], writes=[RSB["vtmp"]])
                for (inv_name, minv_name, eps) in eps_list:
                    ec_ = EPS_COL[eps]
                    E("act", (lambda n=n, ec_=ec_, minv_name=minv_name:
                              lambda h: h.activation(
                                  out=rs(minv_name, n), in_=rs("vtmp", n),
                                  func=AF.Ln, bias=epscv[0:1, ec_:ec_ + 1]))(),
                      reads=[RSB["vtmp"], b_epsc], writes=[RSB[minv_name]])
                    E("act", (lambda n=n, inv_name=inv_name,
                              minv_name=minv_name: lambda h: h.activation(
                                  out=rs(inv_name, n), in_=rs(minv_name, n),
                                  func=AF.Exp, scale=epscv[0:1, 2:3]))(),
                      reads=[RSB[minv_name], b_epsc], writes=[RSB[inv_name]])
                    E("pool", (lambda n=n, inv_name=inv_name,
                               minv_name=minv_name: lambda h: h.tensor_mul(
                                   out=rs(minv_name, n), in0=rs("m", n),
                                   in1=rs(inv_name, n)))(),
                      reads=[RSB["m"], RSB[inv_name]], writes=[RSB[minv_name]])
            else:
                E("dve", (lambda n=n: lambda h: h.tensor_scalar_max(
                    out=rs("norm", n), in0=ps[BK_S2][0:1, 0:n],
                    scalar1=1e-24))(),
                  reads=[bank[BK_S2]], writes=[RSB["norm"]])
                E("act", (lambda n=n: lambda h: h.activation(
                    out=rs("minvE", n), in_=rs("norm", n), func=AF.Ln))(),
                  reads=[RSB["norm"]], writes=[RSB["minvE"]])
                E("act", (lambda n=n: lambda h: h.activation(
                    out=rs("r", n), in_=rs("minvE", n), func=AF.Exp,
                    scale=epscv[0:1, 2:3]))(),
                  reads=[RSB["minvE"], b_epsc], writes=[RSB["r"]])

        def apply_ln(n, src_fn, src_bufs, gain_row, inv_name, minv_name,
                     out_fn, out_bufs, bb=6, bc=7, tmp_i=0):
            for ch in range(DC):
                p2 = 2 * (ch % 2)
                bbx, bcx = 4 + p2, 5 + p2
                g = gain_slice(gain_row, ch) if gain_row is not None else ones_row()
                E("pe", (lambda g=g, n=n, inv_name=inv_name, bbx=bbx:
                         lambda h: h.matmul(ps[bbx][:, 0:n], g, rs(inv_name, n),
                                            start=True, stop=True))(),
                  reads=[b_Rg, RSB[inv_name]], writes=[bank[bbx]])
                E("pe", (lambda g=g, n=n, minv_name=minv_name, bcx=bcx:
                         lambda h: h.matmul(ps[bcx][:, 0:n], g, rs(minv_name, n),
                                            start=True, stop=True))(),
                  reads=[b_Rg, RSB[minv_name]], writes=[bank[bcx]])
                t = tmpv[:, (tmp_i + ch % 2) * tt: (tmp_i + ch % 2) * tt + n]
                E("dve", (lambda t=t, ch=ch, n=n, bbx=bbx: lambda h: h.tensor_mul(
                    out=t, in0=src_fn(ch), in1=ps[bbx][:, 0:n]))(),
                  reads=src_bufs + [bank[bbx]], writes=[b_tmp[tmp_i + ch % 2]])
                E("dve", (lambda t=t, ch=ch, n=n, bcx=bcx: lambda h: h.tensor_sub(
                    out=out_fn(ch), in0=t, in1=ps[bcx][:, 0:n]))(),
                  reads=[b_tmp[tmp_i + ch % 2], bank[bcx]], writes=[out_bufs[ch]])

        def apply_rms(n, src_fn, src_bufs, gain_row, out_fn, out_bufs, bb=6):
            for ch in range(DC):
                bbx = 4 + (ch % 4)
                g = gain_slice(gain_row, ch)
                E("pe", (lambda g=g, n=n, bbx=bbx: lambda h: h.matmul(
                    ps[bbx][:, 0:n], g, rs("r", n), start=True, stop=True))(),
                  reads=[b_Rg, RSB["r"]], writes=[bank[bbx]])
                E("dve", (lambda ch=ch, n=n, bbx=bbx: lambda h: h.tensor_mul(
                    out=out_fn(ch), in0=src_fn(ch), in1=ps[bbx][:, 0:n]))(),
                  reads=src_bufs + [bank[bbx]], writes=[out_bufs[ch]])

        def exchange_prep(colA_fn, colA_bufs, colB_fn, colB_bufs):
            al_w = [b_rows2, b_zeros, b_fs[4], b_fs[5]]
            for ch in range(DC):
                bk_ = 6 if ch < 4 else 5
                co = (ch % 4) * 128
                E("pe", (lambda ch=ch: lambda h: h.transpose(
                    ps[4][0:1, 0:128], colA_fn(ch), identv[:, :]))(),
                  reads=colA_bufs + [b_ident], writes=[bank[4]])
                E("dve", (lambda ch=ch: lambda h: h.tensor_copy(
                    out=rows2v[0:1, ch * 128:(ch + 1) * 128],
                    in_=ps[4][0:1, 0:128]))(),
                  reads=[bank[4]], writes=al_w)
                E("pe", (lambda ch=ch: lambda h: h.transpose(
                    ps[7][0:1, 0:128], colB_fn(ch), identv[:, :]))(),
                  reads=colB_bufs + [b_ident], writes=[bank[7]])
                E("dve", (lambda ch=ch: lambda h: h.tensor_copy(
                    out=rows2v[0:1, D + ch * 128: D + (ch + 1) * 128],
                    in_=ps[7][0:1, 0:128]))(),
                  reads=[bank[7]], writes=al_w)
                E("pe", (lambda ch=ch, bk_=bk_, co=co: lambda h: h.matmul(
                    ps[bk_][0:16, co:co + 128], smrv[0:1, 0:16],
                    rows2v[0:1, ch * 128:(ch + 1) * 128],
                    start=True, stop=False))(),
                  reads=[b_smr, b_rows2], writes=[bank[bk_]])
                E("pe", (lambda ch=ch, bk_=bk_, co=co: lambda h: h.matmul(
                    ps[bk_][0:16, co:co + 128], smrv[0:1, 16:32],
                    rows2v[0:1, D + ch * 128: D + (ch + 1) * 128],
                    start=False, stop=True))(),
                  reads=[b_smr, b_rows2], writes=[bank[bk_]])

        def exchange_send(tag, cc_ap):
            E("dve", lambda h: h.tensor_copy(
                out=ccstv[0:16, 0:512], in_=ps[6][0:16, 0:512]),
              reads=[bank[6]], writes=[b_ccst])
            E("dve", lambda h: h.tensor_copy(
                out=ccstv[0:16, 512:1024], in_=ps[5][0:16, 0:512]),
              reads=[bank[5]], writes=[b_ccst])
            E("sp", (lambda cc_ap=cc_ap: lambda h: h.dma_start(
                out=cc_ap[:, :], in_=ccstv))(),
              reads=[b_ccst], writes=[b_cc[tag]], dma=True, sem="dcb")
            flat = cc_ap.rearrange("a b -> (a b)").rearrange("(o f) -> o f",
                                                             o=1)
            E("pool", (lambda flat=flat: lambda h: h.collective_compute(
                "AllReduce", ALU.add, replica_groups=[list(range(8))],
                ins=[flat], outs=[flat]))(),
              reads=[b_cc[tag]], writes=[b_cc[tag]], sem="cc", incval=1)

        def exchange_recv(tag, cc_ap):
            E("sp", (lambda cc_ap=cc_ap: lambda h: h.dma_start(
                out=ccst[:], in_=cc_ap[:, :]))(),
              reads=[b_cc[tag]], writes=[b_ccst], dma=True, sem="dcb")
            for ch in range(DC):
                E("pe", (lambda ch=ch: lambda h: h.transpose(
                    ps[7][:, 0:16], ccstv[0:16, ch * 128:(ch + 1) * 128],
                    identv[0:16, 0:16]))(),
                  reads=[b_ccst, b_ident], writes=[bank[7]])
                E("dve", (lambda ch=ch: lambda h: h.tensor_copy(
                    out=Gv[:, ch * 16:(ch + 1) * 16], in_=ps[7][:, 0:16]))(),
                  reads=[bank[7]], writes=[b_G])

        qpf_bufs = [b_tA[0], b_tA[1], b_tV[0], b_tV[1],
                    b_tQ[0], b_tQ[1], b_tH[0], b_tH[1]]

        def qpf_v(ch):
            return Uv[:, ch * tt: (ch + 1) * tt]

        def prefetch_corr(scr, scr_buf):
            for ch in range(DC):
                E("pool", (lambda ch=ch, scr=scr: lambda h: h.dma_start(
                    out=qpf_v(ch), in_=scr[0, ch, :, :]))(),
                  reads=[scr_buf], writes=[qpf_bufs[ch], b_uext[ch]],
                  dma=True, sem="dpf")

        def combine_carries():
            Hc = tmpv[:, 0:8]
            E("dve", lambda h: h.memset(hsel[:], 0.0), writes=[b_hsel])
            Gr = Gv.rearrange("p (c s) -> p c s", s=16)
            for k in range(7):
                Ak = Gr[:, :, 2 * k:2 * k + 1]
                bk_ = Gr[:, :, 2 * k + 1:2 * k + 2]
                if k == 0:
                    E("dve", (lambda bk_=bk_: lambda h: h.tensor_copy(
                        out=Hc, in_=bk_))(),
                      reads=[b_G], writes=[b_tmp[0]])
                else:
                    E("dve", (lambda Ak=Ak: lambda h: h.tensor_mul(
                        out=Hc, in0=Hc, in1=Ak))(),
                      reads=[b_G, b_tmp[0]], writes=[b_tmp[0]])
                    E("dve", (lambda bk_=bk_: lambda h: h.tensor_add(
                        out=Hc, in0=Hc, in1=bk_))(),
                      reads=[b_G, b_tmp[0]], writes=[b_tmp[0]])
                E("dve", (lambda k=k: lambda h: h.scalar_tensor_tensor(
                    out=hsel[:], in0=Hc, scalar=mprevv[:, k:k + 1],
                    in1=hsel[:], op0=ALU.mult, op1=ALU.add))(),
                  reads=[b_tmp[0], b_mprev, b_hsel], writes=[b_hsel])

        em.fence()

        E("dve", lambda h: h.memset(hcar[:], 0.0), writes=b_hcar)
        E("dve", lambda h: h.memset(pcar[:], 1.0), writes=b_pcar)
        E("dve", lambda h: h.memset(zerosv, 0.0), writes=[b_zeros])
        sfeed = Feeder(units_s)
        sfeed.take(3)

        cpieces_a = (cast_pieces(pw_p, wimg_pw, b_wimg["pw"], 128)
                     + cast_pieces(gru_p, wimg_gru, b_wimg["gru"], 128))
        cpieces_b = (cast_pieces(ff1_p, wimg_ff1, b_wimg["ff1"], 128)
                     + cast_pieces(ff2_p, wimg_ff2, b_wimg["ff2"], 32))
        cpa_i = [0]
        cpb_i = [0]

        def feed_casts_a(k):
            for fn in cpieces_a[cpa_i[0]:cpa_i[0] + k]:
                fn()
            cpa_i[0] = min(cpa_i[0] + k, len(cpieces_a))

        def feed_casts_b(k):
            for fn in cpieces_b[cpb_i[0]:cpb_i[0] + k]:
                fn()
            cpb_i[0] = min(cpb_i[0] + k, len(cpieces_b))

        def xnb_v(j, ch):
            base = xnbv if j % 2 == 0 else hbfv
            return base[:, ch * tt:(ch + 1) * tt]

        def xnb_b(j):
            return b_xnb if j % 2 == 0 else b_xnbh

        def a_stats(j):
            site_stats(tt, lambda ch, j=j: x_ch(ch, j), [b_x[j]], True,
                       [("invE", "minvE", 1e-30)])

        def a_apply(j):
            apply_ln(tt, lambda ch, j=j: x_ch(ch, j), [b_x[j]], ROW_GAMMA,
                     "invE", "minvE", lambda ch, j=j: xnb_v(j, ch), xnb_b(j))

        a_stats(0)
        a_apply(0)
        for j in range(nt):
            qk_rot = [0]
            for ch in range(DC):
                sl = ch % 4
                wv3, wb3 = wload(wimg_q, b_wq_ch[ch], ch * 384, 3 * D,
                                 nrow=384)
                trip = [(0, "kv"), (1, "a"), (2, "q")]
                for (ti, kind) in trip:
                    bi = qk_rot[0] % 4
                    qk_rot[0] += 1
                    wv = wv3[:, ti * D: (ti + 1) * D]
                    items = [(ps[bi][:, 0:tt], wv[:, kc * 128:(kc + 1) * 128],
                              xnb_v(j, kc),
                              [wb3, xnb_b(j)[kc]]) for kc in range(DC)]
                    matmul_group2(items, bank[bi])
                    if kind == "kv":
                        E("act", (lambda ch=ch, bi=bi: lambda h: h.activation(
                            out=tV_v(ch), in_=ps[bi][:, 0:tt],
                            func=AF.Copy))(),
                          reads=[bank[bi]], writes=[b_tV[sl]])
                    elif kind == "a":
                        E("act", (lambda ch=ch, bi=bi: lambda h: h.activation(
                            out=tA_v(ch), in_=ps[bi][:, 0:tt],
                            func=AF.Sigmoid))(),
                          reads=[bank[bi]], writes=[b_tA[sl]])
                    else:
                        E("act", (lambda ch=ch, bi=bi: lambda h: h.activation(
                            out=tQ_v(ch), in_=ps[bi][:, 0:tt], func=AF.Copy))(),
                          reads=[bank[bi]], writes=[b_tQ[sl]])
                E("dve", (lambda ch=ch: lambda h: h.tensor_tensor_scan(
                    out=tH_v(ch), data0=tA_v(ch), data1=tV_v(ch),
                    initial=hcarv[:, ch:ch + 1], op0=ALU.mult, op1=ALU.add))(),
                  reads=[b_tA[sl], b_tV[sl], b_hcar[ch]], writes=[b_tH[sl]])
                E("dve", (lambda ch=ch: lambda h: h.tensor_copy(
                    out=hcarv[:, ch:ch + 1], in_=tH_v(ch)[:, tt - 1:tt]))(),
                  reads=[b_tH[sl]], writes=[b_hcar[ch]])
                E("dve", (lambda ch=ch: lambda h: h.tensor_tensor_scan(
                    out=tV_v(ch), data0=tA_v(ch), data1=zerosv[:, :],
                    initial=pcarv[:, ch:ch + 1], op0=ALU.mult, op1=ALU.add))(),
                  reads=[b_tA[sl], b_zeros, b_pcar[ch], b_tV[sl]],
                  writes=[b_tV[sl]])
                E("dve", (lambda ch=ch: lambda h: h.tensor_copy(
                    out=pcarv[:, ch:ch + 1], in_=tV_v(ch)[:, tt - 1:tt]))(),
                  reads=[b_tV[sl]], writes=[b_pcar[ch]])
                tq = tq_v(ch)
                E("pool", (lambda ch=ch, tq=tq: lambda h: h.tensor_mul(
                    out=tq, in0=tQ_v(ch), in1=tH_v(ch)))(),
                  reads=[b_tQ[sl], b_tH[sl]], writes=[b_tq[sl]])
                E("dve", (lambda ch=ch, j=j, tq=tq: lambda h: h.tensor_add(
                    out=x_ch(ch, j), in0=x_ch(ch, j), in1=tq))(),
                  reads=[b_tq[sl], b_x[j]], writes=[b_x[j]])
                E("pool", (lambda ch=ch: lambda h: h.tensor_mul(
                    out=tH_v(ch), in0=tQ_v(ch), in1=tV_v(ch)))(),
                  reads=[b_tQ[sl], b_tV[sl]], writes=[b_tH[sl]])
                E("pool", (lambda j=j, ch=ch: lambda h: h.dma_start(
                    out=qp_scr[j, ch, :, :], in_=tH_v(ch)))(),
                  reads=[b_tH[sl]], writes=[b_qp[j]], dma=True, sem="dq")
                if j == 0 and ch < 6:
                    cast_img(wq_p, wimg_q, b_wq_ch[ch + 2], sem=f"dcq{ch + 2}",
                             rows=((ch + 2) * 384, (ch + 3) * 384))
                if ch == 3 and j + 1 < nt:
                    a_stats(j + 1)
                if ch == 6 and j + 1 < nt:
                    a_apply(j + 1)
                sfeed.take(3)
                if j > 0 or nt == 1:
                    feed_casts_a(1)

        sfeed.drain()
        feed_casts_a(len(cpieces_a))
        E("dve", lambda h: h.tensor_scalar_mul(
            out=a1v[:, :], in0=a1v[:, :], scalar1=1.0 / D),
          reads=[b_alpha1], writes=[b_alpha1])
        E("dve", lambda h: h.tensor_scalar_mul(
            out=a2v[:, :], in0=a2v[:, :], scalar1=1.0 / (4 * D)),
          reads=[b_alpha2], writes=[b_alpha2])

        exchange_prep(lambda ch: pcarv[:, ch:ch + 1], list(b_pcar),
                      lambda ch: hcarv[:, ch:ch + 1], list(b_hcar))
        exchange_send("cc1", cc1)
        prefetch_corr(qp_scr, b_qp[0])
        exchange_recv("cc1", cc1)
        combine_carries()

        def corr_tile(scr, j, pref=False):
            for ch in range(DC):
                sl = ch % 2
                if pref and j == 0:
                    E("dve", (lambda j=j, ch=ch: lambda h:
                              h.scalar_tensor_tensor(
                                  out=x_ch(ch, j), in0=qpf_v(ch),
                                  scalar=hselv[:, ch:ch + 1], in1=x_ch(ch, j),
                                  op0=ALU.mult, op1=ALU.add))(),
                      reads=[qpf_bufs[ch], b_uext[ch], b_hsel, b_x[j]],
                      writes=[b_x[j]])
                    continue
                cr = corrv[:, sl * tt: (sl + 1) * tt]
                E("pool", (lambda j=j, ch=ch, cr=cr, scr=scr:
                           lambda h: h.dma_start(
                               out=cr, in_=scr[j, ch, :, :]))(),
                  reads=[b_qp[j] if scr is qp_scr else b_pg[j]],
                  writes=[b_corr[sl]], dma=True, sem=f"dcp{sl}")
                E("dve", (lambda j=j, ch=ch, cr=cr:
                          lambda h: h.scalar_tensor_tensor(
                              out=x_ch(ch, j), in0=cr,
                              scalar=hselv[:, ch:ch + 1], in1=x_ch(ch, j),
                              op0=ALU.mult, op1=ALU.add))(),
                  reads=[b_corr[sl], b_hsel, b_x[j]], writes=[b_x[j]])

        def emit_mlp(kind, w1_img, w1_buf, w2_img, w2_buf, b1_col, epilogue,
                     alpha, bunits=None, gain_row=None, pre_tile=None,
                     cast_fn=None):
            bfeed = Feeder(bunits) if bunits else None

            def head_stats(j):
                if kind == "ln":
                    site_stats(tt, lambda ch, j=j: x_ch(ch, j), [b_x[j]], True,
                               [("invE", "minvE", 1e-5)])
                else:
                    site_stats(tt, lambda ch, j=j: x_ch(ch, j), [b_x[j]],
                               False, [])

            def head_apply(j):
                if kind == "ln":
                    apply_ln(tt, lambda ch, j=j: x_ch(ch, j), [b_x[j]], None,
                             "invE", "minvE",
                             lambda ch: xnbv[:, ch * tt:(ch + 1) * tt], b_xnb)
                else:
                    apply_rms(tt, lambda ch, j=j: x_ch(ch, j), [b_x[j]],
                              gain_row,
                              lambda ch: xnbv[:, ch * tt:(ch + 1) * tt], b_xnb)

            if pre_tile is not None:
                pre_tile(0)
            head_stats(0)
            head_apply(0)
            for j in range(nt):
                for ec in range(32):
                    bi = ec % 2
                    wv, wb = wload(w1_img, w1_buf, ec * 128, D)
                    items = [(ps[bi][:, 0:tt], wv[:, kc * 128:(kc + 1) * 128],
                              xnbv[:, kc * tt:(kc + 1) * tt],
                              [wb, b_xnb[kc]]) for kc in range(DC)]
                    matmul_group2(items, bank[bi])
                    bi_c = b1_col(ec)
                    rd = [bank[bi], b_vcols] + ([b_alpha1] if alpha else [])
                    if alpha:
                        sc = a1v[:, ec:ec + 1]
                        E("act", (lambda ec=ec, bi=bi, sc=sc, bi_c=bi_c:
                                  lambda h: h.activation(
                                      out=hbf_v(ec), in_=ps[bi][:, 0:tt],
                                      func=AF.Gelu, scale=sc, bias=bi_c))(),
                          reads=rd, writes=[b_hbf[ec]])
                    else:
                        E("act", (lambda ec=ec, bi=bi, bi_c=bi_c:
                                  lambda h: h.activation(
                                      out=hbf_v(ec), in_=ps[bi][:, 0:tt],
                                      func=AF.Gelu, bias=bi_c))(),
                          reads=rd, writes=[b_hbf[ec]])
                    if bfeed:
                        bfeed.take(1)
                    if cast_fn:
                        cast_fn()
                    if ec == 14 and j + 1 < nt and pre_tile is not None:
                        pre_tile(j + 1)
                    if ec == 16 and j + 1 < nt:
                        head_stats(j + 1)
                for dc in range(DC):
                    bi = 2 + dc % 2
                    wv1, wb1 = wload(w2_img, w2_buf, dc * 128, 4096)
                    items = [(ps[bi][:, 0:tt], wv1[:, kc * 128:(kc + 1) * 128],
                              hbf_v(kc), [wb1, b_hbf[kc]]) for kc in range(32)]
                    matmul_group2(items, bank[bi])
                    epilogue(j, dc, ps[bi][:, 0:tt], bank[bi])
                    if bfeed:
                        bfeed.take(1)
                    if cast_fn:
                        cast_fn()
                    if dc == 2 and j + 1 < nt:
                        head_apply(j + 1)
            if bfeed:
                bfeed.drain()

        def ffn_epilogue(store):
            def ep(j, dc, psum_ap, bk):
                sl = dc % 2
                t = tmpv[:, sl * tt: (sl + 1) * tt]
                E("dve", (lambda psum_ap=psum_ap, dc=dc, t=t:
                          lambda h: h.tensor_scalar(
                              out=t, in0=psum_ap, scalar1=a2v[:, dc:dc + 1],
                              scalar2=vcol(VR_NB2, dc), op0=ALU.mult,
                              op1=ALU.add))(),
                  reads=[bk, b_alpha2, b_vcols], writes=[b_tmp[sl]])
                E("dve", (lambda j=j, dc=dc, t=t: lambda h: h.tensor_add(
                    out=x_ch(dc, j), in0=x_ch(dc, j), in1=t))(),
                  reads=[b_tmp[sl], b_x[j]], writes=[b_x[j]])
                if store:
                    E("pool", (lambda j=j, dc=dc: lambda h: h.dma_start(
                        out=out_o.rearrange("(c p) t -> p c t", p=128)
                        [:, dc, j * tt:(j + 1) * tt],
                        in_=x_ch(dc, j)))(),
                      reads=[b_x[j]], writes=[b_out], dma=True, sem="dout")
            return ep

        def _bcasts(j):
            corr_tile(qp_scr, j, pref=(j == 0))

        em.fence()
        if upto >= 2:
            emit_mlp("ln", wimg_s1, b_wimg["s1"], wimg_s2, b_wimg["s2"],
                     lambda ec: vcol(VR_NB1, ec), ffn_epilogue(False), True,
                     bunits=units_b, pre_tile=_bcasts,
                     cast_fn=lambda: feed_casts_b(1))
        elif upto == 1:
            pass
        else:
            for j in range(nt):
                corr_tile(qp_scr, j)
        feed_casts_b(len(cpieces_b))

        class _EndPhase3(Exception):
            pass
        try:
         if upto >= 3:
            build_gain(VR_GRUG, ROW_GRU32, 1.0, 32.0)
            site_stats(2, lambda ch: x_cols(ch, tcore - 2, 2), [b_x[nt - 1]],
                       True, [("inv0", "minv0", 1e-30)])
            for ch in range(DC):
                g = gain_slice(ROW_CONVG, ch)
                E("pe", (lambda g=g: lambda h: h.matmul(
                    ps[6][:, 0:2], g, rs("inv0", 2), start=True, stop=True))(),
                  reads=[b_Rg, RSB["inv0"]], writes=[bank[6]])
                E("pe", (lambda g=g: lambda h: h.matmul(
                    ps[7][:, 0:2], g, rs("minv0", 2), start=True, stop=True))(),
                  reads=[b_Rg, RSB["minv0"]], writes=[bank[7]])
                t = tmpv[:, 0:2]
                E("dve", (lambda ch=ch, t=t: lambda h: h.tensor_mul(
                    out=t, in0=x_cols(ch, tcore - 2, 2), in1=ps[6][:, 0:2]))(),
                  reads=[b_x[nt - 1], bank[6]], writes=[b_tmp[0]])
                E("dve", (lambda ch=ch, t=t: lambda h: h.tensor_sub(
                    out=uh2v[:, ch * 2: ch * 2 + 2], in0=t,
                    in1=ps[7][:, 0:2]))(),
                  reads=[b_tmp[0], bank[7]], writes=[b_uh2])
            em.fence()
            exchange_prep(lambda ch: uh2v[:, ch * 2: ch * 2 + 1], [b_uh2],
                          lambda ch: uh2v[:, ch * 2 + 1: ch * 2 + 2], [b_uh2])
            exchange_send("cc2", cc2)
            _s3 = SUB3

            uer = Uv[:, 0:DC * (tt + 4)].rearrange("p (c t) -> p c t", c=DC)
            Gr = Gv.rearrange("p (c s) -> p c s", s=16)

            def a3_stats(j):
                for ch in range(DC):
                    E("sp", (lambda j=j, ch=ch: lambda h: h.dma_start(
                        out=xold_scr[j, ch, :, :], in_=x_ch(ch, j)))(),
                      reads=[b_x[j]], writes=[b_xold[j]], dma=True, sem="dqh")
                site_stats(tt, lambda ch, j=j: x_ch(ch, j), [b_x[j]], True,
                           [("invE", "minvE", 1e-5), ("inv0", "minv0", 1e-30)])

            def a3_uln(j):
                for ch in range(DC):
                    g2 = gain_slice(ROW_CONVG, ch)
                    E("pe", (lambda g2=g2: lambda h: h.matmul(
                        ps[6][:, 0:tt], g2, rs("inv0", tt),
                        start=True, stop=True))(),
                      reads=[b_Rg, RSB["inv0"]], writes=[bank[6]])
                    E("pe", (lambda g2=g2: lambda h: h.matmul(
                        ps[7][:, 0:tt], g2, rs("minv0", tt),
                        start=True, stop=True))(),
                      reads=[b_Rg, RSB["minv0"]], writes=[bank[7]])
                    t0 = tmpv[:, 0:tt]
                    E("dve", (lambda ch=ch, j=j, t0=t0: lambda h: h.tensor_mul(
                        out=t0, in0=x_ch(ch, j), in1=ps[6][:, 0:tt]))(),
                      reads=[b_x[j], bank[6]], writes=[b_tmp[0]])
                    E("dve", (lambda ch=ch, t0=t0: lambda h: h.tensor_sub(
                        out=u_ch(ch, 2, tt), in0=t0, in1=ps[7][:, 0:tt]))(),
                      reads=[b_tmp[0], bank[7]], writes=[b_uext[ch]])
                    o = ones_row()
                    E("pe", (lambda o=o: lambda h: h.matmul(
                        ps[4][:, 0:tt], o, rs("invE", tt),
                        start=True, stop=True))(),
                      reads=[b_Rg, RSB["invE"]], writes=[bank[4]])
                    E("pe", (lambda o=o: lambda h: h.matmul(
                        ps[5][:, 0:tt], o, rs("minvE", tt),
                        start=True, stop=True))(),
                      reads=[b_Rg, RSB["minvE"]], writes=[bank[5]])
                    t1 = tmpv[:, tt:2 * tt]
                    E("dve", (lambda ch=ch, j=j, t1=t1: lambda h: h.tensor_mul(
                        out=t1, in0=x_ch(ch, j), in1=ps[4][:, 0:tt]))(),
                      reads=[b_x[j], bank[4], b_xold[j]], writes=[b_tmp[1]])
                    E("dve", (lambda ch=ch, j=j, t1=t1: lambda h: h.tensor_sub(
                        out=x_ch(ch, j), in0=t1, in1=ps[5][:, 0:tt]))(),
                      reads=[b_tmp[1], bank[5]], writes=[b_x[j]])

            def a3_taps_pw(j):
                for ch in range(DC):
                    E("dve", (lambda ch=ch: lambda h: h.tensor_scalar(
                        out=ycvv, in0=u_ch(ch, 2, tt),
                        scalar1=vcol(VR_T2, ch), scalar2=vcol(VR_BDW, ch),
                        op0=ALU.mult, op1=ALU.add))(),
                      reads=[b_uext[ch], b_vcols], writes=[b_ycv])
                    E("dve", (lambda ch=ch: lambda h: h.scalar_tensor_tensor(
                        out=ycvv, in0=u_ch(ch, 1, tt),
                        scalar=vcol(VR_T1, ch), in1=ycvv,
                        op0=ALU.mult, op1=ALU.add))(),
                      reads=[b_uext[ch], b_vcols, b_ycv], writes=[b_ycv])
                    E("dve", (lambda ch=ch: lambda h: h.scalar_tensor_tensor(
                        out=ybf_v(ch), in0=u_ch(ch, 0, tt),
                        scalar=vcol(VR_T0, ch), in1=ycvv,
                        op0=ALU.mult, op1=ALU.add))(),
                      reads=[b_uext[ch], b_vcols, b_ycv], writes=[b_ybf[ch]])
                    E("dve", (lambda ch=ch: lambda h: h.tensor_copy(
                        out=u_ch(ch, 0, 2), in_=u_ch(ch, tt, 2)))(),
                      reads=[b_uext[ch]], writes=[b_uext[ch]])
                for ec in range(DC):
                    bi = 2 + ec % 2
                    wv, wb = wload(wimg_pw, b_wimg["pw"], ec * 128, D)
                    items = [(ps[bi][:, 0:tt], wv[:, kc * 128:(kc + 1) * 128],
                              ybf_v(kc), [wb, b_ybf[kc]]) for kc in range(DC)]
                    matmul_group2(items, bank[bi])
                    sl = ec % 2
                    t = tmpv[:, (2 + sl) * tt: (3 + sl) * tt]
                    E("dve", (lambda bi=bi, ec=ec, t=t:
                              lambda h: h.tensor_scalar_add(
                                  out=t, in0=ps[bi][:, 0:tt],
                                  scalar1=vcol(VR_BPW, ec)))(),
                      reads=[bank[bi], b_vcols], writes=[b_tmp[2 + sl]])
                    E("dve", (lambda j=j, ec=ec, t=t: lambda h: h.tensor_add(
                        out=x_ch(ec, j), in0=x_ch(ec, j), in1=t))(),
                      reads=[b_tmp[2 + sl], b_x[j]], writes=[b_x[j]])
                    if ec == 2 and j + 1 < nt and _s3 >= 3:
                        a3_stats(j + 1)

            if _s3 >= 3:
                a3_stats(0)
                a3_uln(0)
            exchange_recv("cc2", cc2)
            for par in range(2):
                dst = uer[:, :, par:par + 1]
                E("dve", (lambda dst=dst: lambda h: h.memset(dst, 0.0))(),
                  writes=list(b_uext))
            for k in range(7):
                for par in range(2):
                    dst = uer[:, :, par:par + 1]
                    src_ = Gr[:, :, 2 * k + par:2 * k + par + 1]
                    E("dve", (lambda dst=dst, src_=src_, k=k:
                              lambda h: h.scalar_tensor_tensor(
                                  out=dst, in0=src_, scalar=mprevv[:, k:k + 1],
                                  in1=dst, op0=ALU.mult, op1=ALU.add))(),
                      reads=[b_G, b_mprev] + list(b_uext),
                      writes=list(b_uext))
            if _s3 >= 3:
                a3_taps_pw(0)
                for j in range(1, nt):
                    a3_uln(j)
                    a3_taps_pw(j)

            em.fence()

            E("dve", lambda h: h.memset(hcar[:], 0.0), writes=b_hcar)
            E("dve", lambda h: h.memset(pcar[:], 1.0), writes=b_pcar)
            E("dve", lambda h: h.memset(zerosv, 0.0), writes=[b_zeros])
            def g_stats(j):
                site_stats(tt, lambda ch, j=j: x_ch(ch, j), [b_x[j]],
                           False, [])

            def g_apply(j):
                apply_rms(tt, lambda ch, j=j: x_ch(ch, j), [b_x[j]],
                          ROW_GRU32,
                          lambda ch, j=j: xnb_v(j, ch), xnb_b(j))

            if _s3 >= 4 and nt > 0:
                g_stats(0)
                g_apply(0)
            for j in range(nt if _s3 >= 4 else 0):
                for ch in range(DC):
                    sl = ch % 4
                    sl2 = ch % 2
                    bh = 2 + sl2
                    bg = sl2
                    wvg, wbg = wload(wimg_gru, b_wimg["gru"], ch * 256, 2 * D,
                                     nrow=256)
                    items = [(ps[bh][:, 0:tt], wvg[:, kc * 128:(kc + 1) * 128],
                              xnb_v(j, kc),
                              [wbg, xnb_b(j)[kc]]) for kc in range(DC)]
                    matmul_group2(items, bank[bh])
                    items = [(ps[bg][:, 0:tt],
                              wvg[:, D + kc * 128: D + (kc + 1) * 128],
                              xnb_v(j, kc),
                              [wbg, xnb_b(j)[kc]]) for kc in range(DC)]
                    matmul_group2(items, bank[bg])
                    s_t = tmpv[:, sl2 * tt: (sl2 + 1) * tt]
                    sv_t = tmpv[:, (2 + sl2) * tt: (3 + sl2) * tt]
                    E("act", (lambda bh=bh, s_t=s_t: lambda h: h.activation(
                        out=s_t, in_=ps[bh][:, 0:tt], func=AF.Sigmoid))(),
                      reads=[bank[bh]], writes=[b_tmp[sl2]])
                    E("act", (lambda bh=bh, ch=ch: lambda h: h.activation(
                        out=tV_v(ch), in_=ps[bh][:, 0:tt], func=AF.Copy,
                        bias=0.5))(),
                      reads=[bank[bh]], writes=[b_tV[sl]])
                    E("act", (lambda bg=bg, ch=ch: lambda h: h.activation(
                        out=tA_v(ch), in_=ps[bg][:, 0:tt], func=AF.Sigmoid,
                        scale=-1.0))(),
                      reads=[bank[bg]], writes=[b_tA[sl]])
                    E("act", (lambda bg=bg, sv_t=sv_t: lambda h: h.activation(
                        out=sv_t, in_=ps[bg][:, 0:tt], func=AF.Sigmoid))(),
                      reads=[bank[bg]], writes=[b_tmp[2 + sl2]])
                    E("dve", (lambda ch=ch, s_t=s_t: lambda h: h.tensor_max(
                        out=tV_v(ch), in0=tV_v(ch), in1=s_t))(),
                      reads=[b_tmp[sl2], b_tV[sl]], writes=[b_tV[sl]])
                    E("pool", (lambda ch=ch, sv_t=sv_t: lambda h: h.tensor_mul(
                        out=tV_v(ch), in0=tV_v(ch), in1=sv_t))(),
                      reads=[b_tmp[2 + sl2], b_tV[sl]], writes=[b_tV[sl]])
                    E("dve", (lambda ch=ch: lambda h: h.tensor_tensor_scan(
                        out=tH_v(ch), data0=tA_v(ch), data1=tV_v(ch),
                        initial=hcarv[:, ch:ch + 1], op0=ALU.mult,
                        op1=ALU.add))(),
                      reads=[b_tA[sl], b_tV[sl], b_hcar[ch]], writes=[b_tH[sl]])
                    E("dve", (lambda ch=ch: lambda h: h.tensor_copy(
                        out=hcarv[:, ch:ch + 1], in_=tH_v(ch)[:, tt - 1:tt]))(),
                      reads=[b_tH[sl]], writes=[b_hcar[ch]])
                    E("dve", (lambda ch=ch: lambda h: h.tensor_tensor_scan(
                        out=tV_v(ch), data0=tA_v(ch), data1=zerosv[:, :],
                        initial=pcarv[:, ch:ch + 1], op0=ALU.mult,
                        op1=ALU.add))(),
                      reads=[b_tA[sl], b_zeros, b_pcar[ch], b_tV[sl]],
                      writes=[b_tV[sl]])
                    E("dve", (lambda ch=ch: lambda h: h.tensor_copy(
                        out=pcarv[:, ch:ch + 1], in_=tV_v(ch)[:, tt - 1:tt]))(),
                      reads=[b_tV[sl]], writes=[b_pcar[ch]])
                    E("pool", (lambda ch=ch, j=j: lambda h: h.tensor_add(
                        out=x_ch(ch, j), in0=x_ch(ch, j), in1=tH_v(ch)))(),
                      reads=[b_tH[sl], b_x[j]], writes=[b_x[j]])
                    E("pool", (lambda j=j, ch=ch: lambda h: h.dma_start(
                        out=pg_scr[j, ch, :, :], in_=tV_v(ch)))(),
                      reads=[b_tV[sl]], writes=[b_pg[j]], dma=True, sem="dq")
                    if ch == 3 and j + 1 < nt:
                        g_stats(j + 1)
                    if ch == 6 and j + 1 < nt:
                        g_apply(j + 1)
            if _s3 < 5:
                raise _EndPhase3()
            exchange_prep(lambda ch: pcarv[:, ch:ch + 1], list(b_pcar),
                          lambda ch: hcarv[:, ch:ch + 1], list(b_hcar))
            exchange_send("cc3", cc3)
            prefetch_corr(pg_scr, b_pg[0])
            build_gain(VR_FFG, ROW_FF32, 1.0, 32.0)
            exchange_recv("cc3", cc3)
            combine_carries()

            em.fence()

            def ff_epilogue(j, dc, psum_ap, bk):
                sl = dc % 2
                t = tmpv[:, sl * tt: (sl + 1) * tt]
                E("dve", (lambda psum_ap=psum_ap, dc=dc, t=t:
                          lambda h: h.tensor_scalar_add(
                              out=t, in0=psum_ap, scalar1=vcol(VR_FB2, dc)))(),
                  reads=[bk, b_vcols], writes=[b_tmp[sl]])
                E("dve", (lambda j=j, dc=dc, t=t: lambda h: h.tensor_add(
                    out=x_ch(dc, j), in0=x_ch(dc, j), in1=t))(),
                  reads=[b_tmp[sl], b_x[j]], writes=[b_x[j]])
                xo = corrv[:, sl * tt: (sl + 1) * tt]
                E("pool", (lambda j=j, dc=dc, xo=xo: lambda h: h.dma_start(
                    out=xo, in_=xold_scr[j, dc, :, :]))(),
                  reads=[b_xold[j]], writes=[b_corr[sl]], dma=True,
                  sem=f"dcp{sl}")
                E("dve", (lambda j=j, dc=dc, xo=xo: lambda h: h.tensor_add(
                    out=x_ch(dc, j), in0=x_ch(dc, j), in1=xo))(),
                  reads=[b_corr[sl], b_x[j]], writes=[b_x[j]])

            emit_mlp("rms", wimg_ff1, b_wimg["ff1"], wimg_ff2, b_wimg["ff2"],
                     lambda ec: vcol(VR_FB1, ec), ff_epilogue, False,
                     gain_row=ROW_FF32,
                     pre_tile=lambda j: corr_tile(pg_scr, j,
                                                   pref=(j == 0)))

        except _EndPhase3:
            pass
        if upto >= 4:
            emit_mlp("ln", wimg_s1, b_wimg["s1"], wimg_s2, b_wimg["s2"],
                     lambda ec: vcol(VR_NB1, ec), ffn_epilogue(True), True)
        else:
            for j in range(nt):
                for dc in range(DC):
                    E("pool", (lambda j=j, dc=dc: lambda h: h.dma_start(
                        out=out_o.rearrange("(c p) t -> p c t", p=128)
                        [:, dc, j * tt:(j + 1) * tt],
                        in_=x_ch(dc, j)))(),
                      reads=[b_x[j]], writes=[b_out], dma=True, sem="dout")

        for s_ in ("dout", "dq", "dqh", "di0", "di1", "dcb", "dcv"):
            em.raw_wait("sp", s_, em.cnt[s_])

        with nc.Block() as block:
            def mk(eng):
                def run(h):
                    for it in em.items[eng]:
                        if it[0] == "w":
                            h.wait_ge(sems[it[1]], it[2])
                        else:
                            inst = it[1](h)
                            if it[2] is not None:
                                inst.then_inc(sems[it[2]], it[3])
                return run
            block.tensor(mk("pe"))
            block.scalar(mk("act"))
            block.vector(mk("dve"))
            block.gpsimd(mk("pool"))
            block.sync(mk("sp"))

    return nc


_NC_CACHE = {}


def _get_nc(key):
    if key not in _NC_CACHE:
        _NC_CACHE[key] = build_nc(*key)
    return _NC_CACHE[key]


def make_inputs(full_inputs, tcore):
    ii = full_inputs
    Tloc = tcore * NCORES
    x = np.asarray(ii["L0_attn_a_x"], dtype=np.float32).reshape(-1, D)[:Tloc]

    vrows = np.zeros((16, 4096), np.float32)
    vrows[VR_BDW, :D] = np.asarray(ii["L0_attn_b_conv_dw_b"], np.float32)
    vrows[VR_BPW, :D] = np.asarray(ii["L0_attn_b_conv_pw_b"], np.float32)
    vrows[VR_FB2, :D] = np.asarray(ii["L0_attn_b_ff_b2"], np.float32)
    vrows[VR_NB2, :D] = np.asarray(ii["L0_ffn_b2"], np.float32)
    vrows[VR_FB1, :] = np.asarray(ii["L0_attn_b_ff_b1"], np.float32)
    vrows[VR_NB1, :] = np.asarray(ii["L0_ffn_b1"], np.float32)
    cdw = np.asarray(ii["L0_attn_b_conv_dw_w"], np.float32)[:, 0, :]
    vrows[VR_T0, :D] = cdw[:, 0]
    vrows[VR_T1, :D] = cdw[:, 1]
    vrows[VR_T2, :D] = cdw[:, 2]
    vrows[VR_GAMMA, :D] = np.asarray(ii["L0_attn_a_gamma"], np.float32)
    vrows[VR_CONVG, :D] = np.asarray(ii["L0_attn_b_conv_norm_g"], np.float32)
    vrows[VR_GRUG, :D] = np.asarray(ii["L0_attn_b_gru_norm_g"], np.float32)
    vrows[VR_FFG, :D] = np.asarray(ii["L0_attn_b_ff_norm_g"], np.float32)

    wq_packed = pack_lhsT(np.asarray(ii["L0_attn_a_w_qkva"], np.float32))
    wq4 = wq_packed.reshape(24, 128, 1024)
    order = []
    for ch in range(8):
        order += [8 + ch, 16 + ch, ch]
    wq_tripped = np.ascontiguousarray(wq4[order].reshape(24 * 128, 1024))
    gru_packed = pack_lhsT(np.asarray(ii["L0_attn_b_gru_w"], np.float32))
    g4 = gru_packed.reshape(16, 128, 1024)
    gorder = []
    for ch in range(8):
        gorder += [ch, 8 + ch]
    gru_paired = np.ascontiguousarray(g4[gorder].reshape(16 * 128, 1024))
    packs = dict(
        wq_p=wq_tripped,
        pw_p=pack_lhsT(np.asarray(ii["L0_attn_b_conv_pw_w"], np.float32)),
        gru_p=gru_paired,
        ff1_p=pack_lhsT(np.asarray(ii["L0_attn_b_ff_w1"], np.float32)),
        ff2_p=pack_lhsT(np.asarray(ii["L0_attn_b_ff_w2"], np.float32)),
        n1_p=pack_lhsT(np.asarray(ii["L0_ffn_w1"], np.float32)),
        n2_p=pack_lhsT(np.asarray(ii["L0_ffn_w2"], np.float32)),
        n1_nat=np.ascontiguousarray(np.asarray(ii["L0_ffn_w1"], np.float32)),
        n2_nat=np.ascontiguousarray(np.asarray(ii["L0_ffn_w2"], np.float32)),
        vrows_in=vrows,
        ident_in=np.eye(128, dtype=np.float32),
    )

    in_maps = []
    for c in range(NCORES):
        mprev = np.zeros((128, 8), np.float32)
        if c > 0:
            mprev[:, c - 1] = 1.0
        smask = np.zeros((16, 1), np.float32)
        smask[2 * c: 2 * c + 2] = 1.0
        smr = np.zeros((1, 32), np.float32)
        smr[0, 2 * c] = 1.0
        smr[0, 16 + 2 * c + 1] = 1.0
        xs = np.ascontiguousarray(x[c * tcore:(c + 1) * tcore, :].T)
        m = dict(packs)
        m.update(x_in=xs, mprev_in=mprev, smask_in=smask, smr_in=smr)
        in_maps.append(m)
    return in_maps


def kernel(**inputs):
    tcore = T // NCORES
    in_maps = make_inputs(inputs, tcore)
    nc = _get_nc((tcore, 512, 4, 0))
    res = run_bass_kernel_spmd(nc, in_maps, core_ids=list(range(NCORES)))
    out = np.empty((T, D), np.float32)
    for c in range(NCORES):
        out[c * tcore:(c + 1) * tcore, :] = res.results[c]["out_o"].T
    return out.reshape(1, T, D)
